# version 05
# baseline (speedup 1.0000x reference)
import contextlib
import numpy as np
import ml_dtypes
import concourse.bass as bass
import concourse.mybir as mybir
from concourse.bass_utils import run_bass_kernel_spmd

F32 = mybir.dt.float32
BF16 = mybir.dt.bfloat16
ALU = mybir.AluOpType
AF = mybir.ActivationFunctionType
AX = mybir.AxisListType.X
BF = ml_dtypes.bfloat16

NCORES = 8
B_, T_, D_ = 2, 8192, 1024
TC = 2048
KC = 8
ENGINES = ("sync", "vector", "scalar", "tensor", "gpsimd")


class Sem:
    __slots__ = ("name", "handle", "n")

    def __init__(self, name, handle):
        self.name, self.handle, self.n = name, handle, 0


class KB:

    def __init__(self):
        self.nc = bass.Bass("TRN2", target_bir_lowering=False)
        self.es = contextlib.ExitStack()
        self.plan = {e: [] for e in ENGINES}
        self.V = self.sem("Vchain")
        self.S = self.sem("Schain")
        self.T = self.sem("Tchain")
        self.BAR = self.sem("BARs")
        self.bar_k = 0

    def inp(self, name, shape, dtype=F32):
        return self.nc.dram_tensor(name, list(shape), dtype, kind="ExternalInput").ap()

    def outp(self, name, shape, dtype=F32):
        return self.nc.dram_tensor(name, list(shape), dtype, kind="ExternalOutput").ap()

    def dram(self, name, shape, dtype=F32):
        return self.nc.dram_tensor(name, list(shape), dtype, kind="Internal").ap()

    def sbuf(self, name, shape, dtype=F32):
        return self.es.enter_context(self.nc.sbuf_tensor(name, list(shape), dtype))

    def psum(self, name, shape, dtype=F32):
        return self.es.enter_context(self.nc.psum_tensor(name, list(shape), dtype))

    def sem(self, name):
        return Sem(name, self.es.enter_context(self.nc.semaphore(name)))

    def op(self, eng, fn, inc=None, amt=1):
        if inc is not None:
            inc.n += amt
        self.plan[eng].append(("op", fn, inc, amt))

    def wait(self, eng, sem, val=None):
        v = sem.n if val is None else val
        self.plan[eng].append(("wait", sem, v, 0))

    def dma(self, eng, out, in_, sem, **kw):
        self.op(eng, lambda e: e.dma_start(out=out, in_=in_, **kw), inc=sem, amt=16)

    def vec(self, fn):
        self.wait("vector", self.V)
        self.op("vector", fn, inc=self.V)

    def sca(self, fn):
        self.wait("scalar", self.S)
        self.op("scalar", fn, inc=self.S)

    def mm(self, fn):
        self.op("tensor", fn)

    def tlast(self):
        kind, fn, inc, amt = self.plan["tensor"][-1]
        assert kind == "op" and inc is None
        self.T.n += 1
        self.plan["tensor"][-1] = ("op", fn, self.T, 1)

    def tmark(self):
        self.tlast()

    def vmark(self):
        pass

    def smark(self):
        pass

    def barrier(self):
        self.bar_k += 1
        self.wait("vector", self.V)
        self.wait("scalar", self.S)
        for eng in ENGINES:
            self.op(eng, lambda e: e.nop(), inc=self.BAR)
        tgt = 5 * self.bar_k
        for eng in ENGINES:
            self.plan[eng].append(("wait", self.BAR, tgt, 0))

    def build(self):
        with self.nc.Block() as block:
            for eng in ENGINES:
                items = self.plan[eng]
                if not items:
                    continue

                def emit(e, items=items):
                    for it in items:
                        if it[0] == "wait":
                            e.wait_ge(it[1].handle, it[2])
                        else:
                            ins = it[1](e)
                            if it[2] is not None:
                                ins.then_inc(it[2].handle, it[3])
                getattr(block, eng)(emit)
        self.es.close()
        return self.nc


def ln_normalize(k, x_f32, raw_b16, sq_b16, xn_b16, scr32, stat, ones_k, ones1f,
                 psA, psB, eps_ap, cast_raw=True):
    m = stat[0:1, 0:TC]
    v = stat[0:1, TC:2 * TC]
    tmp = stat[0:1, 2 * TC:3 * TC]
    if cast_raw:
        for c in range(KC):
            k.vec(lambda e, c=c: e.tensor_copy(
                raw_b16[:, c * TC:(c + 1) * TC], x_f32[:, c * TC:(c + 1) * TC]))
        k.vmark()
    k.wait("scalar", k.V)
    for c in range(KC):
        k.sca(lambda e, c=c: e.activation(
            out=sq_b16[:, c * TC:(c + 1) * TC], in_=raw_b16[:, c * TC:(c + 1) * TC],
            func=AF.Square))
    k.smark()
    k.wait("tensor", k.V)
    k.wait("tensor", k.S)
    for n in range(4):
        for c in range(KC):
            k.mm(lambda e, n=n, c=c: e.matmul(
                psA[0:1, n * 512:(n + 1) * 512], ones_k[:, 0:1],
                raw_b16[:, c * TC + n * 512: c * TC + n * 512 + 512],
                start=(c == 0), stop=(c == KC - 1)))
        for c in range(KC):
            k.mm(lambda e, n=n, c=c: e.matmul(
                psB[0:1, n * 512:(n + 1) * 512], ones_k[:, 0:1],
                sq_b16[:, c * TC + n * 512: c * TC + n * 512 + 512],
                start=(c == 0), stop=(c == KC - 1)))
    k.tmark()
    k.wait("vector", k.T)
    k.vec(lambda e: e.tensor_scalar(m, psA[0:1, 0:TC], 1.0 / D_, None, ALU.mult))
    k.vec(lambda e: e.tensor_scalar(v, psB[0:1, 0:TC], 1.0 / D_, None, ALU.mult))
    k.vec(lambda e: e.tensor_tensor(tmp, m, m, ALU.mult))
    k.vec(lambda e: e.tensor_tensor(v, v, tmp, ALU.subtract))
    k.vmark()
    k.wait("scalar", k.V)
    k.sca(lambda e: e.activation(out=tmp, in_=v, func=AF.Ln, bias=eps_ap))
    k.sca(lambda e: e.activation(out=v, in_=tmp, func=AF.Exp, scale=-0.5))
    k.smark()
    k.wait("tensor", k.S)
    for n in range(4):
        k.mm(lambda e, n=n: e.matmul(
            psA[:, n * 512:(n + 1) * 512], ones1f[0:1, :], m[:, n * 512:(n + 1) * 512],
            start=True, stop=True))
        k.mm(lambda e, n=n: e.matmul(
            psB[:, n * 512:(n + 1) * 512], ones1f[0:1, :], v[:, n * 512:(n + 1) * 512],
            start=True, stop=True))
    k.tmark()
    k.wait("vector", k.T)
    for c in range(KC):
        k.vec(lambda e, c=c: e.tensor_tensor(
            scr32[:, :], x_f32[:, c * TC:(c + 1) * TC], psA[:, 0:TC], ALU.subtract))
        k.vec(lambda e, c=c: e.tensor_tensor(
            xn_b16[:, c * TC:(c + 1) * TC], scr32[:, :], psB[:, 0:TC], ALU.mult))
    k.vmark()


def build_A():
    k = KB()
    xT = k.inp("xT", [D_, TC])
    Wd = k.inp("Wd", [D_, D_])
    bd = k.inp("bd", [D_, 1])
    Wb = k.inp("Wb", [D_, 16])
    bb = k.inp("bb", [16, 1])
    Wc = k.inp("Wc", [D_, 16])
    bc = k.inp("bc", [16, 1])
    A_ = k.inp("A", [D_, 16])
    xn_out = k.outp("xn_out", [D_, TC], BF16)
    a_out = k.outp("a_out", [1, TC])
    b_out = k.outp("b_out", [16, TC])
    ct_out = k.outp("ct_out", [16, TC])

    raw = k.sbuf("raw", [128, KC * TC], BF16)
    sq = k.sbuf("sq", [128, KC * TC], BF16)
    xn = k.sbuf("xn", [128, KC * TC], BF16)
    scr = k.sbuf("scr", [128, TC])
    scr2 = k.sbuf("scr2", [128, TC])
    stat = k.sbuf("stat", [1, 3 * TC])
    wd_sb = k.sbuf("wd_sb", [128, KC * D_], BF16)
    wbc_sb = k.sbuf("wbc_sb", [128, KC * 32], BF16)
    asb = k.sbuf("asb", [128, KC * 16], BF16)
    bias_sb = k.sbuf("bias_sb", [128, KC])
    bb_sb = k.sbuf("bb_sb", [16, 2])
    ones_k = k.sbuf("ones_k", [128, 1], BF16)
    ones16 = k.sbuf("ones16", [16, 1], BF16)
    ones1f = k.sbuf("ones1f", [1, 128])
    small = k.sbuf("small", [16, TC])
    small2 = k.sbuf("small2", [16, TC])
    small3 = k.sbuf("small3", [16, TC])
    srow = k.sbuf("srow", [1, 2 * TC])
    sa_sb = k.sbuf("sa_sb", [16, 1])
    sa_row = k.sbuf("sa_row", [1, 16])
    eps5 = k.sbuf("eps5", [1, 1])

    psA = k.psum("psA", [128, 2048])
    psB = k.psum("psB", [128, 2048])

    LD = k.sem("LD")
    LDG = k.sem("LDG")
    ST = k.sem("ST")

    for c in range(KC):
        k.dma("gpsimd", raw[:, c * TC:(c + 1) * TC], xT[c * 128:(c + 1) * 128, :], LDG)
        k.dma("gpsimd", wd_sb[:, c * D_:(c + 1) * D_], Wd[c * 128:(c + 1) * 128, :], LDG)
        k.dma("gpsimd", wbc_sb[:, c * 32:c * 32 + 16], Wb[c * 128:(c + 1) * 128, :], LDG)
        k.dma("gpsimd", wbc_sb[:, c * 32 + 16:c * 32 + 32], Wc[c * 128:(c + 1) * 128, :], LDG)
        k.dma("gpsimd", asb[:, c * 16:(c + 1) * 16], A_[c * 128:(c + 1) * 128, :], LDG)
        k.dma("sync", bias_sb[:, c:c + 1], bd[c * 128:(c + 1) * 128, :], LD)
    k.dma("sync", bb_sb[0:16, 0:1], bb[:, :], LD)
    k.dma("sync", bb_sb[0:16, 1:2], bc[:, :], LD)
    k.vec(lambda e: e.memset(ones_k[:, :], 1.0))
    k.vec(lambda e: e.memset(ones16[:, :], 1.0))
    k.vec(lambda e: e.memset(ones1f[:, :], 1.0))
    k.vec(lambda e: e.memset(eps5[0:1, 0:1], 1e-5))
    for eng in ("vector", "scalar", "tensor"):
        k.wait(eng, LD)
        k.wait(eng, LDG)
    k.vmark()

    m = stat[0:1, 0:TC]
    v = stat[0:1, TC:2 * TC]
    tmp = stat[0:1, 2 * TC:3 * TC]
    k.wait("scalar", k.V)
    for c in range(KC):
        k.sca(lambda e, c=c: e.activation(
            out=sq[:, c * TC:(c + 1) * TC], in_=raw[:, c * TC:(c + 1) * TC],
            func=AF.Square))
    k.smark()
    k.wait("tensor", k.S)
    for n in range(4):
        for c in range(KC):
            k.mm(lambda e, n=n, c=c: e.matmul(
                psA[0:1, n * 512:(n + 1) * 512], ones_k[:, 0:1],
                raw[:, c * TC + n * 512: c * TC + n * 512 + 512],
                start=(c == 0), stop=(c == KC - 1)))
        for c in range(KC):
            k.mm(lambda e, n=n, c=c: e.matmul(
                psB[0:1, n * 512:(n + 1) * 512], ones_k[:, 0:1],
                sq[:, c * TC + n * 512: c * TC + n * 512 + 512],
                start=(c == 0), stop=(c == KC - 1)))
    k.tmark()
    k.wait("vector", k.T)
    k.vec(lambda e: e.tensor_scalar(m, psA[0:1, 0:TC], 1.0 / D_, None, ALU.mult))
    k.vec(lambda e: e.tensor_scalar(v, psB[0:1, 0:TC], 1.0 / D_, None, ALU.mult))
    k.vec(lambda e: e.tensor_tensor(tmp, m, m, ALU.mult))
    k.vec(lambda e: e.tensor_tensor(v, v, tmp, ALU.subtract))
    k.vmark()
    k.wait("scalar", k.V)
    k.sca(lambda e: e.activation(out=tmp, in_=v, func=AF.Ln, bias=eps5[0:1, 0:1]))
    k.sca(lambda e: e.activation(out=v, in_=tmp, func=AF.Exp, scale=-0.5))
    k.smark()
    k.wait("tensor", k.S)
    for n in range(4):
        k.mm(lambda e, n=n: e.matmul(
            psA[:, n * 512:(n + 1) * 512], ones1f[0:1, :],
            m[:, n * 512:(n + 1) * 512], start=True, stop=True))
        k.mm(lambda e, n=n: e.matmul(
            psB[:, n * 512:(n + 1) * 512], ones1f[0:1, :],
            v[:, n * 512:(n + 1) * 512], start=True, stop=True))
    k.tmark()
    k.wait("vector", k.T)
    for c in range(KC):
        k.vec(lambda e, c=c: e.tensor_tensor(
            scr[:, :], raw[:, c * TC:(c + 1) * TC], psA[:, 0:TC], ALU.subtract))
        k.vec(lambda e, c=c: e.tensor_tensor(
            xn[:, c * TC:(c + 1) * TC], scr[:, :], psB[:, 0:TC], ALU.mult))
    k.vmark()
    k.wait("sync", k.V)
    for c in range(KC):
        k.dma("sync", xn_out[c * 128:(c + 1) * 128, :], xn[:, c * TC:(c + 1) * TC], ST)
    k.barrier()

    for mi in range(KC):
        ps = psA if mi % 2 == 0 else psB
        if mi >= 2:
            k.wait("tensor", k.S)
        for n in range(4):
            for c in range(KC):
                k.mm(lambda e, mi=mi, n=n, c=c, ps=ps: e.matmul(
                    ps[:, n * 512:(n + 1) * 512],
                    wd_sb[:, c * D_ + mi * 128: c * D_ + (mi + 1) * 128],
                    xn[:, c * TC + n * 512: c * TC + n * 512 + 512],
                    start=(c == 0), stop=(c == KC - 1)))
        k.tmark()
        k.wait("scalar", k.T)
        k.wait("scalar", k.V)
        k.sca(lambda e, mi=mi, ps=ps: e.activation(
            out=scr[:, :], in_=ps[:, 0:TC], func=AF.Relu, bias=bias_sb[:, mi:mi + 1]))
        k.sca(lambda e, mi=mi, ps=ps: e.activation(
            out=scr2[:, :], in_=ps[:, 0:TC], func=AF.Abs, bias=bias_sb[:, mi:mi + 1]))
        k.sca(lambda e: e.activation(out=scr2[:, :], in_=scr2[:, :], func=AF.Exp,
                                     scale=-1.0))
        k.sca(lambda e: e.activation(out=scr2[:, :], in_=scr2[:, :], func=AF.Ln,
                                     bias=1.0))
        k.smark()
        k.wait("vector", k.S)
        k.vec(lambda e, mi=mi: e.tensor_tensor(
            sq[:, mi * TC:(mi + 1) * TC], scr[:, :], scr2[:, :], ALU.add))
    k.vmark()
    k.barrier()

    dbar = srow[0:1, 0:TC]
    arow = srow[0:1, TC:2 * TC]
    k.wait("tensor", k.V)
    for n in range(4):
        for c in range(KC):
            k.mm(lambda e, n=n, c=c: e.matmul(
                psA[0:1, n * 512:(n + 1) * 512], ones_k[:, 0:1],
                sq[:, c * TC + n * 512: c * TC + n * 512 + 512],
                start=(c == 0), stop=(c == KC - 1)))
    k.tmark()
    k.wait("vector", k.T)
    k.vec(lambda e: e.tensor_scalar(dbar, psA[0:1, 0:TC], 1.0 / D_, None, ALU.mult))
    k.vec(lambda e: e.tensor_scalar(arow, dbar, -1.0, 1.0, ALU.mult, ALU.add))
    k.vmark()
    k.wait("sync", k.V)
    k.dma("sync", a_out[:, :], arow, ST)

    for j, dst in ((0, small), (1, small2)):
        k.wait("tensor", k.V)
        for n in range(4):
            for c in range(KC):
                k.mm(lambda e, j=j, n=n, c=c: e.matmul(
                    psB[0:16, n * 512:(n + 1) * 512],
                    wbc_sb[:, c * 32 + 16 * j: c * 32 + 16 * j + 16],
                    xn[:, c * TC + n * 512: c * TC + n * 512 + 512],
                    start=(c == 0), stop=(c == KC - 1)))
        k.tmark()
        k.wait("vector", k.T)
        k.vec(lambda e, j=j, dst=dst: e.tensor_scalar(
            dst[0:16, :], psB[0:16, 0:TC], bb_sb[0:16, j:j + 1], None, ALU.add))
    k.vmark()
    k.wait("sync", k.V)
    k.dma("sync", ct_out[:, :], small2[0:16, :], ST)

    k.wait("tensor", k.V)
    for c in range(KC):
        k.mm(lambda e, c=c: e.matmul(
            psA[0:1, 0:16], ones_k[:, 0:1], asb[:, c * 16:(c + 1) * 16],
            start=(c == 0), stop=(c == KC - 1)))
    k.tmark()
    k.wait("vector", k.T)
    k.vec(lambda e: e.tensor_scalar(sa_row[0:1, :], psA[0:1, 0:16], 1.0 / D_,
                                    None, ALU.mult))
    k.vmark()
    k.wait("scalar", k.V)
    k.sca(lambda e: e.activation(out=sa_row[0:1, :], in_=sa_row[0:1, :],
                                 func=AF.Exp, scale=-1.0))
    k.smark()
    k.wait("vector", k.S)
    k.vec(lambda e: e.tensor_scalar(sa_row[0:1, :], sa_row[0:1, :], 1.0, None, ALU.add))
    k.vec(lambda e: e.reciprocal(sa_row[0:1, :], sa_row[0:1, :]))
    k.vmark()
    k.wait("tensor", k.V)
    k.mm(lambda e: e.matmul(psB[0:16, 0:1], sa_row[0:1, 0:16], ones1f[0:1, 0:1],
                            start=True, stop=True))
    k.tmark()
    k.wait("vector", k.T)
    k.vec(lambda e: e.tensor_copy(sa_sb[0:16, :], psB[0:16, 0:1]))
    k.vmark()

    k.wait("tensor", k.V)
    for n in range(4):
        k.mm(lambda e, n=n: e.matmul(
            psA[0:16, n * 512:(n + 1) * 512], ones1f[0:1, 0:16],
            dbar[:, n * 512:(n + 1) * 512], start=True, stop=True))
    k.tmark()
    k.wait("vector", k.T)
    k.vec(lambda e: e.tensor_tensor(small3[0:16, :], psA[0:16, 0:TC], small[0:16, :],
                                    ALU.mult))
    k.vec(lambda e: e.tensor_scalar(small3[0:16, :], small3[0:16, :], sa_sb[0:16, 0:1],
                                    None, ALU.mult))
    k.vmark()
    k.wait("sync", k.V)
    k.dma("sync", b_out[:, :], small3[0:16, :], ST)
    k.wait("sync", ST)
    return k.build()


def build_B():
    k = KB()
    a_in = k.inp("a_in", [4, T_])
    b_in = k.inp("b_in", [4, T_])
    hs_out = k.outp("hs_out", [4, T_])
    at = k.sbuf("at", [4, T_])
    bt = k.sbuf("bt", [4, T_])
    ht = k.sbuf("ht", [4, T_])
    LD = k.sem("LD")
    ST = k.sem("ST")
    k.dma("sync", at[:, :], a_in[:, :], LD)
    k.dma("sync", bt[:, :], b_in[:, :], LD)
    k.wait("vector", LD)
    k.vec(lambda e: e.tensor_tensor_scan(ht[:, :], at[:, :], bt[:, :], 0.0,
                                         ALU.mult, ALU.add))
    k.vmark()
    k.wait("sync", k.V)
    k.dma("sync", hs_out[:, :], ht[:, :], ST)
    k.wait("sync", ST)
    return k.build()



def ln_inplace(k, xsb, bx, xnb, scr, stat, ones_k, ones1f, psA, psB, eps_ap):
    m = stat[0:1, 0:TC]
    v = stat[0:1, TC:2 * TC]
    tmp = psA[64:65, 0:TC]
    for c in range(KC):
        k.vec(lambda e, c=c: e.tensor_copy(
            bx[:, c * TC:(c + 1) * TC], xsb[:, c * TC:(c + 1) * TC]))
    k.wait("tensor", k.V)
    for n in range(4):
        for c in range(KC):
            k.mm(lambda e, n=n, c=c: e.matmul(
                psA[0:1, n * 512:(n + 1) * 512], ones_k[:, 0:1],
                bx[:, c * TC + n * 512: c * TC + n * 512 + 512],
                start=(c == 0), stop=(c == KC - 1)))
    k.tlast()
    k.wait("scalar", k.T)
    for c in range(KC):
        k.sca(lambda e, c=c: e.activation(
            out=bx[:, c * TC:(c + 1) * TC], in_=bx[:, c * TC:(c + 1) * TC],
            func=AF.Square))
    k.wait("tensor", k.S)
    for n in range(4):
        for c in range(KC):
            k.mm(lambda e, n=n, c=c: e.matmul(
                psB[0:1, n * 512:(n + 1) * 512], ones_k[:, 0:1],
                bx[:, c * TC + n * 512: c * TC + n * 512 + 512],
                start=(c == 0), stop=(c == KC - 1)))
    k.tlast()
    k.wait("vector", k.T)
    k.vec(lambda e: e.tensor_scalar(m, psA[0:1, 0:TC], 1.0 / D_, None, ALU.mult))
    k.vec(lambda e: e.tensor_scalar(v, psB[0:1, 0:TC], 1.0 / D_, None, ALU.mult))
    k.vec(lambda e: e.tensor_tensor(tmp, m, m, ALU.mult))
    k.vec(lambda e: e.tensor_tensor(v, v, tmp, ALU.subtract))
    k.wait("scalar", k.V)
    k.sca(lambda e: e.activation(out=tmp, in_=v, func=AF.Ln, bias=eps_ap))
    k.sca(lambda e: e.activation(out=v, in_=tmp, func=AF.Exp, scale=-0.5))
    k.wait("tensor", k.S)
    k.wait("tensor", k.V)
    for n in range(4):
        k.mm(lambda e, n=n: e.matmul(
            psA[:, n * 512:(n + 1) * 512], ones1f[0:1, :],
            m[:, n * 512:(n + 1) * 512], start=True, stop=True))
        k.mm(lambda e, n=n: e.matmul(
            psB[:, n * 512:(n + 1) * 512], ones1f[0:1, :],
            v[:, n * 512:(n + 1) * 512], start=True, stop=True))
    k.tlast()
    k.wait("vector", k.T)
    for c in range(KC):
        k.vec(lambda e, c=c: e.tensor_tensor(
            scr[:, :], xsb[:, c * TC:(c + 1) * TC], psA[:, 0:TC], ALU.subtract))
        k.vec(lambda e, c=c: e.tensor_tensor(
            xnb[:, c * TC:(c + 1) * TC], scr[:, :], psB[:, 0:TC], ALU.mult))


def build_C():
    k = KB()
    xT = k.inp("xT", [D_, TC])
    xn_in = k.inp("xn_in", [D_, TC], BF16)
    hs_in = k.inp("hs_in", [16, TC])
    ct_in = k.inp("ct_in", [16, TC])
    Wo = k.inp("Wo", [D_, D_])
    bo = k.inp("bo", [D_, 1])
    mk_bd = k.inp("mk_bd", [4 * 8, 128, 128])
    up_bd = k.inp("up_bd", [4 * 8, 128, 512])
    wcp = k.inp("wcp", [128 * 128, D_])
    WqT = k.inp("WqT", [D_, D_])
    WkT = k.inp("WkT", [D_, D_])
    WvT = k.inp("WvT", [D_, D_])
    wcp16 = k.dram("wcp16", [128 * 128, D_], BF16)
    x2_out = k.outp("x2_out", [D_, TC])
    q_out = k.outp("q_out", [D_, TC], BF16)
    k_out = k.outp("k_out", [D_, TC], BF16)
    v_out = k.outp("v_out", [TC, D_], BF16)
    qn2_out = k.outp("qn2_out", [16, TC])
    kn2_out = k.outp("kn2_out", [16, TC])

    xsb = k.sbuf("xsb", [128, KC * TC])
    bx = k.sbuf("bx", [128, KC * TC], BF16)
    xnb = k.sbuf("xnb", [128, KC * TC], BF16)
    c8 = k.sbuf("c8", [128, 32 * 512], BF16)
    scr = k.sbuf("scr", [128, TC])
    stat = k.sbuf("stat", [1, 2 * TC])
    wst = k.sbuf("wst", [128, 2048], BF16)
    hbuf = k.sbuf("hbuf", [128, 8 * 256], BF16)
    stage = k.sbuf("stage", [128, TC], BF16)
    stage2 = hbuf
    small = k.sbuf("small", [16, TC])
    rows = small[0:1, 0:TC]
    bo_sb = k.sbuf("bo_sb", [128, KC])
    ones_k = k.sbuf("ones_k", [128, 1], BF16)
    ones1f = k.sbuf("ones1f", [1, 128])
    eps5 = k.sbuf("eps5", [1, 1])
    psA = k.psum("psA", [128, 2048])
    psB = k.psum("psB", [128, 2048])

    LD = k.sem("LD")
    LW = (k.sem("LW0"), k.sem("LW1"))
    LWs = k.sem("LWs")
    LB = k.sem("LB")
    CV = k.sem("CV")
    CVL = (k.sem("CVL0"), k.sem("CVL1"))
    ST = k.sem("ST")
    STS = k.sem("STS")
    STR = k.sem("STR")

    for c in range(KC):
        k.dma("sync", xsb[:, c * TC:(c + 1) * TC], xT[c * 128:(c + 1) * 128, :], LD)
        k.dma("sync", xnb[:, c * TC:(c + 1) * TC], xn_in[c * 128:(c + 1) * 128, :], LD)
        k.dma("sync", bo_sb[:, c:c + 1], bo[c * 128:(c + 1) * 128, :], LD)
    k.dma("sync", small[0:16, :], hs_in[:, :], LD)
    k.dma("sync", scr[0:16, :], ct_in[:, :], LD)
    k.vec(lambda e: e.memset(ones_k[:, :], 1.0))
    k.vec(lambda e: e.memset(ones1f[:, :], 1.0))
    k.vec(lambda e: e.memset(eps5[0:1, 0:1], 1e-5))
    for i in range(32):
        k.dma("gpsimd", bx[:, i * 128:(i + 1) * 128], mk_bd[i, :, :], LB)
    for eng in ("vector", "scalar", "tensor"):
        k.wait(eng, LD)
        k.wait(eng, LB)

    twar = []
    vwar = []
    for i in range(32):
        slot = (i % 2) * 1024
        if i >= 2:
            k.wait("gpsimd", k.T, twar[i - 2])
        k.dma("gpsimd", wst[:, slot:slot + 512], up_bd[i, :, :], LW[i % 2])
        k.wait("tensor", LW[i % 2])
        if i >= 2:
            k.wait("tensor", k.V, vwar[i - 2])
        ps = psA if i % 2 == 0 else psB
        k.mm(lambda e, i=i, slot=slot, ps=ps: e.matmul(
            ps[:, 0:512], bx[:, i * 128:(i + 1) * 128], wst[:, slot:slot + 512],
            start=True, stop=True))
        k.tlast()
        twar.append(k.T.n)
        k.wait("vector", k.T)
        k.vec(lambda e, i=i, ps=ps: e.tensor_copy(
            c8[:, i * 512:(i + 1) * 512], ps[:, 0:512]))
        vwar.append(k.V.n)
    k.barrier()

    cv_done = []
    for i in range(128):
        sl = (i % 2) * 1024
        if i >= 2:
            k.wait("gpsimd", CV, cv_done[i - 2])
        k.dma("gpsimd", hbuf[:, sl:sl + 1024], wcp[i * 128:(i + 1) * 128, :],
              CVL[i % 2])
        k.wait("sync", CVL[i % 2])
        k.dma("sync", wcp16[i * 128:(i + 1) * 128, :], hbuf[:, sl:sl + 1024], CV)
        cv_done.append(CV.n)

    k.vec(lambda e: e.tensor_tensor(small[0:16, :], small[0:16, :], scr[0:16, :],
                                    ALU.mult))
    k.vec(lambda e: e.tensor_copy(stage[0:16, :], small[0:16, :]))
    k.wait("tensor", k.V)
    for n in range(4):
        k.mm(lambda e, n=n: e.matmul(
            psB[0:1, n * 512:(n + 1) * 512], ones_k[0:16, 0:1],
            stage[0:16, n * 512:(n + 1) * 512], start=True, stop=True))
    k.tlast()
    k.wait("vector", k.T)
    k.vec(lambda e: e.tensor_copy(rows, psB[0:1, 0:TC]))
    k.wait("tensor", k.V)
    for n in range(4):
        k.mm(lambda e, n=n: e.matmul(
            psA[:, n * 512:(n + 1) * 512], ones1f[0:1, :],
            rows[:, n * 512:(n + 1) * 512], start=True, stop=True))
    k.tlast()
    k.wait("vector", k.T)
    for c in range(KC):
        k.vec(lambda e, c=c: e.tensor_tensor(
            bx[:, c * TC:(c + 1) * TC], xnb[:, c * TC:(c + 1) * TC], psA[:, 0:TC],
            ALU.mult))
    k.barrier()

    twar = []
    vwar = []
    for mi in range(KC):
        ps = psA if mi % 2 == 0 else psB
        slot = (mi % 2) * 1024
        if mi >= 2:
            k.wait("gpsimd", k.T, twar[mi - 2])
        for c in range(KC):
            k.dma("gpsimd", wst[:, slot + c * 128: slot + (c + 1) * 128],
                  Wo[c * 128:(c + 1) * 128, mi * 128:(mi + 1) * 128], LW[mi % 2])
        k.wait("tensor", LW[mi % 2])
        if mi >= 2:
            k.wait("tensor", k.V, vwar[mi - 2])
        for n in range(4):
            for c in range(KC):
                k.mm(lambda e, n=n, c=c, ps=ps, slot=slot: e.matmul(
                    ps[:, n * 512:(n + 1) * 512],
                    wst[:, slot + c * 128: slot + (c + 1) * 128],
                    bx[:, c * TC + n * 512: c * TC + n * 512 + 512],
                    start=(c == 0), stop=(c == KC - 1)))
        k.tlast()
        twar.append(k.T.n)
        k.wait("vector", k.T)
        k.vec(lambda e, mi=mi, ps=ps: e.scalar_tensor_tensor(
            xsb[:, mi * TC:(mi + 1) * TC], ps[:, 0:TC], bo_sb[:, mi:mi + 1],
            xsb[:, mi * TC:(mi + 1) * TC], ALU.add, ALU.add))
        vwar.append(k.V.n)
    k.barrier()

    ln_inplace(k, xsb, bx, xnb, scr, stat, ones_k, ones1f, psA, psB,
               eps5[0:1, 0:1])
    k.barrier()

    NT = 256
    twar_cp = []
    swar = []
    k.wait("sync", CV)
    k.wait("gpsimd", CV)
    k.wait("scalar", CV)
    for tt in range(KC):
        t0 = tt * NT
        for kidx in range(128):
            h, g, msub = kidx // 32, (kidx // 4) % 8, kidx % 4
            poff = (kidx % 2) * 512
            wslot = (kidx % 2) * 1024
            qeng = "sync" if kidx % 2 == 0 else "gpsimd"
            qsem = LWs if kidx % 2 == 0 else LW[1]
            if len(twar_cp) >= 2:
                k.wait(qeng, k.T, twar_cp[-2])
            k.dma(qeng, wst[:, wslot:wslot + 1024],
                  wcp16[kidx * 128:(kidx + 1) * 128, :], qsem)
            if len(swar) >= 2:
                k.wait("tensor", k.S, swar[-2])
            k.mm(lambda e, h=h, g=g, msub=msub, poff=poff, t0=t0: e.matmul(
                psB[:, poff:poff + NT],
                c8[:, (h * 8 + g) * 512 + msub * 128:
                   (h * 8 + g) * 512 + (msub + 1) * 128],
                xnb[:, g * TC + t0: g * TC + t0 + NT],
                start=True, stop=True))
            k.tlast()
            hslot = kidx % 8
            k.wait("scalar", k.T)
            k.sca(lambda e, hslot=hslot, poff=poff: e.activation(
                out=hbuf[:, hslot * NT:(hslot + 1) * NT], in_=psB[:, poff:poff + NT],
                func=AF.Gelu))
            swar.append(k.S.n)
            k.wait("tensor", k.S)
            k.wait("tensor", qsem)
            if tt > 0 and kidx == 0:
                k.wait("tensor", k.V)
            for mo in range(KC):
                k.mm(lambda e, mo=mo, kidx=kidx, hslot=hslot, wslot=wslot: e.matmul(
                    psA[:, mo * NT:(mo + 1) * NT],
                    wst[:, wslot + mo * 128: wslot + (mo + 1) * 128],
                    hbuf[:, hslot * NT:(hslot + 1) * NT],
                    start=(kidx == 0 and mo % 2 == 0),
                    stop=(kidx == 127 and mo % 2 == 1)))
            k.tlast()
            twar_cp.append(k.T.n)
        k.wait("vector", k.T)
        for mo in range(KC):
            k.vec(lambda e, mo=mo, t0=t0: e.tensor_tensor(
                xsb[:, mo * TC + t0: mo * TC + t0 + NT], psA[:, mo * NT:(mo + 1) * NT],
                xsb[:, mo * TC + t0: mo * TC + t0 + NT], ALU.add))
    k.barrier()
    k.wait("sync", k.V)
    for c in range(KC):
        k.dma("sync", x2_out[c * 128:(c + 1) * 128, :], xsb[:, c * TC:(c + 1) * TC], ST)

    ln_inplace(k, xsb, bx, xnb, scr, stat, ones_k, ones1f, psA, psB,
               eps5[0:1, 0:1])
    k.barrier()

    st_stage = None
    st_rows = None
    for which, wt, o_main, o_n2, scale in (
            (0, WqT, q_out, qn2_out, 1.0 / 32.0), (1, WkT, k_out, kn2_out, 1.0)):
        k.wait("gpsimd", k.T)
        for c in range(KC):
            k.dma("gpsimd", c8[:, c * 1024:(c + 1) * 1024],
                  wt[c * 128:(c + 1) * 128, :], LB)
        k.wait("tensor", LB)
        vwar = []
        for mi in range(KC):
            ps = psA if mi % 2 == 0 else psB
            pso = psB if mi % 2 == 0 else psA
            if mi >= 1:
                k.wait("tensor", k.V, vwar[mi - 1])
            for n in range(4):
                for c in range(KC):
                    k.mm(lambda e, mi=mi, n=n, c=c, ps=ps: e.matmul(
                        ps[:, n * 512:(n + 1) * 512],
                        c8[:, c * 1024 + mi * 128: c * 1024 + (mi + 1) * 128],
                        xnb[:, c * TC + n * 512: c * TC + n * 512 + 512],
                        start=(c == 0), stop=(c == KC - 1)))
            k.tlast()
            k.wait("vector", k.T)
            if st_stage is not None:
                k.wait("vector", STS, st_stage)
            k.wait("vector", k.S)
            k.vec(lambda e, ps=ps, scale=scale: e.tensor_scalar(
                stage[:, :], ps[:, 0:TC], scale, None, ALU.mult))
            k.wait("sync", k.V)
            k.dma("sync", o_main[mi * 128:(mi + 1) * 128, :], stage[:, :], STS)
            st_stage = STS.n
            k.wait("scalar", k.V)
            k.sca(lambda e: e.activation(out=stage2[:, 0:TC], in_=stage[:, :],
                                         func=AF.Square))
            k.wait("tensor", k.S)
            for half in (0, 64):
                k.wait("tensor", k.V)
                for n in range(4):
                    k.mm(lambda e, pso=pso, half=half, n=n: e.matmul(
                        pso[0:1, n * 512:(n + 1) * 512],
                        ones_k[half:half + 64, 0:1],
                        stage2[half:half + 64, n * 512:(n + 1) * 512],
                        start=True, stop=True))
                k.tlast()
                k.wait("vector", k.T)
                if st_rows is not None:
                    k.wait("vector", STR, st_rows)
                k.vec(lambda e, pso=pso: e.tensor_copy(rows, pso[0:1, 0:TC]))
                k.wait("sync", k.V)
                k.dma("sync", o_n2[mi * 2 + half // 64: mi * 2 + half // 64 + 1, :],
                      rows, STR)
                st_rows = STR.n
            vwar.append(k.V.n)
        k.barrier()

    k.wait("gpsimd", k.T)
    for c in range(KC):
        k.dma("gpsimd", c8[:, c * 1024:(c + 1) * 1024],
              WvT[c * 128:(c + 1) * 128, :], LB)
    k.wait("tensor", LB)
    vwar = []
    for tokm in range(16):
        ps = psA if tokm % 2 == 0 else psB
        if tokm >= 2:
            k.wait("tensor", k.V, vwar[tokm - 2])
        for dh in range(2):
            for c in range(KC):
                k.mm(lambda e, tokm=tokm, dh=dh, c=c, ps=ps: e.matmul(
                    ps[:, dh * 512:(dh + 1) * 512],
                    xnb[:, c * TC + tokm * 128: c * TC + (tokm + 1) * 128],
                    c8[:, c * 1024 + dh * 512: c * 1024 + (dh + 1) * 512],
                    start=(c == 0), stop=(c == KC - 1)))
        k.tlast()
        k.wait("vector", k.T)
        if st_stage is not None:
            k.wait("vector", STS, st_stage)
        k.vec(lambda e, ps=ps: e.tensor_copy(stage[:, 0:1024], ps[:, 0:1024]))
        vwar.append(k.V.n)
        k.wait("sync", k.V)
        k.dma("sync", v_out[tokm * 128:(tokm + 1) * 128, :], stage[:, 0:1024], STS)
        st_stage = STS.n
    k.wait("sync", ST)
    k.wait("sync", STS)
    k.wait("sync", STR)
    return k.build()


def build_D():
    k = KB()
    qh = k.inp("qh", [4, 64, T_], BF16)
    kh = k.inp("kh", [4, 64, T_], BF16)
    vh = k.inp("vh", [4, T_, 64], BF16)
    qn2h = k.inp("qn2h", [4, T_])
    kn2h = k.inp("kn2h", [4, T_])
    oh = k.outp("oh", [4, 64, T_])

    ktil = k.sbuf("ktil", [65, T_], BF16)
    qtil = k.sbuf("qtil", [65, T_], BF16)
    vt = k.sbuf("vt", [128, 64 * 65], BF16)
    ptil = k.sbuf("ptil", [128, 2048], BF16)
    rowb = k.sbuf("rowb", [1, T_])
    otmp = k.sbuf("otmp", [64, 512])
    rc = k.sbuf("rc", [1, 512])
    sc = k.sbuf("sc", [1, 16])
    ones1f = k.sbuf("ones1f", [1, 128])
    psA = k.psum("psA", [128, 2048])
    psB = k.psum("psB", [128, 2048])

    LDK = k.sem("LDK")
    LDQ = k.sem("LDQ")
    LDV = k.sem("LDV")
    LDR = k.sem("LDR")
    STO = k.sem("STO")

    k.vec(lambda e: e.memset(ktil[64:65, :], 1.0))
    k.vec(lambda e: e.memset(vt[:, :], 1.0))
    k.vec(lambda e: e.memset(ones1f[:, :], 1.0))

    st_o = None
    for p in range(4):
        if p > 0:
            k.wait("gpsimd", k.T)
            k.wait("gpsimd", k.V)
        k.dma("gpsimd", ktil[0:64, :], kh[p, :, :], LDK)
        k.dma("gpsimd", qtil[0:64, :], qh[p, :, :], LDQ)
        for kt in range(64):
            k.dma("gpsimd", vt[:, kt * 65:kt * 65 + 64],
                  vh[p, kt * 128:(kt + 1) * 128, :], LDV)
        k.dma("gpsimd", rowb[0:1, :], kn2h[p:p + 1, :], LDR)
        k.wait("vector", LDR)
        k.vec(lambda e: e.tensor_reduce(sc[0:1, 0:1], rowb[0:1, :], AX, ALU.max))
        k.wait("gpsimd", k.V)
        k.dma("gpsimd", rowb[0:1, :], qn2h[p:p + 1, :], LDR)
        k.wait("vector", LDR)
        k.vec(lambda e: e.tensor_scalar(rowb[0:1, :], rowb[0:1, :], sc[0:1, 0:1],
                                        None, ALU.mult))
        k.wait("scalar", k.V)
        k.sca(lambda e: e.activation(out=rowb[0:1, :], in_=rowb[0:1, :],
                                     func=AF.Sqrt))
        k.wait("vector", k.S)
        k.wait("vector", LDQ)
        k.vec(lambda e: e.tensor_scalar(qtil[64:65, :], rowb[0:1, :], -1.0,
                                        None, ALU.mult))
        k.wait("tensor", LDK)
        k.wait("tensor", LDV)
        k.wait("tensor", k.V)
        k.wait("scalar", k.V)
        swar = []
        twar_pv = []
        vwar_q = None

        for qt in range(16):
            for kg in range(33):
                pslot = kg % 2
                if kg < 32:
                    if len(swar) >= 2:
                        k.wait("tensor", k.S, swar[-2])
                    if kg == 0 and vwar_q is not None:
                        k.wait("tensor", k.V, vwar_q)
                    for j in (0, 1):
                        kt = kg * 2 + j
                        k.mm(lambda e, kt=kt, j=j, pslot=pslot, qt=qt: e.matmul(
                            psA[:, pslot * 1024 + j * 512:
                                pslot * 1024 + (j + 1) * 512],
                            ktil[:, kt * 128:(kt + 1) * 128],
                            qtil[:, qt * 512:(qt + 1) * 512],
                            start=True, stop=True))
                    k.tlast()
                    k.wait("scalar", k.T)
                    if len(twar_pv) >= 2:
                        k.wait("scalar", k.T, twar_pv[-2])
                    k.sca(lambda e, pslot=pslot: e.activation(
                        out=ptil[:, pslot * 1024:(pslot + 1) * 1024],
                        in_=psA[:, pslot * 1024:(pslot + 1) * 1024], func=AF.Exp))
                    swar.append(k.S.n)
                if kg >= 1:
                    pv = kg - 1
                    pvslot = pv % 2
                    k.wait("tensor", k.S, swar[-1] if kg == 32 else swar[-2])
                    for j in (0, 1):
                        kt = pv * 2 + j
                        k.mm(lambda e, kt=kt, j=j, pvslot=pvslot: e.matmul(
                            psB[0:65, 0:512], vt[:, kt * 65:(kt + 1) * 65],
                            ptil[:, pvslot * 1024 + j * 512:
                                 pvslot * 1024 + (j + 1) * 512],
                            start=(kt == 0), stop=(kt == 63)))
                    k.tlast()
                    twar_pv.append(k.T.n)
            k.wait("vector", k.T)
            if st_o is not None:
                k.wait("vector", STO, st_o)
            k.vec(lambda e: e.reciprocal(rc[0:1, :], psB[64:65, 0:512]))
            k.wait("tensor", k.V)
            k.mm(lambda e: e.matmul(psA[0:64, 0:512], ones1f[0:1, 0:64],
                                    rc[0:1, :], start=True, stop=True))
            k.tlast()
            k.wait("vector", k.T)
            k.vec(lambda e: e.tensor_copy(otmp[:, :], psB[0:64, 0:512]))
            k.vec(lambda e: e.tensor_tensor(otmp[:, :], otmp[:, :],
                                            psA[0:64, 0:512], ALU.mult))
            vwar_q = k.V.n
            k.wait("sync", k.V)
            k.dma("sync", oh[p, :, qt * 512:(qt + 1) * 512], otmp[:, :], STO)
            st_o = STO.n
    
    k.wait("sync", STO)
    return k.build()


def build_E():
    twar_w1 = []
    twar_w2 = []
    vwar_w2 = []
    """Residual + double-LN (folded) + router/top-2 + dense 8-expert MoE.
    W1 feature-major; W2 token-major (gate = per-partition scalar at evac);
    expert contributions accumulated in HBM via DMA accum_op=add."""
    k = KB()
    xT2 = k.inp("xT2", [D_, TC])
    oT = k.inp("oT", [D_, TC])
    rls = k.inp("rls", [D_, 1])
    rlb = k.inp("rlb", [D_, 1])
    rwT = k.inp("rwT", [D_, 8])
    rb_in = k.inp("rb_in", [1, 8])
    els = k.inp("els", [8, D_])
    elb = k.inp("elb", [8, D_])
    w1t = k.inp("w1t", [8, D_, 4096])
    b1_in = k.inp("b1_in", [8, 4096])
    w2t = k.inp("w2t", [8, 4096, D_])
    b2_in = k.inp("b2_in", [8, D_])
    eyef = k.inp("eyef", [128, 128])
    out = k.outp("out", [TC, D_])

    xsb = k.sbuf("xsb", [128, KC * TC])
    xhb = k.sbuf("xhb", [128, KC * TC], BF16)
    xeb = k.sbuf("xeb", [128, KC * TC], BF16)
    hq = k.sbuf("hq", [128, KC * TC], BF16)
    w2res = k.sbuf("w2res", [128, KC * 1024], BF16)
    wst = k.sbuf("wst", [128, 2048], BF16)
    stage = k.sbuf("stage", [128, 1024])
    stat = k.sbuf("stat", [1, 2 * TC])
    gates = k.sbuf("gates", [128, 16 * 8])
    rwb = k.sbuf("rwb", [128, KC * 8], BF16)
    rwu = k.sbuf("rwu", [128, KC * 8], BF16)
    rlbb = k.sbuf("rlbb", [128, KC], BF16)
    lsb = k.sbuf("lsb", [128, 2 * KC])
    b1sb = k.sbuf("b1sb", [128, KC])
    b2r = k.sbuf("b2r", [1, D_], BF16)
    onesb = k.sbuf("onesb", [1, 128], BF16)
    rbr = k.sbuf("rbr", [1, 8])
    eyf = k.sbuf("eyf", [128, 128])
    scp = k.sbuf("scp", [128, 16])
    bcb = k.sbuf("bcb", [128, 8])
    ones_k = k.sbuf("ones_k", [128, 1], BF16)
    ones1f = k.sbuf("ones1f", [1, 128])
    eps5 = k.sbuf("eps5", [1, 2])
    psA = k.psum("psA", [128, 2048])
    psB = k.psum("psB", [128, 2048])

    LD = k.sem("LD")
    LO = k.sem("LO")
    LB = k.sem("LB")
    LWq = k.sem("LWq")
    LW = (k.sem("LW0"), k.sem("LW1"))
    STA = (k.sem("STA0"), k.sem("STA1"))
    STX = (k.sem("STX0"), k.sem("STX1"))

    for c in range(KC):
        k.dma("sync", xsb[:, c * TC:(c + 1) * TC], xT2[c * 128:(c + 1) * 128, :], LD)
        k.dma("gpsimd", rwu[:, c * 8:(c + 1) * 8], rwT[c * 128:(c + 1) * 128, :], LB)
        k.dma("sync", lsb[:, c:c + 1], rls[c * 128:(c + 1) * 128, :], LD)
        k.dma("sync", lsb[:, KC + c:KC + c + 1], rlb[c * 128:(c + 1) * 128, :], LD)
    k.dma("sync", eyf[:, :], eyef[:, :], LD)
    k.dma("sync", rbr[0:1, :], rb_in[:, :], LD)
    k.vec(lambda e: e.memset(onesb[:, :], 1.0))
    k.vec(lambda e: e.memset(ones_k[:, :], 1.0))
    k.vec(lambda e: e.memset(ones1f[:, :], 1.0))
    k.vec(lambda e: e.memset(eps5[0:1, 0:1], 1e-5))
    k.vec(lambda e: e.memset(eps5[0:1, 1:2], 1e-6))
    for eng in ("vector", "scalar", "tensor"):
        k.wait(eng, LD)
        k.wait(eng, LB)

    for c in range(KC):
        for hhalf in range(2):
            if c > 0 or hhalf > 0:
                k.wait("gpsimd", k.V)
            k.dma("gpsimd", stage[:, :],
                  oT[c * 128:(c + 1) * 128, hhalf * 1024:(hhalf + 1) * 1024], LO)
            k.wait("vector", LO)
            k.vec(lambda e, c=c, hhalf=hhalf: e.tensor_tensor(
                xsb[:, c * TC + hhalf * 1024: c * TC + (hhalf + 1) * 1024],
                xsb[:, c * TC + hhalf * 1024: c * TC + (hhalf + 1) * 1024],
                stage[:, :], ALU.add))
    k.barrier()

    m = stat[0:1, 0:TC]
    v = stat[0:1, TC:2 * TC]
    tmp = psA[64:65, 0:TC]
    r1row = psB[64:65, 0:TC]
    for c in range(KC):
        k.vec(lambda e, c=c: e.tensor_copy(
            xeb[:, c * TC:(c + 1) * TC], xsb[:, c * TC:(c + 1) * TC]))
    k.wait("tensor", k.V)
    for n in range(4):
        for c in range(KC):
            k.mm(lambda e, n=n, c=c: e.matmul(
                psA[0:1, n * 512:(n + 1) * 512], ones_k[:, 0:1],
                xeb[:, c * TC + n * 512: c * TC + n * 512 + 512],
                start=(c == 0), stop=(c == KC - 1)))
    k.tlast()
    k.wait("scalar", k.T)
    for c in range(KC):
        k.sca(lambda e, c=c: e.activation(
            out=xeb[:, c * TC:(c + 1) * TC], in_=xeb[:, c * TC:(c + 1) * TC],
            func=AF.Square))
    k.wait("tensor", k.S)
    for n in range(4):
        for c in range(KC):
            k.mm(lambda e, n=n, c=c: e.matmul(
                psB[0:1, n * 512:(n + 1) * 512], ones_k[:, 0:1],
                xeb[:, c * TC + n * 512: c * TC + n * 512 + 512],
                start=(c == 0), stop=(c == KC - 1)))
    k.tlast()
    k.wait("vector", k.T)
    k.vec(lambda e: e.tensor_scalar(m, psA[0:1, 0:TC], 1.0 / D_, None, ALU.mult))
    k.vec(lambda e: e.tensor_scalar(v, psB[0:1, 0:TC], 1.0 / D_, None, ALU.mult))
    k.vec(lambda e: e.tensor_tensor(tmp, m, m, ALU.mult))
    k.vec(lambda e: e.tensor_tensor(v, v, tmp, ALU.subtract))
    k.wait("scalar", k.V)
    k.sca(lambda e: e.activation(out=tmp, in_=v, func=AF.Ln, bias=eps5[0:1, 0:1]))
    k.sca(lambda e: e.activation(out=r1row, in_=tmp, func=AF.Exp, scale=-0.5))
    k.wait("vector", k.S)
    k.vec(lambda e: e.tensor_tensor(v, v, r1row, ALU.mult))
    k.vec(lambda e: e.tensor_tensor(v, v, r1row, ALU.mult))
    k.wait("scalar", k.V)
    k.sca(lambda e: e.activation(out=tmp, in_=v, func=AF.Ln, bias=eps5[0:1, 1:2]))
    k.sca(lambda e: e.activation(out=v, in_=tmp, func=AF.Exp, scale=-0.5))
    k.wait("vector", k.S)
    k.vec(lambda e: e.tensor_tensor(v, v, r1row, ALU.mult))
    k.wait("tensor", k.V)
    k.wait("tensor", k.S)
    for n in range(4):
        k.mm(lambda e, n=n: e.matmul(
            psA[:, n * 512:(n + 1) * 512], ones1f[0:1, :],
            m[:, n * 512:(n + 1) * 512], start=True, stop=True))
        k.mm(lambda e, n=n: e.matmul(
            psB[:, n * 512:(n + 1) * 512], ones1f[0:1, :],
            v[:, n * 512:(n + 1) * 512], start=True, stop=True))
    k.tlast()
    k.wait("vector", k.T)
    for c in range(KC):
        for hhalf in range(2):
            k.vec(lambda e, c=c, hhalf=hhalf: e.tensor_tensor(
                stage[:, :],
                xsb[:, c * TC + hhalf * 1024: c * TC + (hhalf + 1) * 1024],
                psA[:, hhalf * 1024:(hhalf + 1) * 1024], ALU.subtract))
            k.vec(lambda e, c=c, hhalf=hhalf: e.tensor_tensor(
                xhb[:, c * TC + hhalf * 1024: c * TC + (hhalf + 1) * 1024],
                stage[:, :], psB[:, hhalf * 1024:(hhalf + 1) * 1024], ALU.mult))
    k.barrier()

    for c in range(KC):
        k.vec(lambda e, c=c: e.tensor_scalar(
            rwb[:, c * 8:(c + 1) * 8], rwu[:, c * 8:(c + 1) * 8],
            lsb[:, c:c + 1], None, ALU.mult))
        k.vec(lambda e, c=c: e.tensor_copy(rlbb[:, c:c + 1],
                                           lsb[:, KC + c:KC + c + 1]))
    k.wait("tensor", k.V)
    for c in range(KC):
        k.mm(lambda e, c=c: e.matmul(
            psA[0:1, 0:8], rlbb[:, c:c + 1], rwu[:, c * 8:(c + 1) * 8],
            start=(c == 0), stop=(c == KC - 1)))
    k.tlast()
    k.wait("vector", k.T)
    k.vec(lambda e: e.tensor_tensor(rbr[0:1, :], psA[0:1, 0:8], rbr[0:1, :],
                                    ALU.add))
    k.wait("tensor", k.V)
    k.mm(lambda e: e.matmul(psA[:, 0:8], ones1f[0:1, :], rbr[0:1, :],
                            start=True, stop=True))
    k.tlast()
    k.wait("vector", k.T)
    k.vec(lambda e: e.tensor_copy(bcb[:, :], psA[:, 0:8]))
    for tokm in range(16):
        if tokm > 0:
            k.wait("tensor", k.V)
        for c in range(KC):
            k.mm(lambda e, tokm=tokm, c=c: e.matmul(
                psB[:, 0:8], xhb[:, c * TC + tokm * 128: c * TC + (tokm + 1) * 128],
                rwb[:, c * 8:(c + 1) * 8], start=(c == 0), stop=(c == KC - 1)))
        k.tlast()
        k.wait("vector", k.T)
        k.vec(lambda e, tokm=tokm: e.tensor_tensor(
            gates[:, tokm * 8:(tokm + 1) * 8], psB[:, 0:8], bcb[:, :], ALU.add))
    k.barrier()

    for tokm in range(16):
        ga = gates[:, tokm * 8:(tokm + 1) * 8]
        msk = scp[:, 8:16]
        k.vec(lambda e, ga=ga: e.tensor_reduce(scp[:, 0:1], ga, AX, ALU.max))
        k.vec(lambda e: e.tensor_scalar(scp[:, 1:2], scp[:, 0:1], -1.0, None,
                                        ALU.mult))
        k.wait("scalar", k.V)
        k.sca(lambda e, ga=ga: e.activation(
            out=ga, in_=ga, func=AF.Exp, bias=scp[:, 1:2]))
        k.wait("vector", k.S)
        k.vec(lambda e, ga=ga: e.tensor_reduce(scp[:, 2:3], ga, AX, ALU.max))
        k.vec(lambda e, ga=ga, msk=msk: e.tensor_scalar(
            msk, ga, scp[:, 2:3], None, ALU.is_ge))
        k.vec(lambda e, ga=ga, msk=msk: e.scalar_tensor_tensor(
            msk, msk, -1e30, ga, ALU.mult, ALU.add))
        k.vec(lambda e, msk=msk: e.tensor_reduce(scp[:, 3:4], msk, AX, ALU.max))
        k.vec(lambda e, ga=ga, msk=msk: e.tensor_scalar(
            msk, ga, scp[:, 3:4], None, ALU.is_ge))
        k.vec(lambda e: e.tensor_tensor(scp[:, 4:5], scp[:, 2:3], scp[:, 3:4],
                                        ALU.add))
        k.vec(lambda e: e.reciprocal(scp[:, 5:6], scp[:, 4:5]))
        k.vec(lambda e, ga=ga, msk=msk: e.tensor_tensor(ga, ga, msk, ALU.mult))
        k.vec(lambda e, ga=ga: e.tensor_scalar(ga, ga, scp[:, 5:6], None,
                                               ALU.mult))
    k.barrier()

    st_sl = [None, None]
    for tokm in range(16):
        ssl = tokm % 2
        if st_sl[ssl] is not None:
            k.wait("vector", STX[ssl], st_sl[ssl])
        for c in range(KC):
            psl = (c % 2) * 512
            if c >= 2 or tokm > 0:
                k.wait("tensor", k.V)
            k.mm(lambda e, tokm=tokm, c=c, psl=psl: e.transpose(
                psB[:, psl:psl + 128],
                xsb[:, c * TC + tokm * 128: c * TC + (tokm + 1) * 128], eyf[:, :]))
            k.tlast()
            k.wait("vector", k.T)
            k.vec(lambda e, c=c, ssl=ssl, psl=psl: e.tensor_copy(
                stage[:, ssl * 512:ssl * 512 + 512][:, 0:128] if False else
                stage[:, ssl * 512 + 0:ssl * 512 + 512][:, c % 4 * 128:
                                                        (c % 4 + 1) * 128],
                psB[:, psl:psl + 128]))
            if c == 3 or c == 7:
                k.wait("sync", k.V)
                k.dma("sync",
                      out[tokm * 128:(tokm + 1) * 128,
                          (c - 3) * 128:(c + 1) * 128],
                      stage[:, ssl * 512:(ssl + 1) * 512], STX[ssl])
                st_sl[ssl] = STX[ssl].n
                if c == 3:
                    k.wait("vector", STX[ssl], st_sl[ssl])
    k.barrier()

    stx_final = (STX[0].n, STX[1].n)
    st_sl = [None, None]
    first_use = [True, True]
    swar_g = []
    for ei in range(8):
        k.wait("gpsimd", k.V)
        k.wait("gpsimd", k.T)
        k.dma("gpsimd", b2r[0:1, :], b2_in[ei:ei + 1, :], LB)
        for c in range(KC):
            k.dma("gpsimd", lsb[:, c:c + 1],
                  els[ei:ei + 1, c * 128:(c + 1) * 128].rearrange("a p -> p a"), LB)
            k.dma("gpsimd", lsb[:, KC + c:KC + c + 1],
                  elb[ei:ei + 1, c * 128:(c + 1) * 128].rearrange("a p -> p a"), LB)
        k.wait("vector", LB)
        k.wait("vector", k.T)
        for c in range(KC):
            k.vec(lambda e, c=c: e.tensor_scalar(
                xeb[:, c * TC:(c + 1) * TC], xhb[:, c * TC:(c + 1) * TC],
                lsb[:, c:c + 1], lsb[:, KC + c:KC + c + 1], ALU.mult, ALU.add))
        k.wait("tensor", k.V)
        for q in range(4):
            k.wait("gpsimd", k.V)
            k.wait("gpsimd", k.S)
            for mg in range(KC):
                k.dma("gpsimd", b1sb[:, mg:mg + 1],
                      b1_in[ei:ei + 1, q * 1024 + mg * 128: q * 1024 + (mg + 1) * 128
                            ].rearrange("a p -> p a"), LB)
            for mg in range(KC):
                wsl = (mg % 2) * 1024
                if len(twar_w1) >= 2:
                    k.wait("gpsimd", k.T, twar_w1[-2])
                k.dma("gpsimd",
                      wst[:, wsl:wsl + 1024].rearrange("p (c m) -> p c m", m=128),
                      w1t[ei, :, q * 1024 + mg * 128: q * 1024 + (mg + 1) * 128
                          ].rearrange("(c p) m -> p c m", p=128), LW[mg % 2])
                k.wait("tensor", LW[mg % 2])
                k.wait("tensor", LB)
                for n in range(4):
                    psl = ((mg * 4 + n) % 4) * 512
                    if len(swar_g) >= 4:
                        k.wait("tensor", k.S, swar_g[-4])
                    for c in range(KC):
                        k.mm(lambda e, mg=mg, n=n, c=c, wsl=wsl, psl=psl: e.matmul(
                            psB[:, psl + 0:psl + 512],
                            wst[:, wsl + c * 128: wsl + (c + 1) * 128],
                            xeb[:, c * TC + n * 512: c * TC + n * 512 + 512],
                            start=(c == 0), stop=(c == KC - 1)))
                    k.tlast()
                    k.wait("scalar", k.T)
                    k.sca(lambda e, mg=mg, n=n, psl=psl: e.activation(
                        out=hq[:, mg * TC + n * 512: mg * TC + (n + 1) * 512],
                        in_=psB[:, psl:psl + 512], func=AF.Gelu,
                        bias=b1sb[:, mg:mg + 1]))
                    swar_g.append(k.S.n)
                twar_w1.append(k.T.n)
            k.wait("gpsimd", k.T)
            for c in range(KC):
                k.dma("gpsimd", w2res[:, c * 1024:(c + 1) * 1024],
                      w2t[ei, q * 1024 + c * 128: q * 1024 + (c + 1) * 128, :], LWq)
            k.wait("tensor", LWq)
            k.wait("tensor", k.S)
            for tokm in range(16):
                for dh in range(2):
                    psl = ((tokm * 2 + dh) % 4) * 512
                    ssl = (tokm * 2 + dh) % 2
                    if len(twar_w2) >= 4:
                        k.wait("tensor", k.V, vwar_w2[-4])
                    if q == 0:
                        k.mm(lambda e, dh=dh, psl=psl: e.matmul(
                            psA[:, psl:psl + 512], onesb[0:1, :],
                            b2r[0:1, dh * 512:(dh + 1) * 512],
                            start=True, stop=False))
                    for c in range(KC):
                        k.mm(lambda e, tokm=tokm, dh=dh, c=c, psl=psl, q=q: e.matmul(
                            psA[:, psl:psl + 512],
                            hq[:, c * TC + tokm * 128: c * TC + (tokm + 1) * 128],
                            w2res[:, c * 1024 + dh * 512: c * 1024 + (dh + 1) * 512],
                            start=(c == 0 and q > 0), stop=(c == KC - 1)))
                    k.tlast()
                    twar_w2.append(k.T.n)
                    k.wait("vector", k.T)
                    if first_use[ssl]:
                        k.wait("vector", STX[ssl], stx_final[ssl])
                        first_use[ssl] = False
                    if st_sl[ssl] is not None:
                        k.wait("vector", STA[ssl], st_sl[ssl])
                    k.vec(lambda e, ei=ei, tokm=tokm, ssl=ssl, psl=psl: e.tensor_scalar(
                        stage[:, ssl * 512:(ssl + 1) * 512], psA[:, psl:psl + 512],
                        gates[:, tokm * 8 + ei: tokm * 8 + ei + 1], None, ALU.mult))
                    vwar_w2.append(k.V.n)
                    k.wait("gpsimd", k.V)
                    k.dma("gpsimd",
                          out[tokm * 128:(tokm + 1) * 128, dh * 512:(dh + 1) * 512],
                          stage[:, ssl * 512:(ssl + 1) * 512], STA[ssl],
                          accum_op=ALU.add)
                    st_sl[ssl] = STA[ssl].n
    for ssl in range(2):
        if st_sl[ssl] is not None:
            k.wait("sync", STA[ssl], st_sl[ssl])
    return k.build()


def _shard_tok(xT_full, c):
    b = c // 4
    t0 = (c % 4) * TC
    return np.ascontiguousarray(xT_full[b, :, t0:t0 + TC])


def kernel(**inputs):
    f32 = np.float32
    xT = np.ascontiguousarray(inputs["L0_attn_x"].transpose(0, 2, 1))

    ncA = build_A()
    common_A = {
        "Wd": np.ascontiguousarray(inputs["L0_attn_Wd"], dtype=f32),
        "bd": np.ascontiguousarray(inputs["L0_attn_bd"].reshape(D_, 1), dtype=f32),
        "Wb": np.ascontiguousarray(inputs["L0_attn_Wb"], dtype=f32),
        "bb": np.ascontiguousarray(inputs["L0_attn_bb"].reshape(16, 1), dtype=f32),
        "Wc": np.ascontiguousarray(inputs["L0_attn_Wc"], dtype=f32),
        "bc": np.ascontiguousarray(inputs["L0_attn_bc"].reshape(16, 1), dtype=f32),
        "A": np.ascontiguousarray(inputs["L0_attn_A"], dtype=f32),
    }
    in_maps = [dict(common_A, xT=_shard_tok(xT, c)) for c in range(NCORES)]
    resA = run_bass_kernel_spmd(ncA, in_maps, core_ids=list(range(NCORES))).results

    a_full = np.zeros((B_, T_), f32)
    b_full = np.zeros((B_, 16, T_), f32)
    ct_full = np.zeros((B_, 16, T_), f32)
    xn_full = np.zeros((B_, D_, T_), BF)
    for c in range(NCORES):
        b = c // 4
        t0 = (c % 4) * TC
        a_full[b, t0:t0 + TC] = resA[c]["a_out"][0]
        b_full[b, :, t0:t0 + TC] = resA[c]["b_out"]
        ct_full[b, :, t0:t0 + TC] = resA[c]["ct_out"]
        xn_full[b, :, t0:t0 + TC] = resA[c]["xn_out"]

    ncB = build_B()
    a_rep = np.repeat(a_full[:, None, :], 16, axis=1).reshape(32, T_)
    b_rows = b_full.reshape(32, T_)
    in_maps = [{"a_in": np.ascontiguousarray(a_rep[4 * c:4 * c + 4]),
                "b_in": np.ascontiguousarray(b_rows[4 * c:4 * c + 4])}
               for c in range(NCORES)]
    resB = run_bass_kernel_spmd(ncB, in_maps, core_ids=list(range(NCORES))).results
    hs_full = np.concatenate([r["hs_out"] for r in resB], axis=0).reshape(B_, 16, T_)

    ncC = build_C()
    maker = np.asarray(inputs["L0_ffn_rail_maker"], f32)
    up = np.asarray(inputs["L0_ffn_rail_up_proj"], f32)
    mk_bd = np.zeros((32, 128, 128), f32)
    up_bd = np.zeros((32, 128, 512), f32)
    for h in range(4):
        for r in range(64):
            g, i = r // 8, r % 8
            blk = h * 8 + g
            mk_bd[blk, i * 16:(i + 1) * 16, i * 16:(i + 1) * 16] = maker[h, r].T
            up_bd[blk, i * 16:(i + 1) * 16, i * 64:(i + 1) * 64] = up[h, r].T
    common_C = {
        "Wo": np.ascontiguousarray(inputs["L0_attn_Wo"], dtype=f32),
        "bo": np.ascontiguousarray(inputs["L0_attn_bo"].reshape(D_, 1), dtype=f32),
        "mk_bd": mk_bd,
        "up_bd": up_bd,
        "wcp": np.ascontiguousarray(np.asarray(inputs["L0_ffn_w_cproj"], f32).T),
        "WqT": np.ascontiguousarray(np.asarray(inputs["L1_attn_Wq"], f32).T),
        "WkT": np.ascontiguousarray(np.asarray(inputs["L1_attn_Wk"], f32).T),
        "WvT": np.ascontiguousarray(np.asarray(inputs["L1_attn_Wv"], f32).T),
    }
    in_maps = []
    for c in range(NCORES):
        b = c // 4
        t0 = (c % 4) * TC
        in_maps.append(dict(
            common_C,
            xT=_shard_tok(xT, c),
            xn_in=np.ascontiguousarray(xn_full[b, :, t0:t0 + TC]),
            hs_in=np.ascontiguousarray(hs_full[b, :, t0:t0 + TC]),
            ct_in=np.ascontiguousarray(ct_full[b, :, t0:t0 + TC]),
        ))
    resC = run_bass_kernel_spmd(ncC, in_maps, core_ids=list(range(NCORES))).results

    x2_full = np.zeros((B_, D_, T_), f32)
    q_full = np.zeros((B_, D_, T_), BF)
    k_full = np.zeros((B_, D_, T_), BF)
    v_full = np.zeros((B_, T_, D_), BF)
    qn2_full = np.zeros((B_, 16, T_), f32)
    kn2_full = np.zeros((B_, 16, T_), f32)
    for c in range(NCORES):
        b = c // 4
        t0 = (c % 4) * TC
        x2_full[b, :, t0:t0 + TC] = resC[c]["x2_out"]
        q_full[b, :, t0:t0 + TC] = resC[c]["q_out"]
        k_full[b, :, t0:t0 + TC] = resC[c]["k_out"]
        v_full[b, t0:t0 + TC, :] = resC[c]["v_out"]
        qn2_full[b, :, t0:t0 + TC] = resC[c]["qn2_out"]
        kn2_full[b, :, t0:t0 + TC] = resC[c]["kn2_out"]

    ncD = build_D()
    in_maps = []
    for c in range(NCORES):
        b = c // 4
        hh = [4 * (c % 4) + j for j in range(4)]
        qh = np.stack([q_full[b, h * 64:(h + 1) * 64, :] for h in hh])
        kh = np.stack([k_full[b, h * 64:(h + 1) * 64, :] for h in hh])
        vh = np.stack([v_full[b, :, h * 64:(h + 1) * 64] for h in hh])
        qn2h = np.stack([qn2_full[b, h, :] for h in hh])
        kn2h = np.stack([kn2_full[b, h, :] for h in hh])
        in_maps.append({"qh": np.ascontiguousarray(qh),
                        "kh": np.ascontiguousarray(kh),
                        "vh": np.ascontiguousarray(vh),
                        "qn2h": np.ascontiguousarray(qn2h),
                        "kn2h": np.ascontiguousarray(kn2h)})
    resD = run_bass_kernel_spmd(ncD, in_maps, core_ids=list(range(NCORES))).results
    o_full = np.zeros((B_, D_, T_), f32)
    for c in range(NCORES):
        b = c // 4
        for j in range(4):
            h = 4 * (c % 4) + j
            o_full[b, h * 64:(h + 1) * 64, :] = resD[c]["oh"][j]

    ncE = build_E()
    common_E = {
        "rls": np.ascontiguousarray(
            inputs["L1_ffn_router_ln_scale"].reshape(D_, 1), dtype=f32),
        "rlb": np.ascontiguousarray(
            inputs["L1_ffn_router_ln_bias"].reshape(D_, 1), dtype=f32),
        "rwT": np.ascontiguousarray(np.asarray(inputs["L1_ffn_router_w"], f32).T),
        "rb_in": np.ascontiguousarray(
            inputs["L1_ffn_router_b"].reshape(1, 8), dtype=f32),
        "els": np.ascontiguousarray(inputs["L1_ffn_expert_ln_scale"], dtype=f32),
        "elb": np.ascontiguousarray(inputs["L1_ffn_expert_ln_bias"], dtype=f32),
        "w1t": np.ascontiguousarray(
            np.asarray(inputs["L1_ffn_w1"], f32).transpose(0, 2, 1)),
        "b1_in": np.ascontiguousarray(inputs["L1_ffn_b1"], dtype=f32),
        "w2t": np.ascontiguousarray(
            np.asarray(inputs["L1_ffn_w2"], f32).transpose(0, 2, 1)),
        "b2_in": np.ascontiguousarray(inputs["L1_ffn_b2"], dtype=f32),
        "eyef": np.eye(128, dtype=f32),
    }
    in_maps = []
    for c in range(NCORES):
        b = c // 4
        t0 = (c % 4) * TC
        in_maps.append(dict(
            common_E,
            xT2=np.ascontiguousarray(x2_full[b, :, t0:t0 + TC]),
            oT=np.ascontiguousarray(o_full[b, :, t0:t0 + TC]),
        ))
    resE = run_bass_kernel_spmd(ncE, in_maps, core_ids=list(range(NCORES))).results
    out_full = np.zeros((B_, T_, D_), f32)
    for c in range(NCORES):
        b = c // 4
        t0 = (c % 4) * TC
        out_full[b, t0:t0 + TC, :] = resE[c]["out"]
    return out_full


# version 06
# speedup vs baseline: 1.3275x; 1.3275x over previous
import contextlib
import numpy as np
import ml_dtypes
import concourse.bass as bass
import concourse.mybir as mybir
from concourse.bass_utils import run_bass_kernel_spmd

F32 = mybir.dt.float32
BF16 = mybir.dt.bfloat16
ALU = mybir.AluOpType
AF = mybir.ActivationFunctionType
AX = mybir.AxisListType.X
BF = ml_dtypes.bfloat16

NCORES = 8
B_, T_, D_ = 2, 8192, 1024
TC = 2048
KC = 8
ENGINES = ("sync", "vector", "scalar", "tensor", "gpsimd")


class Sem:
    __slots__ = ("name", "handle", "n")

    def __init__(self, name, handle):
        self.name, self.handle, self.n = name, handle, 0


class KB:

    def __init__(self):
        self.nc = bass.Bass("TRN2", target_bir_lowering=False)
        self.es = contextlib.ExitStack()
        self.plan = {e: [] for e in ENGINES}
        self.V = self.sem("Vchain")
        self.S = self.sem("Schain")
        self.T = self.sem("Tchain")
        self.BAR = self.sem("BARs")
        self.bar_k = 0

    def inp(self, name, shape, dtype=F32):
        return self.nc.dram_tensor(name, list(shape), dtype, kind="ExternalInput").ap()

    def outp(self, name, shape, dtype=F32):
        return self.nc.dram_tensor(name, list(shape), dtype, kind="ExternalOutput").ap()

    def dram(self, name, shape, dtype=F32):
        return self.nc.dram_tensor(name, list(shape), dtype, kind="Internal").ap()

    def sbuf(self, name, shape, dtype=F32):
        return self.es.enter_context(self.nc.sbuf_tensor(name, list(shape), dtype))

    def psum(self, name, shape, dtype=F32):
        return self.es.enter_context(self.nc.psum_tensor(name, list(shape), dtype))

    def sem(self, name):
        return Sem(name, self.es.enter_context(self.nc.semaphore(name)))

    def op(self, eng, fn, inc=None, amt=1):
        if inc is not None:
            inc.n += amt
        self.plan[eng].append(("op", fn, inc, amt))

    def wait(self, eng, sem, val=None):
        v = sem.n if val is None else val
        self.plan[eng].append(("wait", sem, v, 0))

    def dma(self, eng, out, in_, sem, **kw):
        self.op(eng, lambda e: e.dma_start(out=out, in_=in_, **kw), inc=sem, amt=16)

    def vec(self, fn):
        self.wait("vector", self.V)
        self.op("vector", fn, inc=self.V)

    def sca(self, fn):
        self.wait("scalar", self.S)
        self.op("scalar", fn, inc=self.S)

    def mm(self, fn):
        self.op("tensor", fn)

    def tlast(self):
        kind, fn, inc, amt = self.plan["tensor"][-1]
        assert kind == "op" and inc is None
        self.T.n += 1
        self.plan["tensor"][-1] = ("op", fn, self.T, 1)

    def tmark(self):
        self.tlast()

    def vmark(self):
        pass

    def smark(self):
        pass

    def barrier(self):
        self.bar_k += 1
        self.wait("vector", self.V)
        self.wait("scalar", self.S)
        for eng in ENGINES:
            self.op(eng, lambda e: e.nop(), inc=self.BAR)
        tgt = 5 * self.bar_k
        for eng in ENGINES:
            self.plan[eng].append(("wait", self.BAR, tgt, 0))

    def build(self):
        with self.nc.Block() as block:
            for eng in ENGINES:
                items = self.plan[eng]
                if not items:
                    continue

                def emit(e, items=items):
                    for it in items:
                        if it[0] == "wait":
                            e.wait_ge(it[1].handle, it[2])
                        else:
                            ins = it[1](e)
                            if it[2] is not None:
                                ins.then_inc(it[2].handle, it[3])
                getattr(block, eng)(emit)
        self.es.close()
        return self.nc


def ln_normalize(k, x_f32, raw_b16, sq_b16, xn_b16, scr32, stat, ones_k, ones1f,
                 psA, psB, eps_ap, cast_raw=True):
    m = stat[0:1, 0:TC]
    v = stat[0:1, TC:2 * TC]
    tmp = stat[0:1, 2 * TC:3 * TC]
    if cast_raw:
        for c in range(KC):
            k.vec(lambda e, c=c: e.tensor_copy(
                raw_b16[:, c * TC:(c + 1) * TC], x_f32[:, c * TC:(c + 1) * TC]))
        k.vmark()
    k.wait("scalar", k.V)
    for c in range(KC):
        k.sca(lambda e, c=c: e.activation(
            out=sq_b16[:, c * TC:(c + 1) * TC], in_=raw_b16[:, c * TC:(c + 1) * TC],
            func=AF.Square))
    k.smark()
    k.wait("tensor", k.V)
    k.wait("tensor", k.S)
    for n in range(4):
        for c in range(KC):
            k.mm(lambda e, n=n, c=c: e.matmul(
                psA[0:1, n * 512:(n + 1) * 512], ones_k[:, 0:1],
                raw_b16[:, c * TC + n * 512: c * TC + n * 512 + 512],
                start=(c == 0), stop=(c == KC - 1)))
        for c in range(KC):
            k.mm(lambda e, n=n, c=c: e.matmul(
                psB[0:1, n * 512:(n + 1) * 512], ones_k[:, 0:1],
                sq_b16[:, c * TC + n * 512: c * TC + n * 512 + 512],
                start=(c == 0), stop=(c == KC - 1)))
    k.tmark()
    k.wait("vector", k.T)
    k.vec(lambda e: e.tensor_scalar(m, psA[0:1, 0:TC], 1.0 / D_, None, ALU.mult))
    k.vec(lambda e: e.tensor_scalar(v, psB[0:1, 0:TC], 1.0 / D_, None, ALU.mult))
    k.vec(lambda e: e.tensor_tensor(tmp, m, m, ALU.mult))
    k.vec(lambda e: e.tensor_tensor(v, v, tmp, ALU.subtract))
    k.vmark()
    k.wait("scalar", k.V)
    k.sca(lambda e: e.activation(out=tmp, in_=v, func=AF.Ln, bias=eps_ap))
    k.sca(lambda e: e.activation(out=v, in_=tmp, func=AF.Exp, scale=-0.5))
    k.smark()
    k.wait("tensor", k.S)
    for n in range(4):
        k.mm(lambda e, n=n: e.matmul(
            psA[:, n * 512:(n + 1) * 512], ones1f[0:1, :], m[:, n * 512:(n + 1) * 512],
            start=True, stop=True))
        k.mm(lambda e, n=n: e.matmul(
            psB[:, n * 512:(n + 1) * 512], ones1f[0:1, :], v[:, n * 512:(n + 1) * 512],
            start=True, stop=True))
    k.tmark()
    k.wait("vector", k.T)
    for c in range(KC):
        k.vec(lambda e, c=c: e.tensor_tensor(
            scr32[:, :], x_f32[:, c * TC:(c + 1) * TC], psA[:, 0:TC], ALU.subtract))
        k.vec(lambda e, c=c: e.tensor_tensor(
            xn_b16[:, c * TC:(c + 1) * TC], scr32[:, :], psB[:, 0:TC], ALU.mult))
    k.vmark()


def build_A():
    k = KB()
    xT = k.inp("xT", [D_, TC])
    Wd = k.inp("Wd", [D_, D_])
    bd = k.inp("bd", [D_, 1])
    Wb = k.inp("Wb", [D_, 16])
    bb = k.inp("bb", [16, 1])
    Wc = k.inp("Wc", [D_, 16])
    bc = k.inp("bc", [16, 1])
    A_ = k.inp("A", [D_, 16])
    xn_out = k.outp("xn_out", [D_, TC], BF16)
    a_out = k.outp("a_out", [1, TC])
    b_out = k.outp("b_out", [16, TC])
    ct_out = k.outp("ct_out", [16, TC])

    raw = k.sbuf("raw", [128, KC * TC], BF16)
    sq = k.sbuf("sq", [128, KC * TC], BF16)
    xn = k.sbuf("xn", [128, KC * TC], BF16)
    scr = k.sbuf("scr", [128, TC])
    scr2 = k.sbuf("scr2", [128, TC])
    stat = k.sbuf("stat", [1, 3 * TC])
    wd_sb = k.sbuf("wd_sb", [128, KC * D_], BF16)
    wbc_sb = k.sbuf("wbc_sb", [128, KC * 32], BF16)
    asb = k.sbuf("asb", [128, KC * 16], BF16)
    bias_sb = k.sbuf("bias_sb", [128, KC])
    bb_sb = k.sbuf("bb_sb", [16, 2])
    ones_k = k.sbuf("ones_k", [128, 1], BF16)
    ones16 = k.sbuf("ones16", [16, 1], BF16)
    ones1f = k.sbuf("ones1f", [1, 128])
    small = k.sbuf("small", [16, TC])
    small2 = k.sbuf("small2", [16, TC])
    small3 = k.sbuf("small3", [16, TC])
    srow = k.sbuf("srow", [1, 2 * TC])
    sa_sb = k.sbuf("sa_sb", [16, 1])
    sa_row = k.sbuf("sa_row", [1, 16])
    eps5 = k.sbuf("eps5", [1, 1])

    psA = k.psum("psA", [128, 2048])
    psB = k.psum("psB", [128, 2048])

    LD = k.sem("LD")
    LDG = k.sem("LDG")
    ST = k.sem("ST")

    for c in range(KC):
        k.dma("gpsimd", raw[:, c * TC:(c + 1) * TC], xT[c * 128:(c + 1) * 128, :], LDG)
        k.dma("gpsimd", wd_sb[:, c * D_:(c + 1) * D_], Wd[c * 128:(c + 1) * 128, :], LDG)
        k.dma("gpsimd", wbc_sb[:, c * 32:c * 32 + 16], Wb[c * 128:(c + 1) * 128, :], LDG)
        k.dma("gpsimd", wbc_sb[:, c * 32 + 16:c * 32 + 32], Wc[c * 128:(c + 1) * 128, :], LDG)
        k.dma("gpsimd", asb[:, c * 16:(c + 1) * 16], A_[c * 128:(c + 1) * 128, :], LDG)
        k.dma("sync", bias_sb[:, c:c + 1], bd[c * 128:(c + 1) * 128, :], LD)
    k.dma("sync", bb_sb[0:16, 0:1], bb[:, :], LD)
    k.dma("sync", bb_sb[0:16, 1:2], bc[:, :], LD)
    k.vec(lambda e: e.memset(ones_k[:, :], 1.0))
    k.vec(lambda e: e.memset(ones16[:, :], 1.0))
    k.vec(lambda e: e.memset(ones1f[:, :], 1.0))
    k.vec(lambda e: e.memset(eps5[0:1, 0:1], 1e-5))
    for eng in ("vector", "scalar", "tensor"):
        k.wait(eng, LD)
        k.wait(eng, LDG)
    k.vmark()

    m = stat[0:1, 0:TC]
    v = stat[0:1, TC:2 * TC]
    tmp = stat[0:1, 2 * TC:3 * TC]
    k.wait("scalar", k.V)
    for c in range(KC):
        k.sca(lambda e, c=c: e.activation(
            out=sq[:, c * TC:(c + 1) * TC], in_=raw[:, c * TC:(c + 1) * TC],
            func=AF.Square))
    k.smark()
    k.wait("tensor", k.S)
    for n in range(4):
        for c in range(KC):
            k.mm(lambda e, n=n, c=c: e.matmul(
                psA[0:1, n * 512:(n + 1) * 512], ones_k[:, 0:1],
                raw[:, c * TC + n * 512: c * TC + n * 512 + 512],
                start=(c == 0), stop=(c == KC - 1)))
        for c in range(KC):
            k.mm(lambda e, n=n, c=c: e.matmul(
                psB[0:1, n * 512:(n + 1) * 512], ones_k[:, 0:1],
                sq[:, c * TC + n * 512: c * TC + n * 512 + 512],
                start=(c == 0), stop=(c == KC - 1)))
    k.tmark()
    k.wait("vector", k.T)
    k.vec(lambda e: e.tensor_scalar(m, psA[0:1, 0:TC], 1.0 / D_, None, ALU.mult))
    k.vec(lambda e: e.tensor_scalar(v, psB[0:1, 0:TC], 1.0 / D_, None, ALU.mult))
    k.vec(lambda e: e.tensor_tensor(tmp, m, m, ALU.mult))
    k.vec(lambda e: e.tensor_tensor(v, v, tmp, ALU.subtract))
    k.vmark()
    k.wait("scalar", k.V)
    k.sca(lambda e: e.activation(out=tmp, in_=v, func=AF.Ln, bias=eps5[0:1, 0:1]))
    k.sca(lambda e: e.activation(out=v, in_=tmp, func=AF.Exp, scale=-0.5))
    k.smark()
    k.wait("tensor", k.S)
    for n in range(4):
        k.mm(lambda e, n=n: e.matmul(
            psA[:, n * 512:(n + 1) * 512], ones1f[0:1, :],
            m[:, n * 512:(n + 1) * 512], start=True, stop=True))
        k.mm(lambda e, n=n: e.matmul(
            psB[:, n * 512:(n + 1) * 512], ones1f[0:1, :],
            v[:, n * 512:(n + 1) * 512], start=True, stop=True))
    k.tmark()
    k.wait("vector", k.T)
    for c in range(KC):
        k.vec(lambda e, c=c: e.tensor_tensor(
            scr[:, :], raw[:, c * TC:(c + 1) * TC], psA[:, 0:TC], ALU.subtract))
        k.vec(lambda e, c=c: e.tensor_tensor(
            xn[:, c * TC:(c + 1) * TC], scr[:, :], psB[:, 0:TC], ALU.mult))
    k.vmark()
    k.wait("sync", k.V)
    for c in range(KC):
        k.dma("sync", xn_out[c * 128:(c + 1) * 128, :], xn[:, c * TC:(c + 1) * TC], ST)
    k.barrier()

    for mi in range(KC):
        ps = psA if mi % 2 == 0 else psB
        if mi >= 2:
            k.wait("tensor", k.S)
        for n in range(4):
            for c in range(KC):
                k.mm(lambda e, mi=mi, n=n, c=c, ps=ps: e.matmul(
                    ps[:, n * 512:(n + 1) * 512],
                    wd_sb[:, c * D_ + mi * 128: c * D_ + (mi + 1) * 128],
                    xn[:, c * TC + n * 512: c * TC + n * 512 + 512],
                    start=(c == 0), stop=(c == KC - 1)))
        k.tmark()
        k.wait("scalar", k.T)
        k.wait("scalar", k.V)
        k.sca(lambda e, mi=mi, ps=ps: e.activation(
            out=scr[:, :], in_=ps[:, 0:TC], func=AF.Relu, bias=bias_sb[:, mi:mi + 1]))
        k.sca(lambda e, mi=mi, ps=ps: e.activation(
            out=scr2[:, :], in_=ps[:, 0:TC], func=AF.Abs, bias=bias_sb[:, mi:mi + 1]))
        k.sca(lambda e: e.activation(out=scr2[:, :], in_=scr2[:, :], func=AF.Exp,
                                     scale=-1.0))
        k.sca(lambda e: e.activation(out=scr2[:, :], in_=scr2[:, :], func=AF.Ln,
                                     bias=1.0))
        k.smark()
        k.wait("vector", k.S)
        k.vec(lambda e, mi=mi: e.tensor_tensor(
            sq[:, mi * TC:(mi + 1) * TC], scr[:, :], scr2[:, :], ALU.add))
    k.vmark()
    k.barrier()

    dbar = srow[0:1, 0:TC]
    arow = srow[0:1, TC:2 * TC]
    k.wait("tensor", k.V)
    for n in range(4):
        for c in range(KC):
            k.mm(lambda e, n=n, c=c: e.matmul(
                psA[0:1, n * 512:(n + 1) * 512], ones_k[:, 0:1],
                sq[:, c * TC + n * 512: c * TC + n * 512 + 512],
                start=(c == 0), stop=(c == KC - 1)))
    k.tmark()
    k.wait("vector", k.T)
    k.vec(lambda e: e.tensor_scalar(dbar, psA[0:1, 0:TC], 1.0 / D_, None, ALU.mult))
    k.vec(lambda e: e.tensor_scalar(arow, dbar, -1.0, 1.0, ALU.mult, ALU.add))
    k.vmark()
    k.wait("sync", k.V)
    k.dma("sync", a_out[:, :], arow, ST)

    for j, dst in ((0, small), (1, small2)):
        k.wait("tensor", k.V)
        for n in range(4):
            for c in range(KC):
                k.mm(lambda e, j=j, n=n, c=c: e.matmul(
                    psB[0:16, n * 512:(n + 1) * 512],
                    wbc_sb[:, c * 32 + 16 * j: c * 32 + 16 * j + 16],
                    xn[:, c * TC + n * 512: c * TC + n * 512 + 512],
                    start=(c == 0), stop=(c == KC - 1)))
        k.tmark()
        k.wait("vector", k.T)
        k.vec(lambda e, j=j, dst=dst: e.tensor_scalar(
            dst[0:16, :], psB[0:16, 0:TC], bb_sb[0:16, j:j + 1], None, ALU.add))
    k.vmark()
    k.wait("sync", k.V)
    k.dma("sync", ct_out[:, :], small2[0:16, :], ST)

    k.wait("tensor", k.V)
    for c in range(KC):
        k.mm(lambda e, c=c: e.matmul(
            psA[0:1, 0:16], ones_k[:, 0:1], asb[:, c * 16:(c + 1) * 16],
            start=(c == 0), stop=(c == KC - 1)))
    k.tmark()
    k.wait("vector", k.T)
    k.vec(lambda e: e.tensor_scalar(sa_row[0:1, :], psA[0:1, 0:16], 1.0 / D_,
                                    None, ALU.mult))
    k.vmark()
    k.wait("scalar", k.V)
    k.sca(lambda e: e.activation(out=sa_row[0:1, :], in_=sa_row[0:1, :],
                                 func=AF.Exp, scale=-1.0))
    k.smark()
    k.wait("vector", k.S)
    k.vec(lambda e: e.tensor_scalar(sa_row[0:1, :], sa_row[0:1, :], 1.0, None, ALU.add))
    k.vec(lambda e: e.reciprocal(sa_row[0:1, :], sa_row[0:1, :]))
    k.vmark()
    k.wait("tensor", k.V)
    k.mm(lambda e: e.matmul(psB[0:16, 0:1], sa_row[0:1, 0:16], ones1f[0:1, 0:1],
                            start=True, stop=True))
    k.tmark()
    k.wait("vector", k.T)
    k.vec(lambda e: e.tensor_copy(sa_sb[0:16, :], psB[0:16, 0:1]))
    k.vmark()

    k.wait("tensor", k.V)
    for n in range(4):
        k.mm(lambda e, n=n: e.matmul(
            psA[0:16, n * 512:(n + 1) * 512], ones1f[0:1, 0:16],
            dbar[:, n * 512:(n + 1) * 512], start=True, stop=True))
    k.tmark()
    k.wait("vector", k.T)
    k.vec(lambda e: e.tensor_tensor(small3[0:16, :], psA[0:16, 0:TC], small[0:16, :],
                                    ALU.mult))
    k.vec(lambda e: e.tensor_scalar(small3[0:16, :], small3[0:16, :], sa_sb[0:16, 0:1],
                                    None, ALU.mult))
    k.vmark()
    k.wait("sync", k.V)
    k.dma("sync", b_out[:, :], small3[0:16, :], ST)
    k.wait("sync", ST)
    return k.build()


def build_B():
    k = KB()
    a_in = k.inp("a_in", [4, T_])
    b_in = k.inp("b_in", [4, T_])
    hs_out = k.outp("hs_out", [4, T_])
    at = k.sbuf("at", [4, T_])
    bt = k.sbuf("bt", [4, T_])
    ht = k.sbuf("ht", [4, T_])
    LD = k.sem("LD")
    ST = k.sem("ST")
    k.dma("sync", at[:, :], a_in[:, :], LD)
    k.dma("sync", bt[:, :], b_in[:, :], LD)
    k.wait("vector", LD)
    k.vec(lambda e: e.tensor_tensor_scan(ht[:, :], at[:, :], bt[:, :], 0.0,
                                         ALU.mult, ALU.add))
    k.vmark()
    k.wait("sync", k.V)
    k.dma("sync", hs_out[:, :], ht[:, :], ST)
    k.wait("sync", ST)
    return k.build()



def ln_inplace(k, xsb, bx, xnb, scr, stat, ones_k, ones1f, psA, psB, eps_ap):
    m = stat[0:1, 0:TC]
    v = stat[0:1, TC:2 * TC]
    tmp = psA[64:65, 0:TC]
    for c in range(KC):
        k.vec(lambda e, c=c: e.tensor_copy(
            bx[:, c * TC:(c + 1) * TC], xsb[:, c * TC:(c + 1) * TC]))
    k.wait("tensor", k.V)
    for n in range(4):
        for c in range(KC):
            k.mm(lambda e, n=n, c=c: e.matmul(
                psA[0:1, n * 512:(n + 1) * 512], ones_k[:, 0:1],
                bx[:, c * TC + n * 512: c * TC + n * 512 + 512],
                start=(c == 0), stop=(c == KC - 1)))
    k.tlast()
    k.wait("scalar", k.T)
    for c in range(KC):
        k.sca(lambda e, c=c: e.activation(
            out=bx[:, c * TC:(c + 1) * TC], in_=bx[:, c * TC:(c + 1) * TC],
            func=AF.Square))
    k.wait("tensor", k.S)
    for n in range(4):
        for c in range(KC):
            k.mm(lambda e, n=n, c=c: e.matmul(
                psB[0:1, n * 512:(n + 1) * 512], ones_k[:, 0:1],
                bx[:, c * TC + n * 512: c * TC + n * 512 + 512],
                start=(c == 0), stop=(c == KC - 1)))
    k.tlast()
    k.wait("vector", k.T)
    k.vec(lambda e: e.tensor_scalar(m, psA[0:1, 0:TC], 1.0 / D_, None, ALU.mult))
    k.vec(lambda e: e.tensor_scalar(v, psB[0:1, 0:TC], 1.0 / D_, None, ALU.mult))
    k.vec(lambda e: e.tensor_tensor(tmp, m, m, ALU.mult))
    k.vec(lambda e: e.tensor_tensor(v, v, tmp, ALU.subtract))
    k.wait("scalar", k.V)
    k.sca(lambda e: e.activation(out=tmp, in_=v, func=AF.Ln, bias=eps_ap))
    k.sca(lambda e: e.activation(out=v, in_=tmp, func=AF.Exp, scale=-0.5))
    k.wait("tensor", k.S)
    k.wait("tensor", k.V)
    for n in range(4):
        k.mm(lambda e, n=n: e.matmul(
            psA[:, n * 512:(n + 1) * 512], ones1f[0:1, :],
            m[:, n * 512:(n + 1) * 512], start=True, stop=True))
        k.mm(lambda e, n=n: e.matmul(
            psB[:, n * 512:(n + 1) * 512], ones1f[0:1, :],
            v[:, n * 512:(n + 1) * 512], start=True, stop=True))
    k.tlast()
    k.wait("vector", k.T)
    for c in range(KC):
        for hh in range(2):
            k.vec(lambda e, c=c, hh=hh: e.tensor_tensor(
                scr[:, :], xsb[:, c * TC + hh * 1024: c * TC + (hh + 1) * 1024],
                psA[:, hh * 1024:(hh + 1) * 1024], ALU.subtract))
            k.vec(lambda e, c=c, hh=hh: e.tensor_tensor(
                xnb[:, c * TC + hh * 1024: c * TC + (hh + 1) * 1024], scr[:, :],
                psB[:, hh * 1024:(hh + 1) * 1024], ALU.mult))


def build_C():
    k = KB()
    xT = k.inp("xT", [D_, TC])
    xn_in = k.inp("xn_in", [D_, TC], BF16)
    hs_in = k.inp("hs_in", [16, TC])
    ct_in = k.inp("ct_in", [16, TC])
    Wo = k.inp("Wo", [D_, D_])
    bo = k.inp("bo", [D_, 1])
    mk_bd = k.inp("mk_bd", [4 * 8, 128, 128])
    up_bd = k.inp("up_bd", [4 * 8, 128, 512])
    wcp = k.inp("wcp", [128 * 128, D_])
    WqT = k.inp("WqT", [D_, D_])
    WkT = k.inp("WkT", [D_, D_])
    WvT = k.inp("WvT", [D_, D_])
    wcp16 = k.dram("wcp16", [128 * 128, D_], BF16)
    x2_out = k.outp("x2_out", [D_, TC])
    q_out = k.outp("q_out", [D_, TC], BF16)
    k_out = k.outp("k_out", [D_, TC], BF16)
    v_out = k.outp("v_out", [TC, D_], BF16)
    qn2_out = k.outp("qn2_out", [16, TC])
    kn2_out = k.outp("kn2_out", [16, TC])

    xsb = k.sbuf("xsb", [128, KC * TC])
    bx = k.sbuf("bx", [128, KC * TC], BF16)
    xnb = k.sbuf("xnb", [128, KC * TC], BF16)
    c8 = k.sbuf("c8", [128, 32 * 512], BF16)
    scr = k.sbuf("scr", [128, TC // 2])
    stat = k.sbuf("stat", [1, 2 * TC])
    wst = k.sbuf("wst", [128, 8 * 1024], BF16)
    hbuf = k.sbuf("hbuf", [128, 8 * 256], BF16)
    stage = k.sbuf("stage", [128, TC], BF16)
    stage2 = hbuf
    rows = stat[0:1, 0:TC]
    bo_sb = k.sbuf("bo_sb", [128, KC])
    ones_k = k.sbuf("ones_k", [128, 1], BF16)
    ones1f = k.sbuf("ones1f", [1, 128])
    eps5 = k.sbuf("eps5", [1, 1])
    psA = k.psum("psA", [128, 2048])
    psB = k.psum("psB", [128, 2048])

    LD = k.sem("LD")
    LW = (k.sem("LW0"), k.sem("LW1"))
    LWs = k.sem("LWs")
    LB = k.sem("LB")
    CV = k.sem("CV")
    CVL = (k.sem("CVL0"), k.sem("CVL1"))
    ST = k.sem("ST")
    STS = k.sem("STS")
    STR = k.sem("STR")

    for c in range(KC):
        k.dma("sync", xsb[:, c * TC:(c + 1) * TC], xT[c * 128:(c + 1) * 128, :], LD)
        k.dma("sync", xnb[:, c * TC:(c + 1) * TC], xn_in[c * 128:(c + 1) * 128, :], LD)
        k.dma("sync", bo_sb[:, c:c + 1], bo[c * 128:(c + 1) * 128, :], LD)
    k.dma("gpsimd", stage[0:16, :], hs_in[:, :], LB)
    k.vec(lambda e: e.memset(ones_k[:, :], 1.0))
    k.vec(lambda e: e.memset(ones1f[:, :], 1.0))
    k.vec(lambda e: e.memset(eps5[0:1, 0:1], 1e-5))
    for i in range(32):
        k.dma("gpsimd", bx[:, i * 128:(i + 1) * 128], mk_bd[i, :, :], LB)
    for eng in ("vector", "scalar", "tensor"):
        k.wait(eng, LD)
        k.wait(eng, LB)

    twar = []
    vwar = []
    for i in range(32):
        slot = (i % 2) * 1024
        if i >= 2:
            k.wait("gpsimd", k.T, twar[i - 2])
        k.dma("gpsimd", wst[:, slot:slot + 512], up_bd[i, :, :], LW[i % 2])
        k.wait("tensor", LW[i % 2])
        if i >= 2:
            k.wait("tensor", k.V, vwar[i - 2])
        ps = psA if i % 2 == 0 else psB
        k.mm(lambda e, i=i, slot=slot, ps=ps: e.matmul(
            ps[:, 0:512], bx[:, i * 128:(i + 1) * 128], wst[:, slot:slot + 512],
            start=True, stop=True))
        k.tlast()
        twar.append(k.T.n)
        k.wait("vector", k.T)
        k.vec(lambda e, i=i, ps=ps: e.tensor_copy(
            c8[:, i * 512:(i + 1) * 512], ps[:, 0:512]))
        vwar.append(k.V.n)
    k.barrier()
    k.dma("gpsimd", bx[:, 0:TC][0:16, :], ct_in[:, :], LB)

    cv_done = []
    for i in range(128):
        sl = (i % 2) * 1024
        if i >= 2:
            k.wait("gpsimd", CV, cv_done[i - 2])
        k.dma("gpsimd", hbuf[:, sl:sl + 1024], wcp[i * 128:(i + 1) * 128, :],
              CVL[i % 2])
        k.wait("sync", CVL[i % 2])
        k.dma("sync", wcp16[i * 128:(i + 1) * 128, :], hbuf[:, sl:sl + 1024], CV)
        cv_done.append(CV.n)

    k.wait("vector", LB)
    k.vec(lambda e: e.tensor_tensor(stage[0:16, :], stage[0:16, :],
                                    bx[:, 0:TC][0:16, :], ALU.mult))
    k.wait("tensor", k.V)
    for n in range(4):
        k.mm(lambda e, n=n: e.matmul(
            psB[0:1, n * 512:(n + 1) * 512], ones_k[0:16, 0:1],
            stage[0:16, n * 512:(n + 1) * 512], start=True, stop=True))
    k.tlast()
    k.wait("vector", k.T)
    k.vec(lambda e: e.tensor_copy(rows, psB[0:1, 0:TC]))
    k.wait("tensor", k.V)
    for n in range(4):
        k.mm(lambda e, n=n: e.matmul(
            psA[:, n * 512:(n + 1) * 512], ones1f[0:1, :],
            rows[:, n * 512:(n + 1) * 512], start=True, stop=True))
    k.tlast()
    k.wait("vector", k.T)
    for c in range(KC):
        k.vec(lambda e, c=c: e.tensor_tensor(
            bx[:, c * TC:(c + 1) * TC], xnb[:, c * TC:(c + 1) * TC], psA[:, 0:TC],
            ALU.mult))
    k.barrier()

    twar = []
    vwar = []
    for mi in range(KC):
        ps = psA if mi % 2 == 0 else psB
        slot = (mi % 2) * 1024
        if mi >= 2:
            k.wait("gpsimd", k.T, twar[mi - 2])
        for c in range(KC):
            k.dma("gpsimd", wst[:, slot + c * 128: slot + (c + 1) * 128],
                  Wo[c * 128:(c + 1) * 128, mi * 128:(mi + 1) * 128], LW[mi % 2])
        k.wait("tensor", LW[mi % 2])
        if mi >= 2:
            k.wait("tensor", k.V, vwar[mi - 2])
        for n in range(4):
            for c in range(KC):
                k.mm(lambda e, n=n, c=c, ps=ps, slot=slot: e.matmul(
                    ps[:, n * 512:(n + 1) * 512],
                    wst[:, slot + c * 128: slot + (c + 1) * 128],
                    bx[:, c * TC + n * 512: c * TC + n * 512 + 512],
                    start=(c == 0), stop=(c == KC - 1)))
        k.tlast()
        twar.append(k.T.n)
        k.wait("vector", k.T)
        k.vec(lambda e, mi=mi, ps=ps: e.scalar_tensor_tensor(
            xsb[:, mi * TC:(mi + 1) * TC], ps[:, 0:TC], bo_sb[:, mi:mi + 1],
            xsb[:, mi * TC:(mi + 1) * TC], ALU.add, ALU.add))
        vwar.append(k.V.n)
    k.barrier()

    ln_inplace(k, xsb, bx, xnb, scr, stat, ones_k, ones1f, psA, psB,
               eps5[0:1, 0:1])
    k.barrier()

    k.wait("sync", CV)
    k.wait("gpsimd", CV)
    k.wait("scalar", CV)
    twar_cp = []
    swar_r = []
    vwar_ev = []
    for kg in range(16):
        for i8 in range(8):
            kidx = kg * 8 + i8
            qeng = "sync" if i8 % 2 == 0 else "gpsimd"
            qsem = LWs if i8 % 2 == 0 else LW[1]
            if twar_cp:
                k.wait(qeng, k.T, twar_cp[-1])
            k.dma(qeng, wst[:, i8 * 1024:(i8 + 1) * 1024],
                  wcp16[kidx * 128:(kidx + 1) * 128, :], qsem)
        for i8 in range(8):
            kidx = kg * 8 + i8
            h, g, msub = kidx // 32, (kidx // 4) % 8, kidx % 4
            ps = psA if i8 % 2 == 0 else psB
            if len(swar_r) >= 2:
                k.wait("tensor", k.S, swar_r[-2])
            if i8 == 0 and len(vwar_ev) >= 2:
                k.wait("tensor", k.V, vwar_ev[-2])
            if i8 == 1 and len(vwar_ev) >= 1:
                k.wait("tensor", k.V, vwar_ev[-1])
            for n in range(4):
                k.mm(lambda e, h=h, g=g, msub=msub, n=n, ps=ps: e.matmul(
                    ps[:, n * 512:(n + 1) * 512],
                    c8[:, (h * 8 + g) * 512 + msub * 128:
                       (h * 8 + g) * 512 + (msub + 1) * 128],
                    xnb[:, g * TC + n * 512: g * TC + n * 512 + 512],
                    start=True, stop=True))
            k.tlast()
            k.wait("scalar", k.T)
            k.sca(lambda e, i8=i8, ps=ps: e.activation(
                out=bx[:, i8 * TC:(i8 + 1) * TC], in_=ps[:, 0:TC], func=AF.Gelu))
            swar_r.append(k.S.n)
        for mo in range(8):
            ps = psA if mo % 2 == 0 else psB
            if mo == 0:
                k.wait("tensor", k.S, swar_r[-2])
                k.wait("tensor", LWs)
                k.wait("tensor", LW[1])
            if mo == 1:
                k.wait("tensor", k.S, swar_r[-1])
            if len(vwar_ev) >= 2:
                k.wait("tensor", k.V, vwar_ev[-2])
            for i8 in range(8):
                for n in range(4):
                    k.mm(lambda e, i8=i8, mo=mo, n=n, ps=ps: e.matmul(
                        ps[:, n * 512:(n + 1) * 512],
                        wst[:, i8 * 1024 + mo * 128: i8 * 1024 + (mo + 1) * 128],
                        bx[:, i8 * TC + n * 512: i8 * TC + n * 512 + 512],
                        start=(i8 == 0), stop=(i8 == 7)))
            k.tlast()
            k.wait("vector", k.T)
            k.vec(lambda e, mo=mo, ps=ps: e.tensor_tensor(
                xsb[:, mo * TC:(mo + 1) * TC], ps[:, 0:TC],
                xsb[:, mo * TC:(mo + 1) * TC], ALU.add))
            vwar_ev.append(k.V.n)
        twar_cp.append(k.T.n)
    k.barrier()
    k.wait("sync", k.V)
    for c in range(KC):
        k.dma("sync", x2_out[c * 128:(c + 1) * 128, :], xsb[:, c * TC:(c + 1) * TC], ST)

    ln_inplace(k, xsb, bx, xnb, scr, stat, ones_k, ones1f, psA, psB,
               eps5[0:1, 0:1])
    k.barrier()

    st_stage = None
    st_rows = None
    for which, wt, o_main, o_n2, scale in (
            (0, WqT, q_out, qn2_out, 1.0 / 32.0), (1, WkT, k_out, kn2_out, 1.0)):
        k.wait("gpsimd", k.T)
        for c in range(KC):
            k.dma("gpsimd", c8[:, c * 1024:(c + 1) * 1024],
                  wt[c * 128:(c + 1) * 128, :], LB)
        k.wait("tensor", LB)
        vwar = []
        for mi in range(KC):
            ps = psA if mi % 2 == 0 else psB
            pso = psB if mi % 2 == 0 else psA
            if mi >= 1:
                k.wait("tensor", k.V, vwar[mi - 1])
            for n in range(4):
                for c in range(KC):
                    k.mm(lambda e, mi=mi, n=n, c=c, ps=ps: e.matmul(
                        ps[:, n * 512:(n + 1) * 512],
                        c8[:, c * 1024 + mi * 128: c * 1024 + (mi + 1) * 128],
                        xnb[:, c * TC + n * 512: c * TC + n * 512 + 512],
                        start=(c == 0), stop=(c == KC - 1)))
            k.tlast()
            k.wait("vector", k.T)
            if st_stage is not None:
                k.wait("vector", STS, st_stage)
            k.wait("vector", k.S)
            k.vec(lambda e, ps=ps, scale=scale: e.tensor_scalar(
                stage[:, :], ps[:, 0:TC], scale, None, ALU.mult))
            k.wait("sync", k.V)
            k.dma("sync", o_main[mi * 128:(mi + 1) * 128, :], stage[:, :], STS)
            st_stage = STS.n
            k.wait("scalar", k.V)
            k.sca(lambda e: e.activation(out=stage2[:, 0:TC], in_=stage[:, :],
                                         func=AF.Square))
            k.wait("tensor", k.S)
            for half in (0, 64):
                k.wait("tensor", k.V)
                for n in range(4):
                    k.mm(lambda e, pso=pso, half=half, n=n: e.matmul(
                        pso[0:1, n * 512:(n + 1) * 512],
                        ones_k[half:half + 64, 0:1],
                        stage2[half:half + 64, n * 512:(n + 1) * 512],
                        start=True, stop=True))
                k.tlast()
                k.wait("vector", k.T)
                if st_rows is not None:
                    k.wait("vector", STR, st_rows)
                k.vec(lambda e, pso=pso: e.tensor_copy(rows, pso[0:1, 0:TC]))
                k.wait("sync", k.V)
                k.dma("sync", o_n2[mi * 2 + half // 64: mi * 2 + half // 64 + 1, :],
                      rows, STR)
                st_rows = STR.n
            vwar.append(k.V.n)
        k.barrier()

    k.wait("gpsimd", k.T)
    for c in range(KC):
        k.dma("gpsimd", c8[:, c * 1024:(c + 1) * 1024],
              WvT[c * 128:(c + 1) * 128, :], LB)
    k.wait("tensor", LB)
    vwar = []
    for tokm in range(16):
        ps = psA if tokm % 2 == 0 else psB
        if tokm >= 2:
            k.wait("tensor", k.V, vwar[tokm - 2])
        for dh in range(2):
            for c in range(KC):
                k.mm(lambda e, tokm=tokm, dh=dh, c=c, ps=ps: e.matmul(
                    ps[:, dh * 512:(dh + 1) * 512],
                    xnb[:, c * TC + tokm * 128: c * TC + (tokm + 1) * 128],
                    c8[:, c * 1024 + dh * 512: c * 1024 + (dh + 1) * 512],
                    start=(c == 0), stop=(c == KC - 1)))
        k.tlast()
        k.wait("vector", k.T)
        if st_stage is not None:
            k.wait("vector", STS, st_stage)
        k.vec(lambda e, ps=ps: e.tensor_copy(stage[:, 0:1024], ps[:, 0:1024]))
        vwar.append(k.V.n)
        k.wait("sync", k.V)
        k.dma("sync", v_out[tokm * 128:(tokm + 1) * 128, :], stage[:, 0:1024], STS)
        st_stage = STS.n
    k.wait("sync", ST)
    k.wait("sync", STS)
    k.wait("sync", STR)
    return k.build()


def build_D():
    k = KB()
    qh = k.inp("qh", [4, 64, T_], BF16)
    kh = k.inp("kh", [4, 64, T_], BF16)
    vh = k.inp("vh", [4, T_, 64], BF16)
    qn2h = k.inp("qn2h", [4, T_])
    kn2h = k.inp("kn2h", [4, T_])
    oh = k.outp("oh", [4, 64, T_])

    ktil = k.sbuf("ktil", [65, T_], BF16)
    qtil = k.sbuf("qtil", [65, T_], BF16)
    vt = k.sbuf("vt", [128, 64 * 65], BF16)
    ptil = k.sbuf("ptil", [128, 2048], BF16)
    rowb = k.sbuf("rowb", [1, T_])
    otmp = k.sbuf("otmp", [64, 512])
    rc = k.sbuf("rc", [1, 512])
    sc = k.sbuf("sc", [1, 16])
    ones1f = k.sbuf("ones1f", [1, 128])
    psA = k.psum("psA", [128, 2048])
    psB = k.psum("psB", [128, 2048])

    LDK = k.sem("LDK")
    LDQ = k.sem("LDQ")
    LDV = k.sem("LDV")
    LDR = k.sem("LDR")
    STO = k.sem("STO")

    k.vec(lambda e: e.memset(ktil[64:65, :], 1.0))
    k.vec(lambda e: e.memset(vt[:, :], 1.0))
    k.vec(lambda e: e.memset(ones1f[:, :], 1.0))

    st_o = None
    for p in range(4):
        if p > 0:
            k.wait("gpsimd", k.T)
            k.wait("gpsimd", k.V)
        k.dma("gpsimd", ktil[0:64, :], kh[p, :, :], LDK)
        k.dma("gpsimd", qtil[0:64, :], qh[p, :, :], LDQ)
        for kt in range(64):
            k.dma("gpsimd", vt[:, kt * 65:kt * 65 + 64],
                  vh[p, kt * 128:(kt + 1) * 128, :], LDV)
        k.dma("gpsimd", rowb[0:1, :], kn2h[p:p + 1, :], LDR)
        k.wait("vector", LDR)
        k.vec(lambda e: e.tensor_reduce(sc[0:1, 0:1], rowb[0:1, :], AX, ALU.max))
        k.wait("gpsimd", k.V)
        k.dma("gpsimd", rowb[0:1, :], qn2h[p:p + 1, :], LDR)
        k.wait("vector", LDR)
        k.vec(lambda e: e.tensor_scalar(rowb[0:1, :], rowb[0:1, :], sc[0:1, 0:1],
                                        None, ALU.mult))
        k.wait("scalar", k.V)
        k.sca(lambda e: e.activation(out=rowb[0:1, :], in_=rowb[0:1, :],
                                     func=AF.Sqrt))
        k.wait("vector", k.S)
        k.wait("vector", LDQ)
        k.vec(lambda e: e.tensor_scalar(qtil[64:65, :], rowb[0:1, :], -1.0,
                                        None, ALU.mult))
        k.wait("tensor", LDK)
        k.wait("tensor", LDV)
        k.wait("tensor", k.V)
        k.wait("scalar", k.V)
        swar = []
        twar_pv = []
        vwar_q = None

        for qt in range(16):
            for kg in range(33):
                pslot = kg % 2
                if kg < 32:
                    if len(swar) >= 2:
                        k.wait("tensor", k.S, swar[-2])
                    if kg == 0 and vwar_q is not None:
                        k.wait("tensor", k.V, vwar_q)
                    for j in (0, 1):
                        kt = kg * 2 + j
                        k.mm(lambda e, kt=kt, j=j, pslot=pslot, qt=qt: e.matmul(
                            psA[:, pslot * 1024 + j * 512:
                                pslot * 1024 + (j + 1) * 512],
                            ktil[:, kt * 128:(kt + 1) * 128],
                            qtil[:, qt * 512:(qt + 1) * 512],
                            start=True, stop=True))
                    k.tlast()
                    k.wait("scalar", k.T)
                    if len(twar_pv) >= 2:
                        k.wait("scalar", k.T, twar_pv[-2])
                    k.sca(lambda e, pslot=pslot: e.activation(
                        out=ptil[:, pslot * 1024:(pslot + 1) * 1024],
                        in_=psA[:, pslot * 1024:(pslot + 1) * 1024], func=AF.Exp))
                    swar.append(k.S.n)
                if kg >= 1:
                    pv = kg - 1
                    pvslot = pv % 2
                    k.wait("tensor", k.S, swar[-1] if kg == 32 else swar[-2])
                    for j in (0, 1):
                        kt = pv * 2 + j
                        k.mm(lambda e, kt=kt, j=j, pvslot=pvslot: e.matmul(
                            psB[0:65, 0:512], vt[:, kt * 65:(kt + 1) * 65],
                            ptil[:, pvslot * 1024 + j * 512:
                                 pvslot * 1024 + (j + 1) * 512],
                            start=(kt == 0), stop=(kt == 63)))
                    k.tlast()
                    twar_pv.append(k.T.n)
            k.wait("vector", k.T)
            if st_o is not None:
                k.wait("vector", STO, st_o)
            k.vec(lambda e: e.reciprocal(rc[0:1, :], psB[64:65, 0:512]))
            k.wait("tensor", k.V)
            k.mm(lambda e: e.matmul(psA[0:64, 0:512], ones1f[0:1, 0:64],
                                    rc[0:1, :], start=True, stop=True))
            k.tlast()
            k.wait("vector", k.T)
            k.vec(lambda e: e.tensor_copy(otmp[:, :], psB[0:64, 0:512]))
            k.vec(lambda e: e.tensor_tensor(otmp[:, :], otmp[:, :],
                                            psA[0:64, 0:512], ALU.mult))
            vwar_q = k.V.n
            k.wait("sync", k.V)
            k.dma("sync", oh[p, :, qt * 512:(qt + 1) * 512], otmp[:, :], STO)
            st_o = STO.n
    
    k.wait("sync", STO)
    return k.build()


def build_E():
    twar_w1 = []
    twar_w2 = []
    vwar_w2 = []
    """Residual + double-LN (folded) + router/top-2 + dense 8-expert MoE.
    W1 feature-major; W2 token-major (gate = per-partition scalar at evac);
    expert contributions accumulated in HBM via DMA accum_op=add."""
    k = KB()
    xT2 = k.inp("xT2", [D_, TC])
    oT = k.inp("oT", [D_, TC])
    rls = k.inp("rls", [D_, 1])
    rlb = k.inp("rlb", [D_, 1])
    rwT = k.inp("rwT", [D_, 8])
    rb_in = k.inp("rb_in", [1, 8])
    els = k.inp("els", [8, D_])
    elb = k.inp("elb", [8, D_])
    w1t = k.inp("w1t", [8, D_, 4096])
    b1_in = k.inp("b1_in", [8, 4096])
    w2t = k.inp("w2t", [8, 4096, D_])
    b2_in = k.inp("b2_in", [8, D_])
    eyef = k.inp("eyef", [128, 128])
    out = k.outp("out", [TC, D_])

    xsb = k.sbuf("xsb", [128, KC * TC])
    xhb = k.sbuf("xhb", [128, KC * TC], BF16)
    xeb = k.sbuf("xeb", [128, KC * TC], BF16)
    hq = k.sbuf("hq", [128, KC * TC], BF16)
    w2res = k.sbuf("w2res", [128, KC * 1024], BF16)
    wst = k.sbuf("wst", [128, 2048], BF16)
    stage = k.sbuf("stage", [128, 1024])
    stat = k.sbuf("stat", [1, 2 * TC])
    gates = k.sbuf("gates", [128, 16 * 8])
    rwb = k.sbuf("rwb", [128, KC * 8], BF16)
    rwu = k.sbuf("rwu", [128, KC * 8], BF16)
    rlbb = k.sbuf("rlbb", [128, KC], BF16)
    lsb = k.sbuf("lsb", [128, 2 * KC])
    b1sb = k.sbuf("b1sb", [128, KC])
    b2r = k.sbuf("b2r", [1, D_], BF16)
    onesb = k.sbuf("onesb", [1, 128], BF16)
    rbr = k.sbuf("rbr", [1, 8])
    eyf = k.sbuf("eyf", [128, 128])
    scp = k.sbuf("scp", [128, 16])
    bcb = k.sbuf("bcb", [128, 8])
    ones_k = k.sbuf("ones_k", [128, 1], BF16)
    ones1f = k.sbuf("ones1f", [1, 128])
    eps5 = k.sbuf("eps5", [1, 2])
    psA = k.psum("psA", [128, 2048])
    psB = k.psum("psB", [128, 2048])

    LD = k.sem("LD")
    LO = k.sem("LO")
    LB = k.sem("LB")
    LWq = k.sem("LWq")
    LW = (k.sem("LW0"), k.sem("LW1"))
    STA = (k.sem("STA0"), k.sem("STA1"))
    STX = (k.sem("STX0"), k.sem("STX1"))

    for c in range(KC):
        k.dma("sync", xsb[:, c * TC:(c + 1) * TC], xT2[c * 128:(c + 1) * 128, :], LD)
        k.dma("gpsimd", rwu[:, c * 8:(c + 1) * 8], rwT[c * 128:(c + 1) * 128, :], LB)
        k.dma("sync", lsb[:, c:c + 1], rls[c * 128:(c + 1) * 128, :], LD)
        k.dma("sync", lsb[:, KC + c:KC + c + 1], rlb[c * 128:(c + 1) * 128, :], LD)
    k.dma("sync", eyf[:, :], eyef[:, :], LD)
    k.dma("sync", rbr[0:1, :], rb_in[:, :], LD)
    k.vec(lambda e: e.memset(onesb[:, :], 1.0))
    k.vec(lambda e: e.memset(ones_k[:, :], 1.0))
    k.vec(lambda e: e.memset(ones1f[:, :], 1.0))
    k.vec(lambda e: e.memset(eps5[0:1, 0:1], 1e-5))
    k.vec(lambda e: e.memset(eps5[0:1, 1:2], 1e-6))
    for eng in ("vector", "scalar", "tensor"):
        k.wait(eng, LD)
        k.wait(eng, LB)

    for c in range(KC):
        for hhalf in range(2):
            if c > 0 or hhalf > 0:
                k.wait("gpsimd", k.V)
            k.dma("gpsimd", stage[:, :],
                  oT[c * 128:(c + 1) * 128, hhalf * 1024:(hhalf + 1) * 1024], LO)
            k.wait("vector", LO)
            k.vec(lambda e, c=c, hhalf=hhalf: e.tensor_tensor(
                xsb[:, c * TC + hhalf * 1024: c * TC + (hhalf + 1) * 1024],
                xsb[:, c * TC + hhalf * 1024: c * TC + (hhalf + 1) * 1024],
                stage[:, :], ALU.add))
    k.barrier()

    m = stat[0:1, 0:TC]
    v = stat[0:1, TC:2 * TC]
    tmp = psA[64:65, 0:TC]
    r1row = psB[64:65, 0:TC]
    for c in range(KC):
        k.vec(lambda e, c=c: e.tensor_copy(
            xeb[:, c * TC:(c + 1) * TC], xsb[:, c * TC:(c + 1) * TC]))
    k.wait("tensor", k.V)
    for n in range(4):
        for c in range(KC):
            k.mm(lambda e, n=n, c=c: e.matmul(
                psA[0:1, n * 512:(n + 1) * 512], ones_k[:, 0:1],
                xeb[:, c * TC + n * 512: c * TC + n * 512 + 512],
                start=(c == 0), stop=(c == KC - 1)))
    k.tlast()
    k.wait("scalar", k.T)
    for c in range(KC):
        k.sca(lambda e, c=c: e.activation(
            out=xeb[:, c * TC:(c + 1) * TC], in_=xeb[:, c * TC:(c + 1) * TC],
            func=AF.Square))
    k.wait("tensor", k.S)
    for n in range(4):
        for c in range(KC):
            k.mm(lambda e, n=n, c=c: e.matmul(
                psB[0:1, n * 512:(n + 1) * 512], ones_k[:, 0:1],
                xeb[:, c * TC + n * 512: c * TC + n * 512 + 512],
                start=(c == 0), stop=(c == KC - 1)))
    k.tlast()
    k.wait("vector", k.T)
    k.vec(lambda e: e.tensor_scalar(m, psA[0:1, 0:TC], 1.0 / D_, None, ALU.mult))
    k.vec(lambda e: e.tensor_scalar(v, psB[0:1, 0:TC], 1.0 / D_, None, ALU.mult))
    k.vec(lambda e: e.tensor_tensor(tmp, m, m, ALU.mult))
    k.vec(lambda e: e.tensor_tensor(v, v, tmp, ALU.subtract))
    k.wait("scalar", k.V)
    k.sca(lambda e: e.activation(out=tmp, in_=v, func=AF.Ln, bias=eps5[0:1, 0:1]))
    k.sca(lambda e: e.activation(out=r1row, in_=tmp, func=AF.Exp, scale=-0.5))
    k.wait("vector", k.S)
    k.vec(lambda e: e.tensor_tensor(v, v, r1row, ALU.mult))
    k.vec(lambda e: e.tensor_tensor(v, v, r1row, ALU.mult))
    k.wait("scalar", k.V)
    k.sca(lambda e: e.activation(out=tmp, in_=v, func=AF.Ln, bias=eps5[0:1, 1:2]))
    k.sca(lambda e: e.activation(out=v, in_=tmp, func=AF.Exp, scale=-0.5))
    k.wait("vector", k.S)
    k.vec(lambda e: e.tensor_tensor(v, v, r1row, ALU.mult))
    k.wait("tensor", k.V)
    k.wait("tensor", k.S)
    for n in range(4):
        k.mm(lambda e, n=n: e.matmul(
            psA[:, n * 512:(n + 1) * 512], ones1f[0:1, :],
            m[:, n * 512:(n + 1) * 512], start=True, stop=True))
        k.mm(lambda e, n=n: e.matmul(
            psB[:, n * 512:(n + 1) * 512], ones1f[0:1, :],
            v[:, n * 512:(n + 1) * 512], start=True, stop=True))
    k.tlast()
    k.wait("vector", k.T)
    for c in range(KC):
        for hhalf in range(2):
            k.vec(lambda e, c=c, hhalf=hhalf: e.tensor_tensor(
                stage[:, :],
                xsb[:, c * TC + hhalf * 1024: c * TC + (hhalf + 1) * 1024],
                psA[:, hhalf * 1024:(hhalf + 1) * 1024], ALU.subtract))
            k.vec(lambda e, c=c, hhalf=hhalf: e.tensor_tensor(
                xhb[:, c * TC + hhalf * 1024: c * TC + (hhalf + 1) * 1024],
                stage[:, :], psB[:, hhalf * 1024:(hhalf + 1) * 1024], ALU.mult))
    k.barrier()

    for c in range(KC):
        k.vec(lambda e, c=c: e.tensor_scalar(
            rwb[:, c * 8:(c + 1) * 8], rwu[:, c * 8:(c + 1) * 8],
            lsb[:, c:c + 1], None, ALU.mult))
        k.vec(lambda e, c=c: e.tensor_copy(rlbb[:, c:c + 1],
                                           lsb[:, KC + c:KC + c + 1]))
    k.wait("tensor", k.V)
    for c in range(KC):
        k.mm(lambda e, c=c: e.matmul(
            psA[0:1, 0:8], rlbb[:, c:c + 1], rwu[:, c * 8:(c + 1) * 8],
            start=(c == 0), stop=(c == KC - 1)))
    k.tlast()
    k.wait("vector", k.T)
    k.vec(lambda e: e.tensor_tensor(rbr[0:1, :], psA[0:1, 0:8], rbr[0:1, :],
                                    ALU.add))
    k.wait("tensor", k.V)
    k.mm(lambda e: e.matmul(psA[:, 0:8], ones1f[0:1, :], rbr[0:1, :],
                            start=True, stop=True))
    k.tlast()
    k.wait("vector", k.T)
    k.vec(lambda e: e.tensor_copy(bcb[:, :], psA[:, 0:8]))
    for tokm in range(16):
        if tokm > 0:
            k.wait("tensor", k.V)
        for c in range(KC):
            k.mm(lambda e, tokm=tokm, c=c: e.matmul(
                psB[:, 0:8], xhb[:, c * TC + tokm * 128: c * TC + (tokm + 1) * 128],
                rwb[:, c * 8:(c + 1) * 8], start=(c == 0), stop=(c == KC - 1)))
        k.tlast()
        k.wait("vector", k.T)
        k.vec(lambda e, tokm=tokm: e.tensor_tensor(
            gates[:, tokm * 8:(tokm + 1) * 8], psB[:, 0:8], bcb[:, :], ALU.add))
    k.barrier()

    for tokm in range(16):
        ga = gates[:, tokm * 8:(tokm + 1) * 8]
        msk = scp[:, 8:16]
        k.vec(lambda e, ga=ga: e.tensor_reduce(scp[:, 0:1], ga, AX, ALU.max))
        k.vec(lambda e: e.tensor_scalar(scp[:, 1:2], scp[:, 0:1], -1.0, None,
                                        ALU.mult))
        k.wait("scalar", k.V)
        k.sca(lambda e, ga=ga: e.activation(
            out=ga, in_=ga, func=AF.Exp, bias=scp[:, 1:2]))
        k.wait("vector", k.S)
        k.vec(lambda e, ga=ga: e.tensor_reduce(scp[:, 2:3], ga, AX, ALU.max))
        k.vec(lambda e, ga=ga, msk=msk: e.tensor_scalar(
            msk, ga, scp[:, 2:3], None, ALU.is_ge))
        k.vec(lambda e, ga=ga, msk=msk: e.scalar_tensor_tensor(
            msk, msk, -1e30, ga, ALU.mult, ALU.add))
        k.vec(lambda e, msk=msk: e.tensor_reduce(scp[:, 3:4], msk, AX, ALU.max))
        k.vec(lambda e, ga=ga, msk=msk: e.tensor_scalar(
            msk, ga, scp[:, 3:4], None, ALU.is_ge))
        k.vec(lambda e: e.tensor_tensor(scp[:, 4:5], scp[:, 2:3], scp[:, 3:4],
                                        ALU.add))
        k.vec(lambda e: e.reciprocal(scp[:, 5:6], scp[:, 4:5]))
        k.vec(lambda e, ga=ga, msk=msk: e.tensor_tensor(ga, ga, msk, ALU.mult))
        k.vec(lambda e, ga=ga: e.tensor_scalar(ga, ga, scp[:, 5:6], None,
                                               ALU.mult))
    k.barrier()

    st_sl = [None, None]
    for tokm in range(16):
        ssl = tokm % 2
        if st_sl[ssl] is not None:
            k.wait("vector", STX[ssl], st_sl[ssl])
        for c in range(KC):
            psl = (c % 2) * 512
            if c >= 2 or tokm > 0:
                k.wait("tensor", k.V)
            k.mm(lambda e, tokm=tokm, c=c, psl=psl: e.transpose(
                psB[:, psl:psl + 128],
                xsb[:, c * TC + tokm * 128: c * TC + (tokm + 1) * 128], eyf[:, :]))
            k.tlast()
            k.wait("vector", k.T)
            k.vec(lambda e, c=c, ssl=ssl, psl=psl: e.tensor_copy(
                stage[:, ssl * 512:ssl * 512 + 512][:, 0:128] if False else
                stage[:, ssl * 512 + 0:ssl * 512 + 512][:, c % 4 * 128:
                                                        (c % 4 + 1) * 128],
                psB[:, psl:psl + 128]))
            if c == 3 or c == 7:
                k.wait("sync", k.V)
                k.dma("sync",
                      out[tokm * 128:(tokm + 1) * 128,
                          (c - 3) * 128:(c + 1) * 128],
                      stage[:, ssl * 512:(ssl + 1) * 512], STX[ssl])
                st_sl[ssl] = STX[ssl].n
                if c == 3:
                    k.wait("vector", STX[ssl], st_sl[ssl])
    k.barrier()

    stx_final = (STX[0].n, STX[1].n)
    st_sl = [None, None]
    first_use = [True, True]
    swar_g = []
    for ei in range(8):
        k.wait("gpsimd", k.V)
        k.wait("gpsimd", k.T)
        k.dma("gpsimd", b2r[0:1, :], b2_in[ei:ei + 1, :], LB)
        for c in range(KC):
            k.dma("gpsimd", lsb[:, c:c + 1],
                  els[ei:ei + 1, c * 128:(c + 1) * 128].rearrange("a p -> p a"), LB)
            k.dma("gpsimd", lsb[:, KC + c:KC + c + 1],
                  elb[ei:ei + 1, c * 128:(c + 1) * 128].rearrange("a p -> p a"), LB)
        k.wait("vector", LB)
        k.wait("vector", k.T)
        for c in range(KC):
            k.vec(lambda e, c=c: e.tensor_scalar(
                xeb[:, c * TC:(c + 1) * TC], xhb[:, c * TC:(c + 1) * TC],
                lsb[:, c:c + 1], lsb[:, KC + c:KC + c + 1], ALU.mult, ALU.add))
        k.wait("tensor", k.V)
        for q in range(4):
            k.wait("gpsimd", k.V)
            k.wait("gpsimd", k.S)
            for mg in range(KC):
                k.dma("gpsimd", b1sb[:, mg:mg + 1],
                      b1_in[ei:ei + 1, q * 1024 + mg * 128: q * 1024 + (mg + 1) * 128
                            ].rearrange("a p -> p a"), LB)
            for mg in range(KC):
                wsl = (mg % 2) * 1024
                if len(twar_w1) >= 2:
                    k.wait("gpsimd", k.T, twar_w1[-2])
                k.dma("gpsimd",
                      wst[:, wsl:wsl + 1024].rearrange("p (c m) -> p c m", m=128),
                      w1t[ei, :, q * 1024 + mg * 128: q * 1024 + (mg + 1) * 128
                          ].rearrange("(c p) m -> p c m", p=128), LW[mg % 2])
                k.wait("tensor", LW[mg % 2])
                k.wait("tensor", LB)
                for n in range(4):
                    psl = ((mg * 4 + n) % 4) * 512
                    if len(swar_g) >= 4:
                        k.wait("tensor", k.S, swar_g[-4])
                    for c in range(KC):
                        k.mm(lambda e, mg=mg, n=n, c=c, wsl=wsl, psl=psl: e.matmul(
                            psB[:, psl + 0:psl + 512],
                            wst[:, wsl + c * 128: wsl + (c + 1) * 128],
                            xeb[:, c * TC + n * 512: c * TC + n * 512 + 512],
                            start=(c == 0), stop=(c == KC - 1)))
                    k.tlast()
                    k.wait("scalar", k.T)
                    k.sca(lambda e, mg=mg, n=n, psl=psl: e.activation(
                        out=hq[:, mg * TC + n * 512: mg * TC + (n + 1) * 512],
                        in_=psB[:, psl:psl + 512], func=AF.Gelu,
                        bias=b1sb[:, mg:mg + 1]))
                    swar_g.append(k.S.n)
                twar_w1.append(k.T.n)
            k.wait("gpsimd", k.T)
            for c in range(KC):
                k.dma("gpsimd", w2res[:, c * 1024:(c + 1) * 1024],
                      w2t[ei, q * 1024 + c * 128: q * 1024 + (c + 1) * 128, :], LWq)
            k.wait("tensor", LWq)
            k.wait("tensor", k.S)
            for tokm in range(16):
                for dh in range(2):
                    psl = ((tokm * 2 + dh) % 4) * 512
                    ssl = (tokm * 2 + dh) % 2
                    if len(twar_w2) >= 4:
                        k.wait("tensor", k.V, vwar_w2[-4])
                    if q == 0:
                        k.mm(lambda e, dh=dh, psl=psl: e.matmul(
                            psA[:, psl:psl + 512], onesb[0:1, :],
                            b2r[0:1, dh * 512:(dh + 1) * 512],
                            start=True, stop=False))
                    for c in range(KC):
                        k.mm(lambda e, tokm=tokm, dh=dh, c=c, psl=psl, q=q: e.matmul(
                            psA[:, psl:psl + 512],
                            hq[:, c * TC + tokm * 128: c * TC + (tokm + 1) * 128],
                            w2res[:, c * 1024 + dh * 512: c * 1024 + (dh + 1) * 512],
                            start=(c == 0 and q > 0), stop=(c == KC - 1)))
                    k.tlast()
                    twar_w2.append(k.T.n)
                    k.wait("vector", k.T)
                    if first_use[ssl]:
                        k.wait("vector", STX[ssl], stx_final[ssl])
                        first_use[ssl] = False
                    if st_sl[ssl] is not None:
                        k.wait("vector", STA[ssl], st_sl[ssl])
                    k.vec(lambda e, ei=ei, tokm=tokm, ssl=ssl, psl=psl: e.tensor_scalar(
                        stage[:, ssl * 512:(ssl + 1) * 512], psA[:, psl:psl + 512],
                        gates[:, tokm * 8 + ei: tokm * 8 + ei + 1], None, ALU.mult))
                    vwar_w2.append(k.V.n)
                    k.wait("gpsimd", k.V)
                    k.dma("gpsimd",
                          out[tokm * 128:(tokm + 1) * 128, dh * 512:(dh + 1) * 512],
                          stage[:, ssl * 512:(ssl + 1) * 512], STA[ssl],
                          accum_op=ALU.add)
                    st_sl[ssl] = STA[ssl].n
    for ssl in range(2):
        if st_sl[ssl] is not None:
            k.wait("sync", STA[ssl], st_sl[ssl])
    return k.build()


def _shard_tok(xT_full, c):
    b = c // 4
    t0 = (c % 4) * TC
    return np.ascontiguousarray(xT_full[b, :, t0:t0 + TC])


def kernel(**inputs):
    f32 = np.float32
    xT = np.ascontiguousarray(inputs["L0_attn_x"].transpose(0, 2, 1))

    ncA = build_A()
    common_A = {
        "Wd": np.ascontiguousarray(inputs["L0_attn_Wd"], dtype=f32),
        "bd": np.ascontiguousarray(inputs["L0_attn_bd"].reshape(D_, 1), dtype=f32),
        "Wb": np.ascontiguousarray(inputs["L0_attn_Wb"], dtype=f32),
        "bb": np.ascontiguousarray(inputs["L0_attn_bb"].reshape(16, 1), dtype=f32),
        "Wc": np.ascontiguousarray(inputs["L0_attn_Wc"], dtype=f32),
        "bc": np.ascontiguousarray(inputs["L0_attn_bc"].reshape(16, 1), dtype=f32),
        "A": np.ascontiguousarray(inputs["L0_attn_A"], dtype=f32),
    }
    in_maps = [dict(common_A, xT=_shard_tok(xT, c)) for c in range(NCORES)]
    resA = run_bass_kernel_spmd(ncA, in_maps, core_ids=list(range(NCORES))).results

    a_full = np.zeros((B_, T_), f32)
    b_full = np.zeros((B_, 16, T_), f32)
    ct_full = np.zeros((B_, 16, T_), f32)
    xn_full = np.zeros((B_, D_, T_), BF)
    for c in range(NCORES):
        b = c // 4
        t0 = (c % 4) * TC
        a_full[b, t0:t0 + TC] = resA[c]["a_out"][0]
        b_full[b, :, t0:t0 + TC] = resA[c]["b_out"]
        ct_full[b, :, t0:t0 + TC] = resA[c]["ct_out"]
        xn_full[b, :, t0:t0 + TC] = resA[c]["xn_out"]

    ncB = build_B()
    a_rep = np.repeat(a_full[:, None, :], 16, axis=1).reshape(32, T_)
    b_rows = b_full.reshape(32, T_)
    in_maps = [{"a_in": np.ascontiguousarray(a_rep[4 * c:4 * c + 4]),
                "b_in": np.ascontiguousarray(b_rows[4 * c:4 * c + 4])}
               for c in range(NCORES)]
    resB = run_bass_kernel_spmd(ncB, in_maps, core_ids=list(range(NCORES))).results
    hs_full = np.concatenate([r["hs_out"] for r in resB], axis=0).reshape(B_, 16, T_)

    ncC = build_C()
    maker = np.asarray(inputs["L0_ffn_rail_maker"], f32)
    up = np.asarray(inputs["L0_ffn_rail_up_proj"], f32)
    mk_bd = np.zeros((32, 128, 128), f32)
    up_bd = np.zeros((32, 128, 512), f32)
    for h in range(4):
        for r in range(64):
            g, i = r // 8, r % 8
            blk = h * 8 + g
            mk_bd[blk, i * 16:(i + 1) * 16, i * 16:(i + 1) * 16] = maker[h, r].T
            up_bd[blk, i * 16:(i + 1) * 16, i * 64:(i + 1) * 64] = up[h, r].T
    common_C = {
        "Wo": np.ascontiguousarray(inputs["L0_attn_Wo"], dtype=f32),
        "bo": np.ascontiguousarray(inputs["L0_attn_bo"].reshape(D_, 1), dtype=f32),
        "mk_bd": mk_bd,
        "up_bd": up_bd,
        "wcp": np.ascontiguousarray(np.asarray(inputs["L0_ffn_w_cproj"], f32).T),
        "WqT": np.ascontiguousarray(np.asarray(inputs["L1_attn_Wq"], f32).T),
        "WkT": np.ascontiguousarray(np.asarray(inputs["L1_attn_Wk"], f32).T),
        "WvT": np.ascontiguousarray(np.asarray(inputs["L1_attn_Wv"], f32).T),
    }
    in_maps = []
    for c in range(NCORES):
        b = c // 4
        t0 = (c % 4) * TC
        in_maps.append(dict(
            common_C,
            xT=_shard_tok(xT, c),
            xn_in=np.ascontiguousarray(xn_full[b, :, t0:t0 + TC]),
            hs_in=np.ascontiguousarray(hs_full[b, :, t0:t0 + TC]),
            ct_in=np.ascontiguousarray(ct_full[b, :, t0:t0 + TC]),
        ))
    resC = run_bass_kernel_spmd(ncC, in_maps, core_ids=list(range(NCORES))).results

    x2_full = np.zeros((B_, D_, T_), f32)
    q_full = np.zeros((B_, D_, T_), BF)
    k_full = np.zeros((B_, D_, T_), BF)
    v_full = np.zeros((B_, T_, D_), BF)
    qn2_full = np.zeros((B_, 16, T_), f32)
    kn2_full = np.zeros((B_, 16, T_), f32)
    for c in range(NCORES):
        b = c // 4
        t0 = (c % 4) * TC
        x2_full[b, :, t0:t0 + TC] = resC[c]["x2_out"]
        q_full[b, :, t0:t0 + TC] = resC[c]["q_out"]
        k_full[b, :, t0:t0 + TC] = resC[c]["k_out"]
        v_full[b, t0:t0 + TC, :] = resC[c]["v_out"]
        qn2_full[b, :, t0:t0 + TC] = resC[c]["qn2_out"]
        kn2_full[b, :, t0:t0 + TC] = resC[c]["kn2_out"]

    ncD = build_D()
    in_maps = []
    for c in range(NCORES):
        b = c // 4
        hh = [4 * (c % 4) + j for j in range(4)]
        qh = np.stack([q_full[b, h * 64:(h + 1) * 64, :] for h in hh])
        kh = np.stack([k_full[b, h * 64:(h + 1) * 64, :] for h in hh])
        vh = np.stack([v_full[b, :, h * 64:(h + 1) * 64] for h in hh])
        qn2h = np.stack([qn2_full[b, h, :] for h in hh])
        kn2h = np.stack([kn2_full[b, h, :] for h in hh])
        in_maps.append({"qh": np.ascontiguousarray(qh),
                        "kh": np.ascontiguousarray(kh),
                        "vh": np.ascontiguousarray(vh),
                        "qn2h": np.ascontiguousarray(qn2h),
                        "kn2h": np.ascontiguousarray(kn2h)})
    resD = run_bass_kernel_spmd(ncD, in_maps, core_ids=list(range(NCORES))).results
    o_full = np.zeros((B_, D_, T_), f32)
    for c in range(NCORES):
        b = c // 4
        for j in range(4):
            h = 4 * (c % 4) + j
            o_full[b, h * 64:(h + 1) * 64, :] = resD[c]["oh"][j]

    ncE = build_E()
    common_E = {
        "rls": np.ascontiguousarray(
            inputs["L1_ffn_router_ln_scale"].reshape(D_, 1), dtype=f32),
        "rlb": np.ascontiguousarray(
            inputs["L1_ffn_router_ln_bias"].reshape(D_, 1), dtype=f32),
        "rwT": np.ascontiguousarray(np.asarray(inputs["L1_ffn_router_w"], f32).T),
        "rb_in": np.ascontiguousarray(
            inputs["L1_ffn_router_b"].reshape(1, 8), dtype=f32),
        "els": np.ascontiguousarray(inputs["L1_ffn_expert_ln_scale"], dtype=f32),
        "elb": np.ascontiguousarray(inputs["L1_ffn_expert_ln_bias"], dtype=f32),
        "w1t": np.ascontiguousarray(
            np.asarray(inputs["L1_ffn_w1"], f32).transpose(0, 2, 1)),
        "b1_in": np.ascontiguousarray(inputs["L1_ffn_b1"], dtype=f32),
        "w2t": np.ascontiguousarray(
            np.asarray(inputs["L1_ffn_w2"], f32).transpose(0, 2, 1)),
        "b2_in": np.ascontiguousarray(inputs["L1_ffn_b2"], dtype=f32),
        "eyef": np.eye(128, dtype=f32),
    }
    in_maps = []
    for c in range(NCORES):
        b = c // 4
        t0 = (c % 4) * TC
        in_maps.append(dict(
            common_E,
            xT2=np.ascontiguousarray(x2_full[b, :, t0:t0 + TC]),
            oT=np.ascontiguousarray(o_full[b, :, t0:t0 + TC]),
        ))
    resE = run_bass_kernel_spmd(ncE, in_maps, core_ids=list(range(NCORES))).results
    out_full = np.zeros((B_, T_, D_), f32)
    for c in range(NCORES):
        b = c // 4
        t0 = (c % 4) * TC
        out_full[b, t0:t0 + TC, :] = resE[c]["out"]
    return out_full


# version 08
# speedup vs baseline: 1.4215x; 1.0708x over previous
import contextlib
import numpy as np
import ml_dtypes
import concourse.bass as bass
import concourse.mybir as mybir
from concourse.bass_utils import run_bass_kernel_spmd

F32 = mybir.dt.float32
BF16 = mybir.dt.bfloat16
ALU = mybir.AluOpType
AF = mybir.ActivationFunctionType
AX = mybir.AxisListType.X
BF = ml_dtypes.bfloat16

NCORES = 8
B_, T_, D_ = 2, 8192, 1024
TC = 2048
KC = 8
ENGINES = ("sync", "vector", "scalar", "tensor", "gpsimd")


class Sem:
    __slots__ = ("name", "handle", "n")

    def __init__(self, name, handle):
        self.name, self.handle, self.n = name, handle, 0


class KB:

    def __init__(self):
        self.nc = bass.Bass("TRN2", target_bir_lowering=False)
        self.es = contextlib.ExitStack()
        self.plan = {e: [] for e in ENGINES}
        self.V = self.sem("Vchain")
        self.S = self.sem("Schain")
        self.T = self.sem("Tchain")
        self.BAR = self.sem("BARs")
        self.bar_k = 0

    def inp(self, name, shape, dtype=F32):
        return self.nc.dram_tensor(name, list(shape), dtype, kind="ExternalInput").ap()

    def outp(self, name, shape, dtype=F32):
        return self.nc.dram_tensor(name, list(shape), dtype, kind="ExternalOutput").ap()

    def dram(self, name, shape, dtype=F32):
        return self.nc.dram_tensor(name, list(shape), dtype, kind="Internal").ap()

    def sbuf(self, name, shape, dtype=F32):
        return self.es.enter_context(self.nc.sbuf_tensor(name, list(shape), dtype))

    def psum(self, name, shape, dtype=F32):
        return self.es.enter_context(self.nc.psum_tensor(name, list(shape), dtype))

    def sem(self, name):
        return Sem(name, self.es.enter_context(self.nc.semaphore(name)))

    def op(self, eng, fn, inc=None, amt=1):
        if inc is not None:
            inc.n += amt
        self.plan[eng].append(("op", fn, inc, amt))

    def wait(self, eng, sem, val=None):
        v = sem.n if val is None else val
        self.plan[eng].append(("wait", sem, v, 0))

    def dma(self, eng, out, in_, sem, **kw):
        self.op(eng, lambda e: e.dma_start(out=out, in_=in_, **kw), inc=sem, amt=16)

    def vec(self, fn):
        self.wait("vector", self.V)
        self.op("vector", fn, inc=self.V)

    def sca(self, fn):
        self.wait("scalar", self.S)
        self.op("scalar", fn, inc=self.S)

    def mm(self, fn):
        self.op("tensor", fn)

    def tlast(self):
        kind, fn, inc, amt = self.plan["tensor"][-1]
        assert kind == "op" and inc is None
        self.T.n += 1
        self.plan["tensor"][-1] = ("op", fn, self.T, 1)

    def tmark(self):
        self.tlast()

    def vmark(self):
        pass

    def smark(self):
        pass

    def barrier(self):
        self.bar_k += 1
        self.wait("vector", self.V)
        self.wait("scalar", self.S)
        for eng in ENGINES:
            self.op(eng, lambda e: e.nop(), inc=self.BAR)
        tgt = 5 * self.bar_k
        for eng in ENGINES:
            self.plan[eng].append(("wait", self.BAR, tgt, 0))

    def build(self):
        with self.nc.Block() as block:
            for eng in ENGINES:
                items = self.plan[eng]
                if not items:
                    continue

                def emit(e, items=items):
                    for it in items:
                        if it[0] == "wait":
                            e.wait_ge(it[1].handle, it[2])
                        else:
                            ins = it[1](e)
                            if it[2] is not None:
                                ins.then_inc(it[2].handle, it[3])
                getattr(block, eng)(emit)
        self.es.close()
        return self.nc


def ln_normalize(k, x_f32, raw_b16, sq_b16, xn_b16, scr32, stat, ones_k, ones1f,
                 psA, psB, eps_ap, cast_raw=True):
    m = stat[0:1, 0:TC]
    v = stat[0:1, TC:2 * TC]
    tmp = stat[0:1, 2 * TC:3 * TC]
    if cast_raw:
        for c in range(KC):
            k.vec(lambda e, c=c: e.tensor_copy(
                raw_b16[:, c * TC:(c + 1) * TC], x_f32[:, c * TC:(c + 1) * TC]))
        k.vmark()
    k.wait("scalar", k.V)
    for c in range(KC):
        k.sca(lambda e, c=c: e.activation(
            out=sq_b16[:, c * TC:(c + 1) * TC], in_=raw_b16[:, c * TC:(c + 1) * TC],
            func=AF.Square))
    k.smark()
    k.wait("tensor", k.V)
    k.wait("tensor", k.S)
    for n in range(4):
        for c in range(KC):
            k.mm(lambda e, n=n, c=c: e.matmul(
                psA[0:1, n * 512:(n + 1) * 512], ones_k[:, 0:1],
                raw_b16[:, c * TC + n * 512: c * TC + n * 512 + 512],
                start=(c == 0), stop=(c == KC - 1)))
        for c in range(KC):
            k.mm(lambda e, n=n, c=c: e.matmul(
                psB[0:1, n * 512:(n + 1) * 512], ones_k[:, 0:1],
                sq_b16[:, c * TC + n * 512: c * TC + n * 512 + 512],
                start=(c == 0), stop=(c == KC - 1)))
    k.tmark()
    k.wait("vector", k.T)
    k.vec(lambda e: e.tensor_scalar(m, psA[0:1, 0:TC], 1.0 / D_, None, ALU.mult))
    k.vec(lambda e: e.tensor_scalar(v, psB[0:1, 0:TC], 1.0 / D_, None, ALU.mult))
    k.vec(lambda e: e.tensor_tensor(tmp, m, m, ALU.mult))
    k.vec(lambda e: e.tensor_tensor(v, v, tmp, ALU.subtract))
    k.vmark()
    k.wait("scalar", k.V)
    k.sca(lambda e: e.activation(out=tmp, in_=v, func=AF.Ln, bias=eps_ap))
    k.sca(lambda e: e.activation(out=v, in_=tmp, func=AF.Exp, scale=-0.5))
    k.smark()
    k.wait("tensor", k.S)
    for n in range(4):
        k.mm(lambda e, n=n: e.matmul(
            psA[:, n * 512:(n + 1) * 512], ones1f[0:1, :], m[:, n * 512:(n + 1) * 512],
            start=True, stop=True))
        k.mm(lambda e, n=n: e.matmul(
            psB[:, n * 512:(n + 1) * 512], ones1f[0:1, :], v[:, n * 512:(n + 1) * 512],
            start=True, stop=True))
    k.tmark()
    k.wait("vector", k.T)
    for c in range(KC):
        k.vec(lambda e, c=c: e.tensor_tensor(
            scr32[:, :], x_f32[:, c * TC:(c + 1) * TC], psA[:, 0:TC], ALU.subtract))
        k.vec(lambda e, c=c: e.tensor_tensor(
            xn_b16[:, c * TC:(c + 1) * TC], scr32[:, :], psB[:, 0:TC], ALU.mult))
    k.vmark()


def build_A():
    k = KB()
    xT = k.inp("xT", [D_, TC])
    Wd = k.inp("Wd", [D_, D_])
    bd = k.inp("bd", [D_, 1])
    Wb = k.inp("Wb", [D_, 16])
    bb = k.inp("bb", [16, 1])
    Wc = k.inp("Wc", [D_, 16])
    bc = k.inp("bc", [16, 1])
    A_ = k.inp("A", [D_, 16])
    xn_out = k.outp("xn_out", [D_, TC], BF16)
    a_out = k.outp("a_out", [1, TC])
    b_out = k.outp("b_out", [16, TC])
    ct_out = k.outp("ct_out", [16, TC])

    raw = k.sbuf("raw", [128, KC * TC], BF16)
    sq = k.sbuf("sq", [128, KC * TC], BF16)
    xn = k.sbuf("xn", [128, KC * TC], BF16)
    scr = k.sbuf("scr", [128, TC])
    scr2 = k.sbuf("scr2", [128, TC])
    stat = k.sbuf("stat", [1, 3 * TC])
    wd_sb = k.sbuf("wd_sb", [128, KC * D_], BF16)
    wbc_sb = k.sbuf("wbc_sb", [128, KC * 32], BF16)
    asb = k.sbuf("asb", [128, KC * 16], BF16)
    bias_sb = k.sbuf("bias_sb", [128, KC])
    bb_sb = k.sbuf("bb_sb", [16, 2])
    ones_k = k.sbuf("ones_k", [128, 1], BF16)
    ones16 = k.sbuf("ones16", [16, 1], BF16)
    ones1f = k.sbuf("ones1f", [1, 128])
    small = k.sbuf("small", [16, TC])
    small2 = k.sbuf("small2", [16, TC])
    small3 = k.sbuf("small3", [16, TC])
    srow = k.sbuf("srow", [1, 2 * TC])
    sa_sb = k.sbuf("sa_sb", [16, 1])
    sa_row = k.sbuf("sa_row", [1, 16])
    eps5 = k.sbuf("eps5", [1, 1])

    psA = k.psum("psA", [128, 2048])
    psB = k.psum("psB", [128, 2048])

    LD = k.sem("LD")
    LDG = k.sem("LDG")
    ST = k.sem("ST")

    for c in range(KC):
        k.dma("gpsimd", raw[:, c * TC:(c + 1) * TC], xT[c * 128:(c + 1) * 128, :], LDG)
        k.dma("gpsimd", wd_sb[:, c * D_:(c + 1) * D_], Wd[c * 128:(c + 1) * 128, :], LDG)
        k.dma("gpsimd", wbc_sb[:, c * 32:c * 32 + 16], Wb[c * 128:(c + 1) * 128, :], LDG)
        k.dma("gpsimd", wbc_sb[:, c * 32 + 16:c * 32 + 32], Wc[c * 128:(c + 1) * 128, :], LDG)
        k.dma("gpsimd", asb[:, c * 16:(c + 1) * 16], A_[c * 128:(c + 1) * 128, :], LDG)
        k.dma("sync", bias_sb[:, c:c + 1], bd[c * 128:(c + 1) * 128, :], LD)
    k.dma("sync", bb_sb[0:16, 0:1], bb[:, :], LD)
    k.dma("sync", bb_sb[0:16, 1:2], bc[:, :], LD)
    k.vec(lambda e: e.memset(ones_k[:, :], 1.0))
    k.vec(lambda e: e.memset(ones16[:, :], 1.0))
    k.vec(lambda e: e.memset(ones1f[:, :], 1.0))
    k.vec(lambda e: e.memset(eps5[0:1, 0:1], 1e-5))
    for eng in ("vector", "scalar", "tensor"):
        k.wait(eng, LD)
        k.wait(eng, LDG)
    k.vmark()

    m = stat[0:1, 0:TC]
    v = stat[0:1, TC:2 * TC]
    tmp = stat[0:1, 2 * TC:3 * TC]
    k.wait("scalar", k.V)
    for c in range(KC):
        k.sca(lambda e, c=c: e.activation(
            out=sq[:, c * TC:(c + 1) * TC], in_=raw[:, c * TC:(c + 1) * TC],
            func=AF.Square))
    k.smark()
    k.wait("tensor", k.S)
    for n in range(4):
        for c in range(KC):
            k.mm(lambda e, n=n, c=c: e.matmul(
                psA[0:1, n * 512:(n + 1) * 512], ones_k[:, 0:1],
                raw[:, c * TC + n * 512: c * TC + n * 512 + 512],
                start=(c == 0), stop=(c == KC - 1)))
        for c in range(KC):
            k.mm(lambda e, n=n, c=c: e.matmul(
                psB[0:1, n * 512:(n + 1) * 512], ones_k[:, 0:1],
                sq[:, c * TC + n * 512: c * TC + n * 512 + 512],
                start=(c == 0), stop=(c == KC - 1)))
    k.tmark()
    k.wait("vector", k.T)
    k.vec(lambda e: e.tensor_scalar(m, psA[0:1, 0:TC], 1.0 / D_, None, ALU.mult))
    k.vec(lambda e: e.tensor_scalar(v, psB[0:1, 0:TC], 1.0 / D_, None, ALU.mult))
    k.vec(lambda e: e.tensor_tensor(tmp, m, m, ALU.mult))
    k.vec(lambda e: e.tensor_tensor(v, v, tmp, ALU.subtract))
    k.vmark()
    k.wait("scalar", k.V)
    k.sca(lambda e: e.activation(out=tmp, in_=v, func=AF.Ln, bias=eps5[0:1, 0:1]))
    k.sca(lambda e: e.activation(out=v, in_=tmp, func=AF.Exp, scale=-0.5))
    k.smark()
    k.wait("tensor", k.S)
    for n in range(4):
        k.mm(lambda e, n=n: e.matmul(
            psA[:, n * 512:(n + 1) * 512], ones1f[0:1, :],
            m[:, n * 512:(n + 1) * 512], start=True, stop=True))
        k.mm(lambda e, n=n: e.matmul(
            psB[:, n * 512:(n + 1) * 512], ones1f[0:1, :],
            v[:, n * 512:(n + 1) * 512], start=True, stop=True))
    k.tmark()
    k.wait("vector", k.T)
    for c in range(KC):
        k.vec(lambda e, c=c: e.tensor_tensor(
            scr[:, :], raw[:, c * TC:(c + 1) * TC], psA[:, 0:TC], ALU.subtract))
        k.vec(lambda e, c=c: e.tensor_tensor(
            xn[:, c * TC:(c + 1) * TC], scr[:, :], psB[:, 0:TC], ALU.mult))
    k.vmark()
    k.wait("sync", k.V)
    for c in range(KC):
        k.dma("sync", xn_out[c * 128:(c + 1) * 128, :], xn[:, c * TC:(c + 1) * TC], ST)
    k.barrier()

    for mi in range(KC):
        ps = psA if mi % 2 == 0 else psB
        if mi >= 2:
            k.wait("tensor", k.S)
        for n in range(4):
            for c in range(KC):
                k.mm(lambda e, mi=mi, n=n, c=c, ps=ps: e.matmul(
                    ps[:, n * 512:(n + 1) * 512],
                    wd_sb[:, c * D_ + mi * 128: c * D_ + (mi + 1) * 128],
                    xn[:, c * TC + n * 512: c * TC + n * 512 + 512],
                    start=(c == 0), stop=(c == KC - 1)))
        k.tmark()
        k.wait("scalar", k.T)
        k.wait("scalar", k.V)
        k.sca(lambda e, mi=mi, ps=ps: e.activation(
            out=scr[:, :], in_=ps[:, 0:TC], func=AF.Relu, bias=bias_sb[:, mi:mi + 1]))
        k.sca(lambda e, mi=mi, ps=ps: e.activation(
            out=scr2[:, :], in_=ps[:, 0:TC], func=AF.Abs, bias=bias_sb[:, mi:mi + 1]))
        k.sca(lambda e: e.activation(out=scr2[:, :], in_=scr2[:, :], func=AF.Exp,
                                     scale=-1.0))
        k.sca(lambda e: e.activation(out=scr2[:, :], in_=scr2[:, :], func=AF.Ln,
                                     bias=1.0))
        k.smark()
        k.wait("vector", k.S)
        k.vec(lambda e, mi=mi: e.tensor_tensor(
            sq[:, mi * TC:(mi + 1) * TC], scr[:, :], scr2[:, :], ALU.add))
    k.vmark()
    k.barrier()

    dbar = srow[0:1, 0:TC]
    arow = srow[0:1, TC:2 * TC]
    k.wait("tensor", k.V)
    for n in range(4):
        for c in range(KC):
            k.mm(lambda e, n=n, c=c: e.matmul(
                psA[0:1, n * 512:(n + 1) * 512], ones_k[:, 0:1],
                sq[:, c * TC + n * 512: c * TC + n * 512 + 512],
                start=(c == 0), stop=(c == KC - 1)))
    k.tmark()
    k.wait("vector", k.T)
    k.vec(lambda e: e.tensor_scalar(dbar, psA[0:1, 0:TC], 1.0 / D_, None, ALU.mult))
    k.vec(lambda e: e.tensor_scalar(arow, dbar, -1.0, 1.0, ALU.mult, ALU.add))
    k.vmark()
    k.wait("sync", k.V)
    k.dma("sync", a_out[:, :], arow, ST)

    for j, dst in ((0, small), (1, small2)):
        k.wait("tensor", k.V)
        for n in range(4):
            for c in range(KC):
                k.mm(lambda e, j=j, n=n, c=c: e.matmul(
                    psB[0:16, n * 512:(n + 1) * 512],
                    wbc_sb[:, c * 32 + 16 * j: c * 32 + 16 * j + 16],
                    xn[:, c * TC + n * 512: c * TC + n * 512 + 512],
                    start=(c == 0), stop=(c == KC - 1)))
        k.tmark()
        k.wait("vector", k.T)
        k.vec(lambda e, j=j, dst=dst: e.tensor_scalar(
            dst[0:16, :], psB[0:16, 0:TC], bb_sb[0:16, j:j + 1], None, ALU.add))
    k.vmark()
    k.wait("sync", k.V)
    k.dma("sync", ct_out[:, :], small2[0:16, :], ST)

    k.wait("tensor", k.V)
    for c in range(KC):
        k.mm(lambda e, c=c: e.matmul(
            psA[0:1, 0:16], ones_k[:, 0:1], asb[:, c * 16:(c + 1) * 16],
            start=(c == 0), stop=(c == KC - 1)))
    k.tmark()
    k.wait("vector", k.T)
    k.vec(lambda e: e.tensor_scalar(sa_row[0:1, :], psA[0:1, 0:16], 1.0 / D_,
                                    None, ALU.mult))
    k.vmark()
    k.wait("scalar", k.V)
    k.sca(lambda e: e.activation(out=sa_row[0:1, :], in_=sa_row[0:1, :],
                                 func=AF.Exp, scale=-1.0))
    k.smark()
    k.wait("vector", k.S)
    k.vec(lambda e: e.tensor_scalar(sa_row[0:1, :], sa_row[0:1, :], 1.0, None, ALU.add))
    k.vec(lambda e: e.reciprocal(sa_row[0:1, :], sa_row[0:1, :]))
    k.vmark()
    k.wait("tensor", k.V)
    k.mm(lambda e: e.matmul(psB[0:16, 0:1], sa_row[0:1, 0:16], ones1f[0:1, 0:1],
                            start=True, stop=True))
    k.tmark()
    k.wait("vector", k.T)
    k.vec(lambda e: e.tensor_copy(sa_sb[0:16, :], psB[0:16, 0:1]))
    k.vmark()

    k.wait("tensor", k.V)
    for n in range(4):
        k.mm(lambda e, n=n: e.matmul(
            psA[0:16, n * 512:(n + 1) * 512], ones1f[0:1, 0:16],
            dbar[:, n * 512:(n + 1) * 512], start=True, stop=True))
    k.tmark()
    k.wait("vector", k.T)
    k.vec(lambda e: e.tensor_tensor(small3[0:16, :], psA[0:16, 0:TC], small[0:16, :],
                                    ALU.mult))
    k.vec(lambda e: e.tensor_scalar(small3[0:16, :], small3[0:16, :], sa_sb[0:16, 0:1],
                                    None, ALU.mult))
    k.vmark()
    k.wait("sync", k.V)
    k.dma("sync", b_out[:, :], small3[0:16, :], ST)
    k.wait("sync", ST)
    return k.build()


def build_B():
    k = KB()
    a_in = k.inp("a_in", [4, T_])
    b_in = k.inp("b_in", [4, T_])
    hs_out = k.outp("hs_out", [4, T_])
    at = k.sbuf("at", [4, T_])
    bt = k.sbuf("bt", [4, T_])
    ht = k.sbuf("ht", [4, T_])
    LD = k.sem("LD")
    ST = k.sem("ST")
    k.dma("sync", at[:, :], a_in[:, :], LD)
    k.dma("sync", bt[:, :], b_in[:, :], LD)
    k.wait("vector", LD)
    k.vec(lambda e: e.tensor_tensor_scan(ht[:, :], at[:, :], bt[:, :], 0.0,
                                         ALU.mult, ALU.add))
    k.vmark()
    k.wait("sync", k.V)
    k.dma("sync", hs_out[:, :], ht[:, :], ST)
    k.wait("sync", ST)
    return k.build()



def ln_inplace(k, xsb, bx, xnb, scr, stat, ones_k, ones1f, psA, psB, eps_ap):
    m = stat[0:1, 0:TC]
    v = stat[0:1, TC:2 * TC]
    tmp = psA[64:65, 0:TC]
    for c in range(KC):
        k.vec(lambda e, c=c: e.tensor_copy(
            bx[:, c * TC:(c + 1) * TC], xsb[:, c * TC:(c + 1) * TC]))
    k.wait("tensor", k.V)
    for n in range(4):
        for c in range(KC):
            k.mm(lambda e, n=n, c=c: e.matmul(
                psA[0:1, n * 512:(n + 1) * 512], ones_k[:, 0:1],
                bx[:, c * TC + n * 512: c * TC + n * 512 + 512],
                start=(c == 0), stop=(c == KC - 1)))
    k.tlast()
    k.wait("scalar", k.T)
    for c in range(KC):
        k.sca(lambda e, c=c: e.activation(
            out=bx[:, c * TC:(c + 1) * TC], in_=bx[:, c * TC:(c + 1) * TC],
            func=AF.Square))
    k.wait("tensor", k.S)
    for n in range(4):
        for c in range(KC):
            k.mm(lambda e, n=n, c=c: e.matmul(
                psB[0:1, n * 512:(n + 1) * 512], ones_k[:, 0:1],
                bx[:, c * TC + n * 512: c * TC + n * 512 + 512],
                start=(c == 0), stop=(c == KC - 1)))
    k.tlast()
    k.wait("vector", k.T)
    k.vec(lambda e: e.tensor_scalar(m, psA[0:1, 0:TC], 1.0 / D_, None, ALU.mult))
    k.vec(lambda e: e.tensor_scalar(v, psB[0:1, 0:TC], 1.0 / D_, None, ALU.mult))
    k.vec(lambda e: e.tensor_tensor(tmp, m, m, ALU.mult))
    k.vec(lambda e: e.tensor_tensor(v, v, tmp, ALU.subtract))
    k.wait("scalar", k.V)
    k.sca(lambda e: e.activation(out=tmp, in_=v, func=AF.Ln, bias=eps_ap))
    k.sca(lambda e: e.activation(out=v, in_=tmp, func=AF.Exp, scale=-0.5))
    k.wait("tensor", k.S)
    k.wait("tensor", k.V)
    for n in range(4):
        k.mm(lambda e, n=n: e.matmul(
            psA[:, n * 512:(n + 1) * 512], ones1f[0:1, :],
            m[:, n * 512:(n + 1) * 512], start=True, stop=True))
        k.mm(lambda e, n=n: e.matmul(
            psB[:, n * 512:(n + 1) * 512], ones1f[0:1, :],
            v[:, n * 512:(n + 1) * 512], start=True, stop=True))
    k.tlast()
    k.wait("vector", k.T)
    for c in range(KC):
        for hh in range(2):
            k.vec(lambda e, c=c, hh=hh: e.tensor_tensor(
                scr[:, :], xsb[:, c * TC + hh * 1024: c * TC + (hh + 1) * 1024],
                psA[:, hh * 1024:(hh + 1) * 1024], ALU.subtract))
            k.vec(lambda e, c=c, hh=hh: e.tensor_tensor(
                xnb[:, c * TC + hh * 1024: c * TC + (hh + 1) * 1024], scr[:, :],
                psB[:, hh * 1024:(hh + 1) * 1024], ALU.mult))


def build_C():
    k = KB()
    xT = k.inp("xT", [D_, TC])
    xn_in = k.inp("xn_in", [D_, TC], BF16)
    hs_in = k.inp("hs_in", [16, TC])
    ct_in = k.inp("ct_in", [16, TC])
    Wo = k.inp("Wo", [D_, D_])
    bo = k.inp("bo", [D_, 1])
    mk_bd = k.inp("mk_bd", [4 * 8, 128, 128])
    up_bd = k.inp("up_bd", [4 * 8, 128, 512])
    wcp = k.inp("wcp", [128 * 128, D_])
    WqT = k.inp("WqT", [D_, D_])
    WkT = k.inp("WkT", [D_, D_])
    WvT = k.inp("WvT", [D_, D_])
    wcp16 = k.dram("wcp16", [128 * 128, D_], BF16)
    x2_out = k.outp("x2_out", [D_, TC])
    q_out = k.outp("q_out", [D_, TC], BF16)
    k_out = k.outp("k_out", [D_, TC], BF16)
    v_out = k.outp("v_out", [TC, D_], BF16)
    qn2_out = k.outp("qn2_out", [16, TC])
    kn2_out = k.outp("kn2_out", [16, TC])

    xsb = k.sbuf("xsb", [128, KC * TC])
    bx = k.sbuf("bx", [128, KC * TC], BF16)
    xnb = k.sbuf("xnb", [128, KC * TC], BF16)
    c8 = k.sbuf("c8", [128, 32 * 512], BF16)
    scr = k.sbuf("scr", [128, TC // 2])
    stat = k.sbuf("stat", [1, 2 * TC])
    wst = k.sbuf("wst", [128, 8 * 1024], BF16)
    hbuf = k.sbuf("hbuf", [128, 8 * 256], BF16)
    stage = k.sbuf("stage", [128, TC], BF16)
    stage2 = hbuf
    rows = stat[0:1, 0:TC]
    bo_sb = k.sbuf("bo_sb", [128, KC])
    ones_k = k.sbuf("ones_k", [128, 1], BF16)
    ones1f = k.sbuf("ones1f", [1, 128])
    eps5 = k.sbuf("eps5", [1, 1])
    psA = k.psum("psA", [128, 2048])
    psB = k.psum("psB", [128, 2048])

    LD = k.sem("LD")
    LW = (k.sem("LW0"), k.sem("LW1"))
    LWs = k.sem("LWs")
    LB = k.sem("LB")
    CV = k.sem("CV")
    CVL = (k.sem("CVL0"), k.sem("CVL1"))
    ST = k.sem("ST")
    STS = k.sem("STS")
    STR = k.sem("STR")

    for c in range(KC):
        k.dma("sync", xsb[:, c * TC:(c + 1) * TC], xT[c * 128:(c + 1) * 128, :], LD)
        k.dma("sync", xnb[:, c * TC:(c + 1) * TC], xn_in[c * 128:(c + 1) * 128, :], LD)
        k.dma("sync", bo_sb[:, c:c + 1], bo[c * 128:(c + 1) * 128, :], LD)
    k.dma("gpsimd", stage[0:16, :], hs_in[:, :], LB)
    k.vec(lambda e: e.memset(ones_k[:, :], 1.0))
    k.vec(lambda e: e.memset(ones1f[:, :], 1.0))
    k.vec(lambda e: e.memset(eps5[0:1, 0:1], 1e-5))
    for i in range(32):
        k.dma("gpsimd", bx[:, i * 128:(i + 1) * 128], mk_bd[i, :, :], LB)
    for eng in ("vector", "scalar", "tensor"):
        k.wait(eng, LD)
        k.wait(eng, LB)

    twar = []
    vwar = []
    for i in range(32):
        slot = (i % 2) * 1024
        if i >= 2:
            k.wait("gpsimd", k.T, twar[i - 2])
        k.dma("gpsimd", wst[:, slot:slot + 512], up_bd[i, :, :], LW[i % 2])
        k.wait("tensor", LW[i % 2])
        if i >= 2:
            k.wait("tensor", k.V, vwar[i - 2])
        ps = psA if i % 2 == 0 else psB
        k.mm(lambda e, i=i, slot=slot, ps=ps: e.matmul(
            ps[:, 0:512], bx[:, i * 128:(i + 1) * 128], wst[:, slot:slot + 512],
            start=True, stop=True))
        k.tlast()
        twar.append(k.T.n)
        k.wait("vector", k.T)
        k.vec(lambda e, i=i, ps=ps: e.tensor_copy(
            c8[:, i * 512:(i + 1) * 512], ps[:, 0:512]))
        vwar.append(k.V.n)
    k.barrier()
    k.dma("gpsimd", bx[:, 0:TC][0:16, :], ct_in[:, :], LB)

    cv_done = []
    for i in range(128):
        sl = (i % 2) * 1024
        if i >= 2:
            k.wait("gpsimd", CV, cv_done[i - 2])
        k.dma("gpsimd", hbuf[:, sl:sl + 1024], wcp[i * 128:(i + 1) * 128, :],
              CVL[i % 2])
        k.wait("sync", CVL[i % 2])
        k.dma("sync", wcp16[i * 128:(i + 1) * 128, :], hbuf[:, sl:sl + 1024], CV)
        cv_done.append(CV.n)

    k.wait("vector", LB)
    k.vec(lambda e: e.tensor_tensor(stage[0:16, :], stage[0:16, :],
                                    bx[:, 0:TC][0:16, :], ALU.mult))
    k.wait("tensor", k.V)
    for n in range(4):
        k.mm(lambda e, n=n: e.matmul(
            psB[0:1, n * 512:(n + 1) * 512], ones_k[0:16, 0:1],
            stage[0:16, n * 512:(n + 1) * 512], start=True, stop=True))
    k.tlast()
    k.wait("vector", k.T)
    k.vec(lambda e: e.tensor_copy(rows, psB[0:1, 0:TC]))
    k.wait("tensor", k.V)
    for n in range(4):
        k.mm(lambda e, n=n: e.matmul(
            psA[:, n * 512:(n + 1) * 512], ones1f[0:1, :],
            rows[:, n * 512:(n + 1) * 512], start=True, stop=True))
    k.tlast()
    k.wait("vector", k.T)
    for c in range(KC):
        k.vec(lambda e, c=c: e.tensor_tensor(
            bx[:, c * TC:(c + 1) * TC], xnb[:, c * TC:(c + 1) * TC], psA[:, 0:TC],
            ALU.mult))
    k.barrier()

    twar = []
    vwar = []
    for mi in range(KC):
        ps = psA if mi % 2 == 0 else psB
        slot = (mi % 2) * 1024
        if mi >= 2:
            k.wait("gpsimd", k.T, twar[mi - 2])
        for c in range(KC):
            k.dma("gpsimd", wst[:, slot + c * 128: slot + (c + 1) * 128],
                  Wo[c * 128:(c + 1) * 128, mi * 128:(mi + 1) * 128], LW[mi % 2])
        k.wait("tensor", LW[mi % 2])
        if mi >= 2:
            k.wait("tensor", k.V, vwar[mi - 2])
        for n in range(4):
            for c in range(KC):
                k.mm(lambda e, n=n, c=c, ps=ps, slot=slot: e.matmul(
                    ps[:, n * 512:(n + 1) * 512],
                    wst[:, slot + c * 128: slot + (c + 1) * 128],
                    bx[:, c * TC + n * 512: c * TC + n * 512 + 512],
                    start=(c == 0), stop=(c == KC - 1)))
        k.tlast()
        twar.append(k.T.n)
        k.wait("vector", k.T)
        k.vec(lambda e, mi=mi, ps=ps: e.scalar_tensor_tensor(
            xsb[:, mi * TC:(mi + 1) * TC], ps[:, 0:TC], bo_sb[:, mi:mi + 1],
            xsb[:, mi * TC:(mi + 1) * TC], ALU.add, ALU.add))
        vwar.append(k.V.n)
    k.barrier()

    ln_inplace(k, xsb, bx, xnb, scr, stat, ones_k, ones1f, psA, psB,
               eps5[0:1, 0:1])
    k.barrier()

    k.wait("sync", CV)
    k.wait("gpsimd", CV)
    k.wait("scalar", CV)
    twar_cp = []
    swar_r = []
    vwar_ev = []
    for kg in range(16):
        for i8 in range(8):
            kidx = kg * 8 + i8
            qeng = "sync" if i8 % 2 == 0 else "gpsimd"
            qsem = LWs if i8 % 2 == 0 else LW[1]
            if twar_cp:
                k.wait(qeng, k.T, twar_cp[-1])
            k.dma(qeng, wst[:, i8 * 1024:(i8 + 1) * 1024],
                  wcp16[kidx * 128:(kidx + 1) * 128, :], qsem)
        for i8 in range(8):
            kidx = kg * 8 + i8
            h, g, msub = kidx // 32, (kidx // 4) % 8, kidx % 4
            ps = psA if i8 % 2 == 0 else psB
            if len(swar_r) >= 2:
                k.wait("tensor", k.S, swar_r[-2])
            if i8 == 0 and len(vwar_ev) >= 2:
                k.wait("tensor", k.V, vwar_ev[-2])
            if i8 == 1 and len(vwar_ev) >= 1:
                k.wait("tensor", k.V, vwar_ev[-1])
            for n in range(4):
                k.mm(lambda e, h=h, g=g, msub=msub, n=n, ps=ps: e.matmul(
                    ps[:, n * 512:(n + 1) * 512],
                    c8[:, (h * 8 + g) * 512 + msub * 128:
                       (h * 8 + g) * 512 + (msub + 1) * 128],
                    xnb[:, g * TC + n * 512: g * TC + n * 512 + 512],
                    start=True, stop=True))
            k.tlast()
            k.wait("scalar", k.T)
            k.sca(lambda e, i8=i8, ps=ps: e.activation(
                out=bx[:, i8 * TC:(i8 + 1) * TC], in_=ps[:, 0:TC], func=AF.Gelu))
            swar_r.append(k.S.n)
        for mo in range(8):
            ps = psA if mo % 2 == 0 else psB
            if mo == 0:
                k.wait("tensor", k.S, swar_r[-2])
                k.wait("tensor", LWs)
                k.wait("tensor", LW[1])
            if mo == 1:
                k.wait("tensor", k.S, swar_r[-1])
            if len(vwar_ev) >= 2:
                k.wait("tensor", k.V, vwar_ev[-2])
            for i8 in range(8):
                for n in range(4):
                    k.mm(lambda e, i8=i8, mo=mo, n=n, ps=ps: e.matmul(
                        ps[:, n * 512:(n + 1) * 512],
                        wst[:, i8 * 1024 + mo * 128: i8 * 1024 + (mo + 1) * 128],
                        bx[:, i8 * TC + n * 512: i8 * TC + n * 512 + 512],
                        start=(i8 == 0), stop=(i8 == 7)))
            k.tlast()
            k.wait("vector", k.T)
            k.vec(lambda e, mo=mo, ps=ps: e.tensor_tensor(
                xsb[:, mo * TC:(mo + 1) * TC], ps[:, 0:TC],
                xsb[:, mo * TC:(mo + 1) * TC], ALU.add))
            vwar_ev.append(k.V.n)
        twar_cp.append(k.T.n)
    k.barrier()
    k.wait("sync", k.V)
    for c in range(KC):
        k.dma("sync", x2_out[c * 128:(c + 1) * 128, :], xsb[:, c * TC:(c + 1) * TC], ST)

    ln_inplace(k, xsb, bx, xnb, scr, stat, ones_k, ones1f, psA, psB,
               eps5[0:1, 0:1])
    k.barrier()

    st_stage = None
    st_rows = None
    for which, wt, o_main, o_n2, scale in (
            (0, WqT, q_out, qn2_out, 1.0 / 32.0), (1, WkT, k_out, kn2_out, 1.0)):
        k.wait("gpsimd", k.T)
        for c in range(KC):
            k.dma("gpsimd", c8[:, c * 1024:(c + 1) * 1024],
                  wt[c * 128:(c + 1) * 128, :], LB)
        k.wait("tensor", LB)
        vwar = []
        for mi in range(KC):
            ps = psA if mi % 2 == 0 else psB
            pso = psB if mi % 2 == 0 else psA
            if mi >= 1:
                k.wait("tensor", k.V, vwar[mi - 1])
            for n in range(4):
                for c in range(KC):
                    k.mm(lambda e, mi=mi, n=n, c=c, ps=ps: e.matmul(
                        ps[:, n * 512:(n + 1) * 512],
                        c8[:, c * 1024 + mi * 128: c * 1024 + (mi + 1) * 128],
                        xnb[:, c * TC + n * 512: c * TC + n * 512 + 512],
                        start=(c == 0), stop=(c == KC - 1)))
            k.tlast()
            k.wait("vector", k.T)
            if st_stage is not None:
                k.wait("vector", STS, st_stage)
            k.wait("vector", k.S)
            k.vec(lambda e, ps=ps, scale=scale: e.tensor_scalar(
                stage[:, :], ps[:, 0:TC], scale, None, ALU.mult))
            k.wait("sync", k.V)
            k.dma("sync", o_main[mi * 128:(mi + 1) * 128, :], stage[:, :], STS)
            st_stage = STS.n
            k.wait("scalar", k.V)
            k.sca(lambda e: e.activation(out=stage2[:, 0:TC], in_=stage[:, :],
                                         func=AF.Square))
            k.wait("tensor", k.S)
            for half in (0, 64):
                k.wait("tensor", k.V)
                for n in range(4):
                    k.mm(lambda e, pso=pso, half=half, n=n: e.matmul(
                        pso[0:1, n * 512:(n + 1) * 512],
                        ones_k[half:half + 64, 0:1],
                        stage2[half:half + 64, n * 512:(n + 1) * 512],
                        start=True, stop=True))
                k.tlast()
                k.wait("vector", k.T)
                if st_rows is not None:
                    k.wait("vector", STR, st_rows)
                k.vec(lambda e, pso=pso: e.tensor_copy(rows, pso[0:1, 0:TC]))
                k.wait("sync", k.V)
                k.dma("sync", o_n2[mi * 2 + half // 64: mi * 2 + half // 64 + 1, :],
                      rows, STR)
                st_rows = STR.n
            vwar.append(k.V.n)
        k.barrier()

    k.wait("gpsimd", k.T)
    for c in range(KC):
        k.dma("gpsimd", c8[:, c * 1024:(c + 1) * 1024],
              WvT[c * 128:(c + 1) * 128, :], LB)
    k.wait("tensor", LB)
    vwar = []
    for tokm in range(16):
        ps = psA if tokm % 2 == 0 else psB
        if tokm >= 2:
            k.wait("tensor", k.V, vwar[tokm - 2])
        for dh in range(2):
            for c in range(KC):
                k.mm(lambda e, tokm=tokm, dh=dh, c=c, ps=ps: e.matmul(
                    ps[:, dh * 512:(dh + 1) * 512],
                    xnb[:, c * TC + tokm * 128: c * TC + (tokm + 1) * 128],
                    c8[:, c * 1024 + dh * 512: c * 1024 + (dh + 1) * 512],
                    start=(c == 0), stop=(c == KC - 1)))
        k.tlast()
        k.wait("vector", k.T)
        if st_stage is not None:
            k.wait("vector", STS, st_stage)
        k.vec(lambda e, ps=ps: e.tensor_copy(stage[:, 0:1024], ps[:, 0:1024]))
        vwar.append(k.V.n)
        k.wait("sync", k.V)
        k.dma("sync", v_out[tokm * 128:(tokm + 1) * 128, :], stage[:, 0:1024], STS)
        st_stage = STS.n
    k.wait("sync", ST)
    k.wait("sync", STS)
    k.wait("sync", STR)
    return k.build()


def build_D():
    k = KB()
    qh = k.inp("qh", [4, 64, T_], BF16)
    kh = k.inp("kh", [4, 64, T_], BF16)
    vh = k.inp("vh", [4, T_, 64], BF16)
    qn2h = k.inp("qn2h", [4, T_])
    kn2h = k.inp("kn2h", [4, T_])
    oh = k.outp("oh", [4, 64, T_])

    ktil = k.sbuf("ktil", [65, T_], BF16)
    qtil = k.sbuf("qtil", [65, T_], BF16)
    vt = k.sbuf("vt", [128, 64 * 65], BF16)
    ptil = k.sbuf("ptil", [128, 2048], BF16)
    rowb = k.sbuf("rowb", [1, T_])
    otmp = k.sbuf("otmp", [64, 512])
    rc = k.sbuf("rc", [1, 512])
    sc = k.sbuf("sc", [1, 16])
    ones1f = k.sbuf("ones1f", [1, 128])
    psA = k.psum("psA", [128, 2048])
    psB = k.psum("psB", [128, 2048])

    LDK = k.sem("LDK")
    LDQ = k.sem("LDQ")
    LDV = k.sem("LDV")
    LDR = k.sem("LDR")
    STO = k.sem("STO")

    k.vec(lambda e: e.memset(ktil[64:65, :], 1.0))
    k.vec(lambda e: e.memset(vt[:, :], 1.0))
    k.vec(lambda e: e.memset(ones1f[:, :], 1.0))

    st_o = None
    for p in range(4):
        if p > 0:
            k.wait("gpsimd", k.T)
            k.wait("gpsimd", k.V)
        k.dma("gpsimd", ktil[0:64, :], kh[p, :, :], LDK)
        k.dma("gpsimd", qtil[0:64, :], qh[p, :, :], LDQ)
        for kt in range(64):
            k.dma("gpsimd", vt[:, kt * 65:kt * 65 + 64],
                  vh[p, kt * 128:(kt + 1) * 128, :], LDV)
        k.dma("gpsimd", rowb[0:1, :], kn2h[p:p + 1, :], LDR)
        k.wait("vector", LDR)
        k.vec(lambda e: e.tensor_reduce(sc[0:1, 0:1], rowb[0:1, :], AX, ALU.max))
        k.wait("gpsimd", k.V)
        k.dma("gpsimd", rowb[0:1, :], qn2h[p:p + 1, :], LDR)
        k.wait("vector", LDR)
        k.vec(lambda e: e.tensor_scalar(rowb[0:1, :], rowb[0:1, :], sc[0:1, 0:1],
                                        None, ALU.mult))
        k.wait("scalar", k.V)
        k.sca(lambda e: e.activation(out=rowb[0:1, :], in_=rowb[0:1, :],
                                     func=AF.Sqrt))
        k.wait("vector", k.S)
        k.wait("vector", LDQ)
        k.vec(lambda e: e.tensor_scalar(qtil[64:65, :], rowb[0:1, :], -1.0,
                                        None, ALU.mult))
        k.wait("tensor", LDK)
        k.wait("tensor", LDV)
        k.wait("tensor", k.V)
        k.wait("scalar", k.V)
        swar = []
        twar_pv = []
        vwar_q = None

        for qt in range(16):
            for kg in range(33):
                pslot = kg % 2
                if kg < 32:
                    if len(swar) >= 2:
                        k.wait("tensor", k.S, swar[-2])
                    if kg == 0 and vwar_q is not None:
                        k.wait("tensor", k.V, vwar_q)
                    for j in (0, 1):
                        kt = kg * 2 + j
                        k.mm(lambda e, kt=kt, j=j, pslot=pslot, qt=qt: e.matmul(
                            psA[:, pslot * 1024 + j * 512:
                                pslot * 1024 + (j + 1) * 512],
                            ktil[:, kt * 128:(kt + 1) * 128],
                            qtil[:, qt * 512:(qt + 1) * 512],
                            start=True, stop=True))
                    k.tlast()
                    k.wait("scalar", k.T)
                    if len(twar_pv) >= 2:
                        k.wait("scalar", k.T, twar_pv[-2])
                    k.sca(lambda e, pslot=pslot: e.activation(
                        out=ptil[:, pslot * 1024:(pslot + 1) * 1024],
                        in_=psA[:, pslot * 1024:(pslot + 1) * 1024], func=AF.Exp))
                    swar.append(k.S.n)
                if kg >= 1:
                    pv = kg - 1
                    pvslot = pv % 2
                    k.wait("tensor", k.S, swar[-1] if kg == 32 else swar[-2])
                    for j in (0, 1):
                        kt = pv * 2 + j
                        k.mm(lambda e, kt=kt, j=j, pvslot=pvslot: e.matmul(
                            psB[0:65, 0:512], vt[:, kt * 65:(kt + 1) * 65],
                            ptil[:, pvslot * 1024 + j * 512:
                                 pvslot * 1024 + (j + 1) * 512],
                            start=(kt == 0), stop=(kt == 63)))
                    k.tlast()
                    twar_pv.append(k.T.n)
            k.wait("vector", k.T)
            if st_o is not None:
                k.wait("vector", STO, st_o)
            k.vec(lambda e: e.reciprocal(rc[0:1, :], psB[64:65, 0:512]))
            k.wait("tensor", k.V)
            k.mm(lambda e: e.matmul(psA[0:64, 0:512], ones1f[0:1, 0:64],
                                    rc[0:1, :], start=True, stop=True))
            k.tlast()
            k.wait("vector", k.T)
            k.vec(lambda e: e.tensor_copy(otmp[:, :], psB[0:64, 0:512]))
            k.vec(lambda e: e.tensor_tensor(otmp[:, :], otmp[:, :],
                                            psA[0:64, 0:512], ALU.mult))
            vwar_q = k.V.n
            k.wait("sync", k.V)
            k.dma("sync", oh[p, :, qt * 512:(qt + 1) * 512], otmp[:, :], STO)
            st_o = STO.n
    
    k.wait("sync", STO)
    return k.build()


def build_E():
    twar_w1 = []
    twar_w2 = []
    vwar_w2 = []
    """Residual + double-LN (folded) + router/top-2 + dense 8-expert MoE.
    W1 feature-major; W2 token-major (gate = per-partition scalar at evac);
    expert contributions accumulated in HBM via DMA accum_op=add."""
    k = KB()
    xT2 = k.inp("xT2", [D_, TC])
    oT = k.inp("oT", [D_, TC])
    rls = k.inp("rls", [D_, 1])
    rlb = k.inp("rlb", [D_, 1])
    rwT = k.inp("rwT", [D_, 8])
    rb_in = k.inp("rb_in", [1, 8])
    els = k.inp("els", [8, D_])
    elb = k.inp("elb", [8, D_])
    w1t = k.inp("w1t", [8, D_, 4096])
    b1_in = k.inp("b1_in", [8, 4096])
    w2t = k.inp("w2t", [8, 4096, D_])
    b2_in = k.inp("b2_in", [8, D_])
    eyef = k.inp("eyef", [128, 128])
    out = k.outp("out", [D_, TC])

    xstr = k.sbuf("xstr", [128, 2048])
    xhb = k.sbuf("xhb", [128, KC * TC], BF16)
    xeb = k.sbuf("xeb", [128, KC * TC], BF16)
    hq = k.sbuf("hq", [128, KC * TC], BF16)
    w2res = k.sbuf("w2res", [128, 2 * KC * 1024], BF16)
    wst = k.sbuf("wst", [128, 2048], BF16)
    stage = k.sbuf("stage", [128, 2048])
    stat = k.sbuf("stat", [1, 2 * TC])
    gates = k.sbuf("gates", [128, 16 * 8])
    garow = k.sbuf("garow", [8, TC], BF16)
    gb = k.sbuf("gb", [128, TC], BF16)
    rwb = k.sbuf("rwb", [128, KC * 8], BF16)
    rwu = k.sbuf("rwu", [128, KC * 8], BF16)
    rlbb = k.sbuf("rlbb", [128, KC], BF16)
    lsb = k.sbuf("lsb", [128, 2 * KC])
    b1sb = k.sbuf("b1sb", [128, KC])
    b2sb = k.sbuf("b2sb", [128, KC])
    rbr = k.sbuf("rbr", [1, 8])
    eyf = k.sbuf("eyf", [128, 128])
    scp = k.sbuf("scp", [128, 8])
    bcb = k.sbuf("bcb", [128, 8])
    ones_k = k.sbuf("ones_k", [128, 1], BF16)
    ones1f = k.sbuf("ones1f", [1, 128])
    eps5 = k.sbuf("eps5", [1, 2])
    psA = k.psum("psA", [128, 2048])
    psB = k.psum("psB", [128, 2048])

    LD = k.sem("LD")
    LO = k.sem("LO")
    LB = k.sem("LB")
    LWQ = (k.sem("LWq0"), k.sem("LWq1"))
    LW = (k.sem("LW0"), k.sem("LW1"))
    STA = (k.sem("STA0"), k.sem("STA1"))
    STX = (k.sem("STX0"), k.sem("STX1"))

    for c in range(KC):
        k.dma("gpsimd", rwu[:, c * 8:(c + 1) * 8], rwT[c * 128:(c + 1) * 128, :], LB)
        k.dma("sync", lsb[:, c:c + 1], rls[c * 128:(c + 1) * 128, :], LD)
        k.dma("sync", lsb[:, KC + c:KC + c + 1], rlb[c * 128:(c + 1) * 128, :], LD)
    k.dma("sync", eyf[:, :], eyef[:, :], LD)
    k.dma("sync", rbr[0:1, :], rb_in[:, :], LD)
    k.vec(lambda e: e.memset(ones_k[:, :], 1.0))
    k.vec(lambda e: e.memset(ones1f[:, :], 1.0))
    k.vec(lambda e: e.memset(eps5[0:1, 0:1], 1e-5))
    k.vec(lambda e: e.memset(eps5[0:1, 1:2], 1e-6))
    for eng in ("vector", "scalar", "tensor"):
        k.wait(eng, LD)
        k.wait(eng, LB)

    for c in range(KC):
        for hh in range(2):
            if c > 0 or hh > 0:
                k.wait("gpsimd", k.V)
            k.dma("gpsimd", xstr[:, 0:1024],
                  xT2[c * 128:(c + 1) * 128, hh * 1024:(hh + 1) * 1024], LO)
            k.dma("gpsimd", xstr[:, 1024:2048],
                  oT[c * 128:(c + 1) * 128, hh * 1024:(hh + 1) * 1024], LO)
            k.wait("vector", LO)
            k.vec(lambda e: e.tensor_tensor(
                xstr[:, 0:1024], xstr[:, 0:1024], xstr[:, 1024:2048], ALU.add))
            k.vec(lambda e, c=c, hh=hh: e.tensor_copy(
                xeb[:, c * TC + hh * 1024: c * TC + (hh + 1) * 1024],
                xstr[:, 0:1024]))
    k.barrier()

    m = stat[0:1, 0:TC]
    v = stat[0:1, TC:2 * TC]
    tmp = psA[64:65, 0:TC]
    r1row = psB[64:65, 0:TC]
    k.wait("tensor", k.V)
    for n in range(4):
        for c in range(KC):
            k.mm(lambda e, n=n, c=c: e.matmul(
                psA[0:1, n * 512:(n + 1) * 512], ones_k[:, 0:1],
                xeb[:, c * TC + n * 512: c * TC + n * 512 + 512],
                start=(c == 0), stop=(c == KC - 1)))
    k.tlast()
    k.wait("scalar", k.T)
    for c in range(KC):
        k.sca(lambda e, c=c: e.activation(
            out=xeb[:, c * TC:(c + 1) * TC], in_=xeb[:, c * TC:(c + 1) * TC],
            func=AF.Square))
    k.wait("tensor", k.S)
    for n in range(4):
        for c in range(KC):
            k.mm(lambda e, n=n, c=c: e.matmul(
                psB[0:1, n * 512:(n + 1) * 512], ones_k[:, 0:1],
                xeb[:, c * TC + n * 512: c * TC + n * 512 + 512],
                start=(c == 0), stop=(c == KC - 1)))
    k.tlast()
    k.wait("vector", k.T)
    k.vec(lambda e: e.tensor_scalar(m, psA[0:1, 0:TC], 1.0 / D_, None, ALU.mult))
    k.vec(lambda e: e.tensor_scalar(v, psB[0:1, 0:TC], 1.0 / D_, None, ALU.mult))
    k.vec(lambda e: e.tensor_tensor(tmp, m, m, ALU.mult))
    k.vec(lambda e: e.tensor_tensor(v, v, tmp, ALU.subtract))
    k.wait("scalar", k.V)
    k.sca(lambda e: e.activation(out=tmp, in_=v, func=AF.Ln, bias=eps5[0:1, 0:1]))
    k.sca(lambda e: e.activation(out=r1row, in_=tmp, func=AF.Exp, scale=-0.5))
    k.wait("vector", k.S)
    k.vec(lambda e: e.tensor_tensor(v, v, r1row, ALU.mult))
    k.vec(lambda e: e.tensor_tensor(v, v, r1row, ALU.mult))
    k.wait("scalar", k.V)
    k.sca(lambda e: e.activation(out=tmp, in_=v, func=AF.Ln, bias=eps5[0:1, 1:2]))
    k.sca(lambda e: e.activation(out=v, in_=tmp, func=AF.Exp, scale=-0.5))
    k.wait("vector", k.S)
    k.vec(lambda e: e.tensor_tensor(v, v, r1row, ALU.mult))
    k.wait("tensor", k.V)
    k.wait("tensor", k.S)
    for n in range(4):
        k.mm(lambda e, n=n: e.matmul(
            psA[:, n * 512:(n + 1) * 512], ones1f[0:1, :],
            m[:, n * 512:(n + 1) * 512], start=True, stop=True))
        k.mm(lambda e, n=n: e.matmul(
            psB[:, n * 512:(n + 1) * 512], ones1f[0:1, :],
            v[:, n * 512:(n + 1) * 512], start=True, stop=True))
    k.tlast()
    k.wait("vector", k.T)
    stx_last = None
    for c in range(KC):
        for hh in range(2):
            k.wait("gpsimd", k.V)
            if stx_last is not None:
                k.wait("gpsimd", STX[0], stx_last)
            k.dma("gpsimd", xstr[:, 0:1024],
                  xT2[c * 128:(c + 1) * 128, hh * 1024:(hh + 1) * 1024], LO)
            k.dma("gpsimd", xstr[:, 1024:2048],
                  oT[c * 128:(c + 1) * 128, hh * 1024:(hh + 1) * 1024], LO)
            k.wait("vector", LO)
            k.vec(lambda e: e.tensor_tensor(
                xstr[:, 0:1024], xstr[:, 0:1024], xstr[:, 1024:2048], ALU.add))
            k.vec(lambda e, hh=hh: e.tensor_tensor(
                xstr[:, 1024:2048], xstr[:, 0:1024],
                psA[:, hh * 1024:(hh + 1) * 1024], ALU.subtract))
            k.vec(lambda e, c=c, hh=hh: e.tensor_tensor(
                xhb[:, c * TC + hh * 1024: c * TC + (hh + 1) * 1024],
                xstr[:, 1024:2048], psB[:, hh * 1024:(hh + 1) * 1024], ALU.mult))
            k.wait("sync", k.V)
            k.dma("sync", out[c * 128:(c + 1) * 128, hh * 1024:(hh + 1) * 1024],
                  xstr[:, 0:1024], STX[0])
            stx_last = STX[0].n
    k.barrier()

    for c in range(KC):
        k.vec(lambda e, c=c: e.tensor_scalar(
            rwb[:, c * 8:(c + 1) * 8], rwu[:, c * 8:(c + 1) * 8],
            lsb[:, c:c + 1], None, ALU.mult))
        k.vec(lambda e, c=c: e.tensor_copy(rlbb[:, c:c + 1],
                                           lsb[:, KC + c:KC + c + 1]))
    k.wait("tensor", k.V)
    for c in range(KC):
        k.mm(lambda e, c=c: e.matmul(
            psA[0:1, 0:8], rlbb[:, c:c + 1], rwu[:, c * 8:(c + 1) * 8],
            start=(c == 0), stop=(c == KC - 1)))
    k.tlast()
    k.wait("vector", k.T)
    k.vec(lambda e: e.tensor_tensor(rbr[0:1, :], psA[0:1, 0:8], rbr[0:1, :],
                                    ALU.add))
    k.wait("tensor", k.V)
    k.mm(lambda e: e.matmul(psA[:, 0:8], ones1f[0:1, :], rbr[0:1, :],
                            start=True, stop=True))
    k.tlast()
    k.wait("vector", k.T)
    k.vec(lambda e: e.tensor_copy(bcb[:, :], psA[:, 0:8]))
    for tokm in range(16):
        if tokm > 0:
            k.wait("tensor", k.V)
        for c in range(KC):
            k.mm(lambda e, tokm=tokm, c=c: e.matmul(
                psB[:, 0:8], xhb[:, c * TC + tokm * 128: c * TC + (tokm + 1) * 128],
                rwb[:, c * 8:(c + 1) * 8], start=(c == 0), stop=(c == KC - 1)))
        k.tlast()
        k.wait("vector", k.T)
        k.vec(lambda e, tokm=tokm: e.tensor_tensor(
            gates[:, tokm * 8:(tokm + 1) * 8], psB[:, 0:8], bcb[:, :], ALU.add))
    k.barrier()

    for tokm in range(16):
        ga = gates[:, tokm * 8:(tokm + 1) * 8]
        msk = lsb[:, 0:8]
        k.vec(lambda e, ga=ga: e.tensor_reduce(scp[:, 0:1], ga, AX, ALU.max))
        k.vec(lambda e: e.tensor_scalar(scp[:, 1:2], scp[:, 0:1], -1.0, None,
                                        ALU.mult))
        k.wait("scalar", k.V)
        k.sca(lambda e, ga=ga: e.activation(
            out=ga, in_=ga, func=AF.Exp, bias=scp[:, 1:2]))
        k.wait("vector", k.S)
        k.vec(lambda e, ga=ga: e.tensor_reduce(scp[:, 2:3], ga, AX, ALU.max))
        k.vec(lambda e, ga=ga, msk=msk: e.tensor_scalar(
            msk, ga, scp[:, 2:3], None, ALU.is_ge))
        k.vec(lambda e, ga=ga, msk=msk: e.scalar_tensor_tensor(
            msk, msk, -1e30, ga, ALU.mult, ALU.add))
        k.vec(lambda e, msk=msk: e.tensor_reduce(scp[:, 3:4], msk, AX, ALU.max))
        k.vec(lambda e, ga=ga, msk=msk: e.tensor_scalar(
            msk, ga, scp[:, 3:4], None, ALU.is_ge))
        k.vec(lambda e: e.tensor_tensor(scp[:, 4:5], scp[:, 2:3], scp[:, 3:4],
                                        ALU.add))
        k.vec(lambda e: e.reciprocal(scp[:, 5:6], scp[:, 4:5]))
        k.vec(lambda e, ga=ga, msk=msk: e.tensor_tensor(ga, ga, msk, ALU.mult))
        k.vec(lambda e, ga=ga: e.tensor_scalar(ga, ga, scp[:, 5:6], None,
                                               ALU.mult))
    k.barrier()

    for tokm in range(16):
        sl = (tokm % 2) * 512
        if tokm >= 2:
            k.wait("tensor", k.V)
        k.mm(lambda e, tokm=tokm, sl=sl: e.transpose(
            psA[0:8, sl:sl + 128], gates[:, tokm * 8:(tokm + 1) * 8], eyf[:, :]))
        k.tlast()
        k.wait("vector", k.T)
        k.vec(lambda e, tokm=tokm, sl=sl: e.tensor_copy(
            garow[0:8, tokm * 128:(tokm + 1) * 128], psA[0:8, sl:sl + 128]))
    k.barrier()

    k.wait("gpsimd", STX[0])

    st_sl = [None, None]
    swar_g = []
    for ei in range(8):
        k.wait("gpsimd", k.V)
        k.wait("gpsimd", k.T)
        for c in range(KC):
            k.dma("gpsimd", lsb[:, c:c + 1],
                  els[ei:ei + 1, c * 128:(c + 1) * 128].rearrange("a p -> p a"), LB)
            k.dma("gpsimd", lsb[:, KC + c:KC + c + 1],
                  elb[ei:ei + 1, c * 128:(c + 1) * 128].rearrange("a p -> p a"), LB)
            k.dma("gpsimd", b2sb[:, c:c + 1],
                  b2_in[ei:ei + 1, c * 128:(c + 1) * 128].rearrange("a p -> p a"), LB)
        k.dma("gpsimd", stat[0:1, 0:TC], garow[ei:ei + 1, :], LB)
        k.wait("tensor", LB)
        k.wait("tensor", k.V)
        for n in range(4):
            k.mm(lambda e, n=n: e.matmul(
                psB[:, n * 512:(n + 1) * 512], ones1f[0:1, :],
                stat[0:1, n * 512:(n + 1) * 512], start=True, stop=True))
        k.tlast()
        k.wait("vector", k.T)
        k.vec(lambda e: e.tensor_copy(gb[:, :], psB[:, 0:TC]))
        k.wait("vector", LB)
        k.wait("vector", k.T)
        for c in range(KC):
            k.vec(lambda e, c=c: e.tensor_scalar(
                xeb[:, c * TC:(c + 1) * TC], xhb[:, c * TC:(c + 1) * TC],
                lsb[:, c:c + 1], lsb[:, KC + c:KC + c + 1], ALU.mult, ALU.add))
        k.wait("tensor", k.V)
        for q in range(4):
            k.wait("gpsimd", k.V)
            k.wait("gpsimd", k.S)
            for mg in range(KC):
                k.dma("gpsimd", b1sb[:, mg:mg + 1],
                      b1_in[ei:ei + 1, q * 1024 + mg * 128: q * 1024 + (mg + 1) * 128
                            ].rearrange("a p -> p a"), LB)
            for mg in range(KC):
                wsl = (mg % 2) * 1024
                if len(twar_w1) >= 2:
                    k.wait("gpsimd", k.T, twar_w1[-2])
                k.dma("gpsimd",
                      wst[:, wsl:wsl + 1024].rearrange("p (c m) -> p c m", m=128),
                      w1t[ei, :, q * 1024 + mg * 128: q * 1024 + (mg + 1) * 128
                          ].rearrange("(c p) m -> p c m", p=128), LW[mg % 2])
                k.wait("tensor", LW[mg % 2])
                k.wait("tensor", LB)
                for n in range(4):
                    psl = ((mg * 4 + n) % 4) * 512
                    if len(swar_g) >= 4:
                        k.wait("tensor", k.S, swar_g[-4])
                    if mg == 0 and n < 2 and len(vwar_w2) >= 2:
                        k.wait("tensor", k.V, vwar_w2[-1 - n])
                    for c in range(KC):
                        k.mm(lambda e, mg=mg, n=n, c=c, wsl=wsl, psl=psl: e.matmul(
                            psB[:, psl + 0:psl + 512],
                            wst[:, wsl + c * 128: wsl + (c + 1) * 128],
                            xeb[:, c * TC + n * 512: c * TC + n * 512 + 512],
                            start=(c == 0), stop=(c == KC - 1)))
                    k.tlast()
                    k.wait("scalar", k.T)
                    k.sca(lambda e, mg=mg, n=n, psl=psl: e.activation(
                        out=hq[:, mg * TC + n * 512: mg * TC + (n + 1) * 512],
                        in_=psB[:, psl:psl + 512], func=AF.Gelu,
                        bias=b1sb[:, mg:mg + 1]))
                    swar_g.append(k.S.n)
                twar_w1.append(k.T.n)
            w2slot = (ei * 4 + q) % 2
            w2off = w2slot * KC * 1024
            if ei == 0 and q == 0:
                for c in range(KC):
                    k.dma("gpsimd", w2res[:, w2off + c * 1024:w2off + (c + 1) * 1024],
                          w2t[ei, q * 1024 + c * 128: q * 1024 + (c + 1) * 128, :],
                          LWQ[w2slot])
            nei, nq = (ei, q + 1) if q < 3 else (ei + 1, 0)
            if nei < 8:
                nslot = (nei * 4 + nq) % 2
                noff = nslot * KC * 1024
                if twar_w2:
                    k.wait("gpsimd", k.T, twar_w2[-1])
                for c in range(KC):
                    k.dma("gpsimd", w2res[:, noff + c * 1024:noff + (c + 1) * 1024],
                          w2t[nei, nq * 1024 + c * 128: nq * 1024 + (c + 1) * 128, :],
                          LWQ[nslot])
            k.wait("tensor", LWQ[w2slot])
            k.wait("tensor", k.S)
            for mo in range(8):
                ps = psA if mo % 2 == 0 else psB
                if len(vwar_w2) >= 2:
                    k.wait("tensor", k.V, vwar_w2[-2])
                for c in range(KC):
                    for tt in range(4):
                        k.mm(lambda e, mo=mo, c=c, tt=tt, ps=ps, w2off=w2off:
                             e.matmul(
                                 ps[:, tt * 512:(tt + 1) * 512],
                                 w2res[:, w2off + c * 1024 + mo * 128:
                                       w2off + c * 1024 + (mo + 1) * 128],
                                 hq[:, c * TC + tt * 512: c * TC + tt * 512 + 512],
                                 start=(c == 0), stop=(c == KC - 1)))
                k.tlast()
                twar_w2.append(k.T.n)
                k.wait("vector", k.T)
                for hhalf in range(2):
                    ssl = hhalf
                    if st_sl[ssl] is not None:
                        k.wait("vector", STA[ssl], st_sl[ssl])
                    k.vec(lambda e, mo=mo, q=q, hhalf=hhalf, ssl=ssl, ps=ps:
                          e.scalar_tensor_tensor(
                              stage[:, ssl * 1024:(ssl + 1) * 1024],
                              ps[:, hhalf * 1024:(hhalf + 1) * 1024],
                              (b2sb[:, mo:mo + 1] if q == 0 else 0.0),
                              gb[:, hhalf * 1024:(hhalf + 1) * 1024],
                              ALU.add, ALU.mult))
                    vwar_w2.append(k.V.n)
                    k.wait("gpsimd", k.V)
                    k.dma("gpsimd",
                          out[mo * 128:(mo + 1) * 128,
                              hhalf * 1024:(hhalf + 1) * 1024],
                          stage[:, ssl * 1024:(ssl + 1) * 1024],
                          STA[ssl], accum_op=ALU.add)
                    st_sl[ssl] = STA[ssl].n
    for ssl in range(2):
        if st_sl[ssl] is not None:
            k.wait("sync", STA[ssl], st_sl[ssl])
    return k.build()


def _shard_tok(xT_full, c):
    b = c // 4
    t0 = (c % 4) * TC
    return np.ascontiguousarray(xT_full[b, :, t0:t0 + TC])


def kernel(**inputs):
    f32 = np.float32
    xT = np.ascontiguousarray(inputs["L0_attn_x"].transpose(0, 2, 1))

    ncA = build_A()
    common_A = {
        "Wd": np.ascontiguousarray(inputs["L0_attn_Wd"], dtype=f32),
        "bd": np.ascontiguousarray(inputs["L0_attn_bd"].reshape(D_, 1), dtype=f32),
        "Wb": np.ascontiguousarray(inputs["L0_attn_Wb"], dtype=f32),
        "bb": np.ascontiguousarray(inputs["L0_attn_bb"].reshape(16, 1), dtype=f32),
        "Wc": np.ascontiguousarray(inputs["L0_attn_Wc"], dtype=f32),
        "bc": np.ascontiguousarray(inputs["L0_attn_bc"].reshape(16, 1), dtype=f32),
        "A": np.ascontiguousarray(inputs["L0_attn_A"], dtype=f32),
    }
    in_maps = [dict(common_A, xT=_shard_tok(xT, c)) for c in range(NCORES)]
    resA = run_bass_kernel_spmd(ncA, in_maps, core_ids=list(range(NCORES))).results

    a_full = np.zeros((B_, T_), f32)
    b_full = np.zeros((B_, 16, T_), f32)
    ct_full = np.zeros((B_, 16, T_), f32)
    xn_full = np.zeros((B_, D_, T_), BF)
    for c in range(NCORES):
        b = c // 4
        t0 = (c % 4) * TC
        a_full[b, t0:t0 + TC] = resA[c]["a_out"][0]
        b_full[b, :, t0:t0 + TC] = resA[c]["b_out"]
        ct_full[b, :, t0:t0 + TC] = resA[c]["ct_out"]
        xn_full[b, :, t0:t0 + TC] = resA[c]["xn_out"]

    ncB = build_B()
    a_rep = np.repeat(a_full[:, None, :], 16, axis=1).reshape(32, T_)
    b_rows = b_full.reshape(32, T_)
    in_maps = [{"a_in": np.ascontiguousarray(a_rep[4 * c:4 * c + 4]),
                "b_in": np.ascontiguousarray(b_rows[4 * c:4 * c + 4])}
               for c in range(NCORES)]
    resB = run_bass_kernel_spmd(ncB, in_maps, core_ids=list(range(NCORES))).results
    hs_full = np.concatenate([r["hs_out"] for r in resB], axis=0).reshape(B_, 16, T_)

    ncC = build_C()
    maker = np.asarray(inputs["L0_ffn_rail_maker"], f32)
    up = np.asarray(inputs["L0_ffn_rail_up_proj"], f32)
    mk_bd = np.zeros((32, 128, 128), f32)
    up_bd = np.zeros((32, 128, 512), f32)
    for h in range(4):
        for r in range(64):
            g, i = r // 8, r % 8
            blk = h * 8 + g
            mk_bd[blk, i * 16:(i + 1) * 16, i * 16:(i + 1) * 16] = maker[h, r].T
            up_bd[blk, i * 16:(i + 1) * 16, i * 64:(i + 1) * 64] = up[h, r].T
    common_C = {
        "Wo": np.ascontiguousarray(inputs["L0_attn_Wo"], dtype=f32),
        "bo": np.ascontiguousarray(inputs["L0_attn_bo"].reshape(D_, 1), dtype=f32),
        "mk_bd": mk_bd,
        "up_bd": up_bd,
        "wcp": np.ascontiguousarray(np.asarray(inputs["L0_ffn_w_cproj"], f32).T),
        "WqT": np.ascontiguousarray(np.asarray(inputs["L1_attn_Wq"], f32).T),
        "WkT": np.ascontiguousarray(np.asarray(inputs["L1_attn_Wk"], f32).T),
        "WvT": np.ascontiguousarray(np.asarray(inputs["L1_attn_Wv"], f32).T),
    }
    in_maps = []
    for c in range(NCORES):
        b = c // 4
        t0 = (c % 4) * TC
        in_maps.append(dict(
            common_C,
            xT=_shard_tok(xT, c),
            xn_in=np.ascontiguousarray(xn_full[b, :, t0:t0 + TC]),
            hs_in=np.ascontiguousarray(hs_full[b, :, t0:t0 + TC]),
            ct_in=np.ascontiguousarray(ct_full[b, :, t0:t0 + TC]),
        ))
    resC = run_bass_kernel_spmd(ncC, in_maps, core_ids=list(range(NCORES))).results

    x2_full = np.zeros((B_, D_, T_), f32)
    q_full = np.zeros((B_, D_, T_), BF)
    k_full = np.zeros((B_, D_, T_), BF)
    v_full = np.zeros((B_, T_, D_), BF)
    qn2_full = np.zeros((B_, 16, T_), f32)
    kn2_full = np.zeros((B_, 16, T_), f32)
    for c in range(NCORES):
        b = c // 4
        t0 = (c % 4) * TC
        x2_full[b, :, t0:t0 + TC] = resC[c]["x2_out"]
        q_full[b, :, t0:t0 + TC] = resC[c]["q_out"]
        k_full[b, :, t0:t0 + TC] = resC[c]["k_out"]
        v_full[b, t0:t0 + TC, :] = resC[c]["v_out"]
        qn2_full[b, :, t0:t0 + TC] = resC[c]["qn2_out"]
        kn2_full[b, :, t0:t0 + TC] = resC[c]["kn2_out"]

    ncD = build_D()
    in_maps = []
    for c in range(NCORES):
        b = c // 4
        hh = [4 * (c % 4) + j for j in range(4)]
        qh = np.stack([q_full[b, h * 64:(h + 1) * 64, :] for h in hh])
        kh = np.stack([k_full[b, h * 64:(h + 1) * 64, :] for h in hh])
        vh = np.stack([v_full[b, :, h * 64:(h + 1) * 64] for h in hh])
        qn2h = np.stack([qn2_full[b, h, :] for h in hh])
        kn2h = np.stack([kn2_full[b, h, :] for h in hh])
        in_maps.append({"qh": np.ascontiguousarray(qh),
                        "kh": np.ascontiguousarray(kh),
                        "vh": np.ascontiguousarray(vh),
                        "qn2h": np.ascontiguousarray(qn2h),
                        "kn2h": np.ascontiguousarray(kn2h)})
    resD = run_bass_kernel_spmd(ncD, in_maps, core_ids=list(range(NCORES))).results
    o_full = np.zeros((B_, D_, T_), f32)
    for c in range(NCORES):
        b = c // 4
        for j in range(4):
            h = 4 * (c % 4) + j
            o_full[b, h * 64:(h + 1) * 64, :] = resD[c]["oh"][j]

    ncE = build_E()
    common_E = {
        "rls": np.ascontiguousarray(
            inputs["L1_ffn_router_ln_scale"].reshape(D_, 1), dtype=f32),
        "rlb": np.ascontiguousarray(
            inputs["L1_ffn_router_ln_bias"].reshape(D_, 1), dtype=f32),
        "rwT": np.ascontiguousarray(np.asarray(inputs["L1_ffn_router_w"], f32).T),
        "rb_in": np.ascontiguousarray(
            inputs["L1_ffn_router_b"].reshape(1, 8), dtype=f32),
        "els": np.ascontiguousarray(inputs["L1_ffn_expert_ln_scale"], dtype=f32),
        "elb": np.ascontiguousarray(inputs["L1_ffn_expert_ln_bias"], dtype=f32),
        "w1t": np.ascontiguousarray(
            np.asarray(inputs["L1_ffn_w1"], f32).transpose(0, 2, 1)),
        "b1_in": np.ascontiguousarray(inputs["L1_ffn_b1"], dtype=f32),
        "w2t": np.ascontiguousarray(
            np.asarray(inputs["L1_ffn_w2"], f32).transpose(0, 2, 1)),
        "b2_in": np.ascontiguousarray(inputs["L1_ffn_b2"], dtype=f32),
        "eyef": np.eye(128, dtype=f32),
    }
    in_maps = []
    for c in range(NCORES):
        b = c // 4
        t0 = (c % 4) * TC
        in_maps.append(dict(
            common_E,
            xT2=np.ascontiguousarray(x2_full[b, :, t0:t0 + TC]),
            oT=np.ascontiguousarray(o_full[b, :, t0:t0 + TC]),
        ))
    resE = run_bass_kernel_spmd(ncE, in_maps, core_ids=list(range(NCORES))).results
    outT = np.zeros((B_, D_, T_), f32)
    for c in range(NCORES):
        b = c // 4
        t0 = (c % 4) * TC
        outT[b, :, t0:t0 + TC] = resE[c]["out"]
    return np.ascontiguousarray(outT.transpose(0, 2, 1))
